# version 08
# baseline (speedup 1.0000x reference)
import numpy as np
import ml_dtypes
import concourse.bass as bass
import concourse.mybir as mybir
from concourse.bass_utils import run_bass_kernel_spmd
from contextlib import ExitStack

F32 = mybir.dt.float32
BF16 = mybir.dt.bfloat16
I32 = mybir.dt.int32
AF = mybir.ActivationFunctionType
ALU = mybir.AluOpType
AX = mybir.AxisListType
BFNP = ml_dtypes.bfloat16

S_LEN = 4096
DM = 1024
NH = 16
HD = 64
HID0 = 3072
NEXP = 16
EDIM = 512
NCORES = 8
SCL = 1.0 / 127.0
EPS_LN = 1e-5
EPS_NORM = 1e-7

TT = 256
NJ = TT // 128
NT0 = S_LEN // TT
NHC = HID0 // 128
ATTN_LAG = 2
LDQ_U = "gpsimd"
LDQ_D = "scalar"
MOE_Q_U = "gpsimd"
MOE_Q_D = "scalar"


class Sched:

    def __init__(self):
        self.ops = []
        self.c = {}
        self.g = {}
        self.bar = {}

    def barrier(self):
        snap = tuple((k, v) for k, v in self.c.items() if v > 0)
        self.bar = {e: snap for e in ("sync", "tensor", "vector", "scalar", "gpsimd")}
        self.g = {}

    def val(self, sem):
        return self.c.get(sem, 0)

    def gate(self, *keys):
        out = []
        for k in keys:
            t = self.g.get(k)
            if t is not None:
                out.append(t)
        return out

    def setg(self, key, sem):
        self.g[key] = (sem, self.val(sem))

    def op(self, eng, emit, waits=(), inc=None):
        if eng in self.bar:
            waits = list(self.bar.pop(eng)) + list(waits)
        if inc is not None:
            self.c[inc[0]] = self.c.get(inc[0], 0) + inc[1]
        self.ops.append((eng, emit, tuple(waits), inc))

    def realize(self, block, sems):
        for name in ("sync", "tensor", "vector", "scalar", "gpsimd"):
            sub = [o for o in self.ops if o[0] == name]
            if not sub:
                continue

            def make(sub):
                def f(eng):
                    seen = {}
                    for _, emit, waits, inc in sub:
                        for sname, v in waits:
                            if v > 0 and seen.get(sname, -1) < v:
                                eng.wait_ge(sems[sname], v)
                                seen[sname] = v
                        inst = emit(eng)
                        if inc is not None:
                            inst.then_inc(sems[inc[0]], inc[1])
                return f

            getattr(block, name)(make(sub))


def _bc(apv, reps, unit_axis=1):
    return apv.rearrange("p (u n) -> p u n", u=1).broadcast_to(
        [apv.shape[0], reps, apv.shape[1]])


def _ln_stats(s, xt_view, nj, st, sqscr, in_waits):
    msum, ss, mean, var, rstd, negmr = (st["msum"], st["ss"], st["mean"],
                                        st["var"], st["rstd"], st["negmr"])
    s.op("vector", lambda v: v.reduce_sum(msum[:, 0:nj], xt_view, axis=AX.X),
         waits=in_waits, inc=("ve", 1))
    ve_msum = s.val("ve")
    for j in range(nj):
        s.op("scalar", lambda a, j=j: a.activation(
            sqscr[:, j, :], xt_view[:, j, :], AF.Square, bias=0.0, scale=1.0),
            waits=in_waits + s.gate("sqscr_free"),
            inc=("ac", 1) if j == nj - 1 else None)
    ac0 = s.val("ac")
    s.op("vector", lambda v: v.reduce_sum(ss[:, 0:nj], sqscr[:, 0:nj, :], axis=AX.X),
         waits=[("ac", ac0)], inc=("ve", 1))
    s.setg("sqscr_free", "ve")
    s.op("vector", lambda v: v.tensor_scalar(mean[:, 0:nj], msum[:, 0:nj], 1.0 / DM, None, ALU.mult),
         waits=[("ve", ve_msum)], inc=("ve", 1))
    s.op("vector", lambda v: v.tensor_tensor(negmr[:, 0:nj], mean[:, 0:nj], mean[:, 0:nj], ALU.mult),
         waits=[("ve", s.val("ve"))], inc=("ve", 1))
    s.op("vector", lambda v: v.scalar_tensor_tensor(var[:, 0:nj], ss[:, 0:nj], 1.0 / DM,
                                                    negmr[:, 0:nj], ALU.mult, ALU.subtract),
         waits=[("ve", s.val("ve"))], inc=("ve", 1))
    s.op("vector", lambda v: v.tensor_scalar(var[:, 0:nj], var[:, 0:nj], EPS_LN, None, ALU.add),
         waits=[("ve", s.val("ve"))], inc=("ve", 1))
    s.op("scalar", lambda a: a.activation(rstd[:, 0:nj], var[:, 0:nj], AF.Sqrt,
                                          bias=0.0, scale=1.0),
         waits=[("ve", s.val("ve"))], inc=("ac", 1))
    s.op("vector", lambda v: v.reciprocal(rstd[:, 0:nj], rstd[:, 0:nj]),
         waits=[("ac", s.val("ac"))], inc=("ve", 1))
    s.op("vector", lambda v: v.tensor_mul(negmr[:, 0:nj], mean[:, 0:nj], rstd[:, 0:nj]),
         waits=[("ve", s.val("ve"))], inc=("ve", 1))
    s.op("vector", lambda v: v.tensor_scalar(negmr[:, 0:nj], negmr[:, 0:nj], -1.0, None, ALU.mult),
         waits=[("ve", s.val("ve"))], inc=("ve", 1))


def _mk_ctx():
    nc = bass.Bass("TRN2", target_bir_lowering=False, debug=False)
    s = Sched()
    cache = {}

    def dram(name, shape, dtype, kind="ExternalInput"):
        if name not in cache:
            cache[name] = nc.dram_tensor(name, shape, dtype, kind=kind).ap()
        return cache[name]

    return nc, s, dram


def _realize(nc, s):
    with nc.sbuf_tensor("fin", [1, 1], F32) as fin:
        s.op("gpsimd", lambda e: e.memset(fin[:], 0.0),
             waits=[(k, v) for k, v in s.c.items() if v > 0])
        sems = {}
        with ExitStack() as es2:
            for n in sorted(s.c):
                sems[n] = es2.enter_context(nc.semaphore(n))
            with nc.Block() as block:
                s.realize(block, sems)
    return nc


def build_l0(stop_after="all", ntiles=NT0, dbg_scratch=False):
    nc, s, dram = _mk_ctx()
    emit_l0(nc, s, dram, stop_after, ntiles, dbg_scratch)
    return _realize(nc, s)


def build_moe():
    nc, s, dram = _mk_ctx()
    emit_moe(nc, s, dram)
    return _realize(nc, s)


def build_attn():
    nc, s, dram = _mk_ctx()
    emit_attn(nc, s, dram)
    return _realize(nc, s)


def build_fused():
    nc, s, dram0 = _mk_ctx()
    handoff = {"x1": None, "xn1": None, "w16": None, "moe": None}

    def dram(name, shape, dtype, kind="ExternalInput"):
        if name in handoff:
            kind = "Internal"
        return dram0(name, shape, dtype, kind)

    emit_l0(nc, s, dram, "all", NT0, False)
    s.barrier()
    emit_moe(nc, s, dram)
    s.barrier()
    emit_attn(nc, s, dram)
    return _realize(nc, s)


def emit_l0(nc, s, dram, stop_after="all", ntiles=NT0, dbg_scratch=False):
    ap = dram
    x = ap("x", [S_LEN, DM], F32, kind="ExternalInput")
    wq = ap("wq", [DM, DM], F32, kind="ExternalInput")
    wk = ap("wk", [DM, DM], F32, kind="ExternalInput")
    wkp = ap("wkp", [DM, DM], F32, kind="ExternalInput")
    wv = ap("wv", [DM, DM], F32, kind="ExternalInput")
    wo = ap("wo", [DM, DM], F32, kind="ExternalInput")
    gam = ap("gam", [1, DM], F32, kind="ExternalInput")
    bet = ap("bet", [1, DM], F32, kind="ExternalInput")
    wgT = ap("wgT", [DM, HID0], I32, kind="ExternalInput")
    wuT = ap("wuT", [DM, HID0], I32, kind="ExternalInput")
    wdT = ap("wdT", [HID0, DM], I32, kind="ExternalInput")
    rwT = ap("rwT", [DM, 16], F32, kind="ExternalInput")
    rbias = ap("rbias", [1, 16], F32, kind="ExternalInput")
    idb = ap("idb", [128, 128], BF16, kind="ExternalInput")
    wmask = ap("wmask", [32, DM], F32, kind="ExternalInput")
    idx16 = ap("idx16", [128, 16], F32, kind="ExternalInput")
    x1 = ap("x1", [S_LEN, DM], F32, kind="ExternalOutput")
    xn1 = ap("xn1", [8, 128, S_LEN], BF16, kind="ExternalOutput")
    w16 = ap("w16", [S_LEN, 16], F32, kind="ExternalOutput")
    scratch_kind = "ExternalOutput" if dbg_scratch else "Internal"
    if dbg_scratch:
        dbg_xn = ap("dbg_xn", [128, NJ, DM], F32, kind="ExternalOutput")
        dbg_fm = ap("dbg_fm", [128, 8, TT], BF16, kind="ExternalOutput")
        dbg_st = ap("dbg_st", [6, 128, NJ], F32, kind="ExternalOutput")
    wgb = ap("wgb", [NHC, 128, 1024], BF16, kind=scratch_kind)
    wub = ap("wub", [NHC, 128, 1024], BF16, kind=scratch_kind)
    wdb = ap("wdb", [NHC, 128, 1024], BF16, kind=scratch_kind)

    es = ExitStack()
    S = lambda n, *a: es.enter_context(nc.sbuf_tensor("a_" + n, *a))
    P = lambda n, *a: es.enter_context(nc.psum_tensor("a_" + n, *a))

    xbuf = [S("xbuf0", [128, NJ, DM], F32), S("xbuf1", [128, NJ, DM], F32)]
    xn = S("xn", [128, NJ, DM], F32)
    t1 = S("t1", [128, NJ, DM], F32)
    Cc = S("Cc", [128, NJ, DM], F32)
    xnb = S("xnb", [128, NJ, DM], BF16)
    xnfm = S("xnfm", [128, 8, TT], BF16)
    gslf = S("gslf", [128, TT], F32)
    hidb = [S("hidb0", [128, TT], BF16), S("hidb1", [128, TT], BF16)]
    gamb = S("gamb", [128, DM], F32)
    betb = S("betb", [128, DM], F32)
    rbb = S("rbb", [128, 16], F32)
    idxt = S("idxt", [128, 16], F32)
    idxm = S("idxm", [128, 16], F32)
    onesf = S("onesf", [1, 128], F32)
    idbt = S("idbt", [128, 128], BF16)
    sqscr = S("sqscr", [128, NJ, DM], F32)
    dbg_snap = S("dbg_snap", [128, 6, NJ], F32)
    sqmini = S("sqmini", [32, DM], F32)
    st = {k: S("st_" + k, [128, NJ], F32)
          for k in ("msum", "ss", "mean", "var", "rstd", "negmr")}
    wgc = [S("wgc0", [128, 8, 128], BF16), S("wgc1", [128, 8, 128], BF16)]
    wuc = [S("wuc0", [128, 8, 128], BF16), S("wuc1", [128, 8, 128], BF16)]
    wdc = [S("wdc0", [128, 1024], BF16), S("wdc1", [128, 1024], BF16)]
    wstage = [S("wstage0", [128, 1024], I32), S("wstage1", [128, 1024], I32)]
    wsf = S("wsf", [128, 1024], F32)
    lg_all = S("lg_all", [128, 32, 16], F32)
    r_m1 = S("r_m1", [128, 32], F32)
    r_m2 = S("r_m2", [128, 32], F32)
    r_i = S("r_i", [128, 32], F32)
    r_p = S("r_p", [128, 32], F32)
    r_c = S("r_c", [128, 32, 16], F32)
    r_d = S("r_d", [128, 32, 16], F32)
    rwb = S("rwb", [128, 8, 16], BF16)
    rwf = S("rwf", [128, 8, 16], F32)
    mst = S("mst", [128, DM], BF16)
    wmt = S("wmt", [32, DM], F32)
    xn32 = S("xn32", [32, DM], F32)
    m_q = S("m_q", [32, DM], F32)
    m_k = S("m_k", [32, DM], F32)
    m_kp = S("m_kp", [32, DM], F32)
    m_v = S("m_v", [32, DM], F32)
    m_sc = S("m_sc", [32, DM], F32)
    m_st = S("m_st", [32, 16], F32)
    m_st2 = S("m_st2", [32, 16], F32)
    m_fm = S("m_fm", [128, 8, 32], BF16)
    q_fm = S("q_fm", [128, 8, 32], BF16)
    k_fm = S("k_fm", [128, 8, 32], BF16)
    kp_fm = S("kp_fm", [128, 8, 32], BF16)
    v_fm = S("v_fm", [128, 8, 32], BF16)
    wfm = S("wfm", [128, DM], BF16)
    ptsb = S("ptsb", [128, 8, 32], BF16)
    utsb = S("utsb", [128, 8, 64], BF16)
    z_tm = S("z_tm", [32, DM], BF16)
    zwo = S("zwo", [32, DM], F32)
    wbig = S("wbig", [128, 8, DM], BF16)

    p_tr_full = [P("p_tr0", [128, 1024], BF16), P("p_tr1", [128, 1024], BF16)]
    p_tr = [t[:, 0:128] for t in p_tr_full]
    p_g_full = P("p_g", [128, 512], F32)
    p_g = p_g_full[:, 0:TT]
    p_u_full = P("p_u", [128, 512], F32)
    p_u = p_u_full[:, 0:TT]
    p_d = [P("p_d0", [128, 512], F32), P("p_d1", [128, 512], F32),
           P("p_d2", [128, 512], F32), P("p_d3", [128, 512], F32)]

    with es:
        s.op("sync", lambda e: e.dma_start(out=idbt[:], in_=idb[:]), inc=("ld", 16))
        s.op("sync", lambda e: e.dma_start(out=idxt[:], in_=idx16[:]), inc=("ld", 16))
        s.op("sync", lambda e: e.dma_start(out=wmt[:], in_=wmask[:]), inc=("ld", 16))
        s.op("sync", lambda e: e.dma_start(out=rwf[:], in_=rwT.rearrange("(c p) e -> p c e", p=128)),
             inc=("ld", 16))
        s.op("sync", lambda e: e.dma_start(out=gamb[0:1, :], in_=gam[:]), inc=("ld", 16))
        s.op("sync", lambda e: e.dma_start(out=betb[0:1, :], in_=bet[:]), inc=("ld", 16))
        s.op("sync", lambda e: e.dma_start(out=rbb[0:1, :], in_=rbias[:]), inc=("ld", 16))
        ld_c = s.val("ld")
        if stop_after == "c1":
            return
        s.op("gpsimd", lambda e: e.memset(onesf[:], 1.0), inc=("gp", 1))
        s.op("gpsimd", lambda e: e.memset(Cc[:], 0.0), inc=("gp", 1))
        s.op("gpsimd", lambda e: e.memset(mst[:], 0.0), inc=("gp", 1))
        gp_ms = s.val("gp")
        s.op("vector", lambda v: v.tensor_scalar(idxm[:], idxt[:], 65536.0, None, ALU.subtract),
             waits=[("ld", ld_c)], inc=("ve", 1))
        s.op("vector", lambda v: v.tensor_copy(rwb[:], rwf[:]),
             waits=[("ld", ld_c)], inc=("ve", 1))
        if stop_after == "c2":
            return
        for (tile, psum, pname) in ((gamb, p_d[0], "pd0"), (betb, p_d[1], "pd1")):
            for half in range(2):
                s.op("tensor", lambda t_, tile=tile, psum=psum, half=half: t_.matmul(
                    psum[:], onesf[0:1, :], tile[0:1, half * 512:(half + 1) * 512],
                    start=True, stop=True),
                    waits=[("ld", ld_c), ("gp", gp_ms)] + s.gate(pname + "_free"),
                    inc=("mm", 1))
                s.op("vector", lambda v, tile=tile, psum=psum, half=half: v.tensor_copy(
                    tile[:, half * 512:(half + 1) * 512], psum[:]),
                    waits=[("mm", s.val("mm"))], inc=("ve", 1))
                s.setg(pname + "_free", "ve")
        s.op("tensor", lambda t_: t_.matmul(p_d[2][:, 0:16], onesf[0:1, :], rbb[0:1, 0:16],
                                            start=True, stop=True),
             waits=[("ld", ld_c), ("gp", gp_ms)], inc=("mm", 1))
        s.op("vector", lambda v: v.tensor_copy(rbb[:], p_d[2][:, 0:16]),
             waits=[("mm", s.val("mm"))], inc=("ve", 1))
        s.setg("gb_ready", "ve")
        s.setg("pd0_free", "ve")
        s.setg("pd1_free", "ve")
        s.setg("pd2_free", "ve")

        if stop_after == "cast":
            return
        s.op("sync", lambda e: e.dma_start(out=xn32[:], in_=x[S_LEN - 32:S_LEN, :]),
             inc=("ld", 16))
        ld32 = s.val("ld")
        s.op("vector", lambda v: v.reduce_sum(m_st[:, 0:1], xn32[:], axis=AX.X),
             waits=[("ld", ld32)], inc=("ve", 1))
        s.op("scalar", lambda a: a.activation(sqmini[:], xn32[:], AF.Square,
                                              bias=0.0, scale=1.0),
             waits=[("ld", ld32)], inc=("ac", 1))
        s.op("vector", lambda v: v.reduce_sum(m_st[:, 1:2], sqmini[:], axis=AX.X),
             waits=[("ac", s.val("ac"))], inc=("ve", 1))
        w0 = [("ve", s.val("ve")), ("ac", s.val("ac"))]
        s.op("vector", lambda v: v.tensor_scalar(m_st[:, 2:3], m_st[:, 0:1], 1.0 / DM, None, ALU.mult),
             waits=w0, inc=("ve", 1))
        s.op("vector", lambda v: v.tensor_scalar(m_st[:, 3:4], m_st[:, 1:2], 1.0 / DM, None, ALU.mult),
             waits=[("ve", s.val("ve"))], inc=("ve", 1))
        s.op("vector", lambda v: v.tensor_mul(m_st[:, 4:5], m_st[:, 2:3], m_st[:, 2:3]),
             waits=[("ve", s.val("ve"))], inc=("ve", 1))
        s.op("vector", lambda v: v.tensor_sub(m_st[:, 3:4], m_st[:, 3:4], m_st[:, 4:5]),
             waits=[("ve", s.val("ve"))], inc=("ve", 1))
        s.op("vector", lambda v: v.tensor_scalar(m_st[:, 3:4], m_st[:, 3:4], EPS_LN, None, ALU.add),
             waits=[("ve", s.val("ve"))], inc=("ve", 1))
        s.op("scalar", lambda a: a.activation(m_st[:, 5:6], m_st[:, 3:4], AF.Sqrt, bias=0.0, scale=1.0),
             waits=[("ve", s.val("ve"))], inc=("ac", 1))
        s.op("vector", lambda v: v.reciprocal(m_st[:, 5:6], m_st[:, 5:6]),
             waits=[("ac", s.val("ac"))], inc=("ve", 1))
        s.op("vector", lambda v: v.tensor_mul(m_st[:, 6:7], m_st[:, 2:3], m_st[:, 5:6]),
             waits=[("ve", s.val("ve"))], inc=("ve", 1))
        s.op("vector", lambda v: v.tensor_scalar(m_st[:, 6:7], m_st[:, 6:7], -1.0, None, ALU.mult),
             waits=[("ve", s.val("ve"))], inc=("ve", 1))
        s.op("scalar", lambda a: a.activation(xn32[:], xn32[:], AF.Identity,
                                              bias=m_st[:, 6:7], scale=m_st[:, 5:6]),
             waits=[("ve", s.val("ve"))], inc=("ac", 1))
        ac_xn32 = s.val("ac")

        def mini_tr(src_f32_or_bf16, rows, out_fm, nchunk=8, outrows=None):
            outrows = outrows or rows
            s.op("vector", lambda v: v.tensor_copy(mst[0:rows, :], src_f32_or_bf16),
                 waits=s.gate("mst_free"), inc=("ve", 1))
            ve_stg = s.val("ve")
            for dc in range(nchunk):
                pt = p_tr[dc % 2]
                s.op("tensor", lambda t_, dc=dc, pt=pt: t_.transpose(
                    pt[:], mst[:, dc * 128:(dc + 1) * 128], idbt[:]),
                    waits=[("ve", ve_stg)] + s.gate("ptr%d_free" % (dc % 2)) + [("ld", ld_c)],
                    inc=("mm", 1))
                s.op("vector", lambda v, dc=dc, pt=pt: v.tensor_copy(
                    out_fm[:, dc, 0:outrows], pt[:, 0:outrows]),
                    waits=[("mm", s.val("mm"))], inc=("ve", 1))
                s.setg("ptr%d_free" % (dc % 2), "ve")
            s.setg("mst_free", "ve")

        s.op("vector", lambda v: v.tensor_copy(mst[0:32, :], xn32[:]),
             waits=[("ac", ac_xn32), ("gp", gp_ms)], inc=("ve", 1))
        ve_stg = s.val("ve")
        for dc in range(8):
            pt = p_tr[dc % 2]
            s.op("tensor", lambda t_, dc=dc, pt=pt: t_.transpose(
                pt[:], mst[:, dc * 128:(dc + 1) * 128], idbt[:]),
                waits=[("ve", ve_stg)] + s.gate("ptr%d_free" % (dc % 2)) + [("ld", ld_c)],
                inc=("mm", 1))
            s.op("vector", lambda v, dc=dc, pt=pt: v.tensor_copy(m_fm[:, dc, :], pt[:, 0:32]),
                 waits=[("mm", s.val("mm"))], inc=("ve", 1))
            s.setg("ptr%d_free" % (dc % 2), "ve")
        s.setg("mst_free", "ve")

        for (whbm, out_tm, out_fm, do_norm) in ((wq, m_q, q_fm, True), (wk, m_k, k_fm, True),
                                                (wkp, m_kp, kp_fm, True), (wv, m_v, v_fm, False)):
            for dc in range(8):
                s.op("sync", lambda e, whbm=whbm, dc=dc: e.dma_start(
                    out=wsf[:], in_=whbm.rearrange("(c p) f -> p c f", p=128)[:, dc, :]),
                    waits=s.gate("wsf_free"), inc=("ld", 16))
                s.op("vector", lambda v, dc=dc: v.tensor_copy(wbig[:, dc, :], wsf[:]),
                     waits=[("ld", s.val("ld"))] + s.gate("wbig_free"), inc=("ve", 1))
                s.setg("wsf_free", "ve")
            ve_w = s.val("ve")
            for half in range(2):
                pp = p_d[2 + half]
                for dc in range(8):
                    s.op("tensor", lambda t_, dc=dc, half=half, pp=pp: t_.matmul(
                        pp[0:32, :], m_fm[:, dc, :], wbig[:, dc, half * 512:(half + 1) * 512],
                        start=(dc == 0), stop=(dc == 7)),
                        waits=[("ve", ve_w)] + s.gate("pd%d_free" % (2 + half)),
                        inc=("mm", 1) if dc == 7 else None)
                s.op("vector", lambda v, half=half, pp=pp, out_tm=out_tm: v.tensor_copy(
                    out_tm[:, half * 512:(half + 1) * 512], pp[0:32, :]),
                    waits=[("mm", s.val("mm"))], inc=("ve", 1))
                s.setg("pd%d_free" % (2 + half), "ve")
            s.setg("wbig_free", "ve")
            ve_proj = s.val("ve")
            if do_norm:
                s.op("scalar", lambda a, out_tm=out_tm: a.activation(
                    sqmini[:], out_tm[:], AF.Square, bias=0.0, scale=1.0),
                    waits=[("ve", ve_proj)], inc=("ac", 1))
                s.op("vector", lambda v: v.reduce_sum(
                    m_st[:, 0:16], sqmini[:].rearrange("q (h d) -> q h d", h=NH), axis=AX.X),
                    waits=[("ac", s.val("ac"))], inc=("ve", 1))
                s.op("scalar", lambda a: a.activation(m_st2[:], m_st[:], AF.Sqrt, bias=0.0, scale=1.0),
                     waits=[("ve", s.val("ve"))], inc=("ac", 1))
                s.op("vector", lambda v: v.tensor_scalar(m_st2[:], m_st2[:], EPS_NORM, None, ALU.add),
                     waits=[("ac", s.val("ac"))], inc=("ve", 1))
                s.op("vector", lambda v: v.reciprocal(m_st2[:], m_st2[:]),
                     waits=[("ve", s.val("ve"))], inc=("ve", 1))
                ve_rn = s.val("ve")
                for h in range(NH):
                    s.op("vector", lambda v, h=h, out_tm=out_tm: v.tensor_scalar(
                        out_tm[:, h * 64:(h + 1) * 64], out_tm[:, h * 64:(h + 1) * 64],
                        m_st2[:, h:h + 1], None, ALU.mult),
                        waits=[("ve", ve_rn)],
                        inc=("ve", 1) if h == NH - 1 else None)
            mini_tr(out_tm[:], 32, out_fm)

        for h in range(NH):
            cc, ro = h // 2, 64 * (h % 2)
            qf = q_fm[:, cc, :][ro:ro + 64, :]
            kf = k_fm[:, cc, :][ro:ro + 64, :]
            kpf = kp_fm[:, cc, :][ro:ro + 64, :]
            vf = v_fm[:, cc, :][ro:ro + 64, :]
            kv = kf.rearrange("d (j u) -> d j u", u=1).broadcast_to([64, 32, 32])
            kpv = kpf.rearrange("d (u k) -> d u k", u=1).broadcast_to([64, 32, 32])
            wfm_h = wfm[ro:ro + 64, :]
            s.op("vector", lambda v, kv=kv, kpv=kpv, wfm_h=wfm_h: v.tensor_tensor(
                wfm_h.rearrange("d (j k) -> d j k", k=32), kv, kpv, ALU.mult),
                waits=s.gate("wfm_free"), inc=("ve", 1))
            ve_wfm = s.val("ve")
            for half in range(2):
                pp = p_d[half]
                s.op("tensor", lambda t_, half=half, pp=pp, qf=qf, wfm_h=wfm_h: t_.matmul(
                    pp[0:32, :], qf, wfm_h[:, half * 512:(half + 1) * 512],
                    start=True, stop=True),
                    waits=[("ve", ve_wfm)] + s.gate("pd%d_free" % half),
                    inc=("mm", 1))
                s.op("vector", lambda v, half=half, pp=pp: v.scalar_tensor_tensor(
                    m_sc[:, half * 512:(half + 1) * 512], pp[0:32, :], 0.125,
                    wmt[:, half * 512:(half + 1) * 512], ALU.mult, ALU.add),
                    waits=[("mm", s.val("mm")), ("ld", ld_c)], inc=("ve", 1))
                s.setg("pd%d_free" % half, "ve")
            s.op("vector", lambda v: v.reduce_max(m_st[:, 0:1], m_sc[:], axis=AX.X),
                 inc=("ve", 1))
            s.op("vector", lambda v: v.tensor_scalar(m_st[:, 1:2], m_st[:, 0:1], -1.0, None, ALU.mult),
                 waits=[("ve", s.val("ve"))], inc=("ve", 1))
            s.op("scalar", lambda a: a.activation(m_sc[:], m_sc[:], AF.Exp,
                                                  bias=m_st[:, 1:2], scale=1.0),
                 waits=[("ve", s.val("ve"))], inc=("ac", 1))
            s.op("vector", lambda v: v.reduce_sum(m_st[:, 2:3], m_sc[:], axis=AX.X),
                 waits=[("ac", s.val("ac"))], inc=("ve", 1))
            s.op("vector", lambda v: v.reciprocal(m_st[:, 3:4], m_st[:, 2:3]),
                 waits=[("ve", s.val("ve"))], inc=("ve", 1))
            ve_recip = s.val("ve")
            mini_tr(m_sc[:], 32, ptsb)
            vv1 = vf.rearrange("d (j u) -> d j u", u=1).broadcast_to([64, 32, 32])
            vv2 = vf.rearrange("d (u k) -> d u k", u=1).broadcast_to([64, 32, 32])
            s.op("vector", lambda v, vv1=vv1, vv2=vv2, wfm_h=wfm_h: v.tensor_tensor(
                wfm_h.rearrange("d (j k) -> d j k", k=32), vv1, vv2, ALU.mult),
                inc=("ve", 1))
            mini_tr(wfm_h, 64, utsb)
            s.setg("wfm_free", "ve")
            pz = p_d[2]
            for dc in range(8):
                s.op("tensor", lambda t_, dc=dc, pz=pz: t_.matmul(
                    pz[0:32, 0:64], ptsb[:, dc, :], utsb[:, dc, :],
                    start=(dc == 0), stop=(dc == 7)),
                    waits=[("ve", s.val("ve"))] + s.gate("pd2_free"),
                    inc=("mm", 1) if dc == 7 else None)
            s.op("vector", lambda v, h=h, pz=pz: v.tensor_scalar(
                z_tm[:, h * 64:(h + 1) * 64], pz[0:32, 0:64], m_st[:, 3:4], None, ALU.mult),
                waits=[("mm", s.val("mm")), ("ve", ve_recip)], inc=("ve", 1))
            s.setg("pd2_free", "ve")
        for dc in range(8):
            s.op("sync", lambda e, dc=dc: e.dma_start(
                out=wsf[:], in_=wo.rearrange("(c p) f -> p c f", p=128)[:, dc, :]),
                waits=s.gate("wsf_free"), inc=("ld", 16))
            s.op("vector", lambda v, dc=dc: v.tensor_copy(wbig[:, dc, :], wsf[:]),
                 waits=[("ld", s.val("ld"))] + s.gate("wbig_free"), inc=("ve", 1))
            s.setg("wsf_free", "ve")
        ve_wo = s.val("ve")
        mini_tr(z_tm[:], 32, m_fm)
        ve_zfm = s.val("ve")
        for half in range(2):
            pp = p_d[half]
            for dc in range(8):
                s.op("tensor", lambda t_, dc=dc, half=half, pp=pp: t_.matmul(
                    pp[0:32, :], m_fm[:, dc, :], wbig[:, dc, half * 512:(half + 1) * 512],
                    start=(dc == 0), stop=(dc == 7)),
                    waits=[("ve", ve_wo), ("ve", ve_zfm)] + s.gate("pd%d_free" % half),
                    inc=("mm", 1) if dc == 7 else None)
            s.op("vector", lambda v, half=half, pp=pp: v.tensor_copy(
                zwo[:, half * 512:(half + 1) * 512], pp[0:32, :]),
                waits=[("mm", s.val("mm"))], inc=("ve", 1))
            s.setg("pd%d_free" % half, "ve")
        s.op("gpsimd", lambda e: e.dma_start(out=Cc[96:128, NJ - 1, :], in_=zwo[:]),
             waits=[("ve", s.val("ve"))], inc=("cst", 16))
        cc_done = s.val("cst")

        if stop_after == "mini":
            return
        for t in range(ntiles):
            xb = xbuf[t % 2]
            s.op("sync", lambda e, t=t, xb=xb: e.dma_start(
                out=xb[:], in_=x.rearrange("(t j p) d -> t p j d", j=NJ, p=128)[t]),
                waits=s.gate("xb%d_free" % (t % 2)), inc=("ld", 16))
            ld_x = s.val("ld")
            inw = [("ld", ld_x)]
            _ln_stats(s, xb[:], NJ, st, sqscr, inw)
            ve_rs = s.val("ve")
            if dbg_scratch and t == 0:
                snap = locals().get("dbg_snap")
                for i, k in enumerate(("msum", "ss", "mean", "var", "rstd", "negmr")):
                    s.op("vector", lambda v, i=i, k=k: v.tensor_copy(
                        dbg_snap[:, i, :], st[k][:]),
                        waits=[("ve", ve_rs), ("ac", s.val("ac"))],
                        inc=("ve", 1) if i == 5 else None)
                s.op("gpsimd", lambda e: e.dma_start(
                    out=dbg_st.rearrange("i p j -> p i j"), in_=dbg_snap[:]),
                    waits=[("ve", s.val("ve"))], inc=("so", 16))
            for j in range(NJ):
                s.op("scalar", lambda a, j=j, xb=xb: a.activation(
                    xn[:, j, :], xb[:, j, :], AF.Identity,
                    bias=st["negmr"][:, j:j + 1], scale=st["rstd"][:, j:j + 1]),
                    waits=[("ve", ve_rs)] + s.gate("xn_free"), inc=("ac", 1))
            ac_xn = s.val("ac")
            s.op("vector", lambda v: v.tensor_copy(xnb[:], xn[:]),
                 waits=[("ac", ac_xn)], inc=("ve", 1))
            ve_xnb = s.val("ve")
            for jj in range(NJ):
                for dc in range(8):
                    pt = p_tr[(jj * 8 + dc) % 2]
                    s.op("tensor", lambda t_, jj=jj, dc=dc, pt=pt: t_.transpose(
                        pt[:], xnb[:, jj, dc * 128:(dc + 1) * 128], idbt[:]),
                        waits=[("ve", ve_xnb)] + s.gate("ptr%d_free" % ((jj * 8 + dc) % 2)),
                        inc=("mm", 1))
                    s.op("vector", lambda v, jj=jj, dc=dc, pt=pt: v.tensor_copy(
                        xnfm[:, dc, jj * 128:(jj + 1) * 128], pt[:]),
                        waits=[("mm", s.val("mm"))], inc=("ve", 1))
                    s.setg("ptr%d_free" % ((jj * 8 + dc) % 2), "ve")
            ve_fm = s.val("ve")
            if dbg_scratch and t == 0:
                s.op("gpsimd", lambda e: e.dma_start(out=dbg_xn[:], in_=xn[:]),
                     waits=[("ve", ve_fm)], inc=("so", 16))
                s.op("gpsimd", lambda e: e.dma_start(out=dbg_fm[:], in_=xnfm[:]),
                     waits=[("ve", ve_fm)], inc=("so", 16))
            def emit_down_l0(pend):
                hcp, ve_hidp = pend
                wbp = hcp % 2
                for mt in range(NJ):
                    for half in range(2):
                        pi = mt * 2 + half
                        last = (mt == NJ - 1 and half == 1)
                        s.op("tensor", lambda t_, mt=mt, half=half, pi=pi, hcp=hcp, wbp=wbp: t_.matmul(
                            p_d[pi][:], hidb[wbp][:, mt * 128:(mt + 1) * 128],
                            wdc[wbp][:, half * 512:(half + 1) * 512],
                            start=(hcp == 0), stop=(hcp == NHC - 1)),
                            waits=([("ve", ve_hidp)] +
                                   (s.gate("pd0_free", "pd1_free", "pd2_free", "pd3_free")
                                    if hcp == 0 else [])),
                            inc=("mm", 1) if last else None)
                s.setg("hid%d_free" % wbp, "mm")
                s.setg("wbuf%d_free" % wbp, "mm")
                s.setg("wbufu%d_free" % wbp, "mm")
                s.setg("wbufd%d_free" % wbp, "mm")

            pend_down = None
            for hc in range(NHC):
                wb = hc % 2
                if t == 0:
                    for wi, (wsrcT, wdst, ldq, lsem) in enumerate(
                            ((wgT, wgb, "sync", "mld"), (wuT, wub, LDQ_U, "mldu"), (wdT, wdb, LDQ_D, "mldd"))):
                        sj = (hc * 3 + wi) % 2
                        if wi < 2:
                            src = wsrcT.rearrange("(c p) h -> p c h", p=128)[:, :, hc * 128:(hc + 1) * 128]
                            dst = wstage[sj][:].rearrange("p (c h) -> p c h", c=8)
                        else:
                            src = wsrcT[hc * 128:(hc + 1) * 128, :]
                            dst = wstage[sj][:]
                        s.op(ldq, lambda e, src=src, dst=dst: e.dma_start(out=dst, in_=src),
                             waits=s.gate("wstage%d_free" % sj), inc=(lsem, 16))
                        cdst = (wgc[wb][:].rearrange("p a b -> p (a b)") if wi == 0 else
                                (wuc[wb][:].rearrange("p a b -> p (a b)") if wi == 1 else wdc[wb][:]))
                        s.op("vector", lambda v, cdst=cdst, sj=sj: v.tensor_copy(cdst, wstage[sj][:]),
                             waits=[(lsem, s.val(lsem)), ("wst", s.val("wst"))] +
                                   (s.gate("wbuf%d_free" % wb) if wi == 0 else []),
                             inc=("ve", 1))
                        s.setg("wstage%d_free" % sj, "ve")
                        s.op("gpsimd", lambda e, cdst=cdst, wdst=wdst, hc=hc: e.dma_start(
                            out=wdst[hc], in_=cdst),
                            waits=[("ve", s.val("ve"))], inc=("wst", 16))
                    ve_wcast = s.val("ve")
                    mm_waits = [("ve", ve_wcast), ("ve", ve_fm)]
                else:
                    wait_w = [("wst", s.val("wst"))] if t == 1 and hc < 2 else []
                    s.op("sync", lambda e, hc=hc, wb=wb: e.dma_start(
                        out=wgc[wb][:].rearrange("p a b -> p (a b)"), in_=wgb[hc]),
                        waits=wait_w + s.gate("wbuf%d_free" % wb), inc=("mld", 16))
                    s.op(LDQ_U, lambda e, hc=hc, wb=wb: e.dma_start(
                        out=wuc[wb][:].rearrange("p a b -> p (a b)"), in_=wub[hc]),
                        waits=([("wst", s.val("wst"))] if t == 1 and hc < 2 else []) +
                              s.gate("wbufu%d_free" % wb), inc=("mldu", 16))
                    s.op(LDQ_D, lambda e, hc=hc, wb=wb: e.dma_start(
                        out=wdc[wb][:], in_=wdb[hc]),
                        waits=([("wst", s.val("wst"))] if t == 1 and hc < 2 else []) +
                              s.gate("wbufd%d_free" % wb), inc=("mldd", 16))
                    mm_waits = [("mld", s.val("mld")), ("mldu", s.val("mldu")),
                                ("mldd", s.val("mldd")), ("ve", ve_fm)]
                for dc in range(8):
                    s.op("tensor", lambda t_, dc=dc, wb=wb: t_.matmul(
                        p_g[:], wgc[wb][:, dc, :], xnfm[:, dc, :],
                        start=(dc == 0), stop=(dc == 7)),
                        waits=mm_waits + s.gate("pg_free"),
                        inc=("mm", 1) if dc == 7 else None)
                mm_g = s.val("mm")
                for dc in range(8):
                    s.op("tensor", lambda t_, dc=dc, wb=wb: t_.matmul(
                        p_u[:], wuc[wb][:, dc, :], xnfm[:, dc, :],
                        start=(dc == 0), stop=(dc == 7)),
                        waits=s.gate("pu_free"),
                        inc=("mm", 1) if dc == 7 else None)
                mm_u = s.val("mm")
                s.op("scalar", lambda a: a.activation(gslf[:], p_g[:], AF.Silu,
                                                      bias=0.0, scale=SCL),
                     waits=[("mm", mm_g)] + s.gate("gslf_free"), inc=("ac", 1))
                s.setg("pg_free", "ac")
                s.op("vector", lambda v, wb=wb: v.tensor_tensor(hidb[wb][:], gslf[:], p_u[:], ALU.mult),
                     waits=[("ac", s.val("ac")), ("mm", mm_u)] + s.gate("hid%d_free" % wb),
                     inc=("ve", 1))
                s.setg("pu_free", "ve")
                s.setg("gslf_free", "ve")
                ve_hid = s.val("ve")
                if pend_down is not None:
                    emit_down_l0(pend_down)
                pend_down = (hc, ve_hid)
            emit_down_l0(pend_down)
            mm_down = s.val("mm")
            if t == NT0 - 1:
                s.op("vector", lambda v: v.tensor_add(xn[:], xn[:], Cc[:]),
                     waits=[("cst", cc_done), ("ve", ve_fm)], inc=("ve", 1))
            _ln_stats(s, xn[:], NJ, st, sqscr,
                      [("ve", s.val("ve")), ("ac", s.val("ac")), ("mm", mm_g)])
            ve_rs2 = s.val("ve")
            for j in range(NJ):
                s.op("scalar", lambda a, j=j: a.activation(
                    t1[:, j, :], xn[:, j, :], AF.Identity,
                    bias=st["negmr"][:, j:j + 1], scale=st["rstd"][:, j:j + 1]),
                    waits=[("ve", ve_rs2)], inc=("ac", 1))
            ac_t1 = s.val("ac")
            s.setg("xn_free", "ac")
            gv = _bc(gamb[:], NJ)
            bv = _bc(betb[:], NJ)
            s.op("vector", lambda v, gv=gv: v.tensor_tensor(t1[:], t1[:], gv, ALU.mult),
                 waits=[("ac", ac_t1)] + s.gate("gb_ready"))
            s.op("vector", lambda v, bv=bv: v.tensor_tensor(t1[:], t1[:], bv, ALU.add))
            s.op("vector", lambda v, xb=xb: v.tensor_add(xb[:], xb[:], t1[:]), inc=("ve", 1))
            ve_s = s.val("ve")
            for mt in range(NJ):
                for half in range(2):
                    pi = mt * 2 + half
                    s.op("vector", lambda v, mt=mt, half=half, pi=pi, xb=xb:
                         v.scalar_tensor_tensor(
                             xb[:, mt, half * 512:(half + 1) * 512], p_d[pi][:],
                             SCL * SCL, xb[:, mt, half * 512:(half + 1) * 512],
                             ALU.mult, ALU.add),
                         waits=[("ve", ve_s), ("mm", mm_down)],
                         inc=("ve", 1) if pi == 3 else None)
            for pi in range(4):
                s.setg("pd%d_free" % pi, "ve")
            ve_x1 = s.val("ve")
            s.op("gpsimd", lambda e, t=t, xb=xb: e.dma_start(
                out=x1.rearrange("(t j p) d -> t p j d", j=NJ, p=128)[t], in_=xb[:]),
                waits=[("ve", ve_x1)], inc=("so", 16))
            _ln_stats(s, xb[:], NJ, st, sqscr, [("ve", ve_x1), ("ac", s.val("ac"))])
            ve_rs3 = s.val("ve")
            for j in range(NJ):
                s.op("scalar", lambda a, j=j, xb=xb: a.activation(
                    xnb[:, j, :], xb[:, j, :], AF.Identity,
                    bias=st["negmr"][:, j:j + 1], scale=st["rstd"][:, j:j + 1]),
                    waits=[("ve", ve_rs3)], inc=("ac", 1))
            ac_xn1 = s.val("ac")
            s.setg("xb%d_free" % (t % 2), "ac")
            for jj in range(NJ):
                for dc in range(8):
                    pt = p_tr[(jj * 8 + dc) % 2]
                    s.op("tensor", lambda t_, jj=jj, dc=dc, pt=pt: t_.transpose(
                        pt[:], xnb[:, jj, dc * 128:(dc + 1) * 128], idbt[:]),
                        waits=[("ac", ac_xn1)] + s.gate("ptr%d_free" % ((jj * 8 + dc) % 2)),
                        inc=("mm", 1))
                    s.op("vector", lambda v, jj=jj, dc=dc, pt=pt: v.tensor_copy(
                        xnfm[:, dc, jj * 128:(jj + 1) * 128], pt[:]),
                        waits=[("mm", s.val("mm"))], inc=("ve", 1))
                    s.setg("ptr%d_free" % ((jj * 8 + dc) % 2), "ve")
            ve_fm1 = s.val("ve")
            s.op("gpsimd", lambda e, t=t: e.dma_start(
                out=xn1[:, :, t * TT:(t + 1) * TT].rearrange("c p u -> p c u"), in_=xnfm[:]),
                waits=[("ve", ve_fm1)], inc=("so", 16))
            for mt in range(NJ):
                mtg = t * NJ + mt
                for dc in range(8):
                    s.op("tensor", lambda t_, mt=mt, dc=dc: t_.matmul(
                        p_g[:, 0:16], xnfm[:, dc, mt * 128:(mt + 1) * 128], rwb[:, dc, :],
                        start=(dc == 0), stop=(dc == 7)),
                        waits=[("ve", ve_fm1)] + s.gate("pg_free"),
                        inc=("mm", 1) if dc == 7 else None)
                s.op("vector", lambda v, mtg=mtg: v.tensor_tensor(
                    lg_all[:, mtg, :], p_g[:, 0:16], rbb[:], ALU.add),
                    waits=[("mm", s.val("mm"))], inc=("ve", 1))
                s.setg("pg_free", "ve")
            s.g["xb%d_free" % (t % 2)] = ("so", s.val("so"))
            s.g["xb%d_acfree" % (t % 2)] = ("ac", ac_xn1)


        if stop_after == "tiles":
            return
        ve_lg = s.val("ve")
        iv = _bc(idxt[:], 32)
        imv = _bc(idxm[:], 32)
        bc32 = lambda m: m[:].rearrange("p (m u) -> p m u", u=1).broadcast_to([128, 32, 16])
        def vg(emit, extra=()):
            s.op("vector", emit, waits=[("ve", s.val("ve"))] + list(extra), inc=("ve", 1))
        vg(lambda v: v.reduce_max(r_m1[:], lg_all[:], axis=AX.X), [("ve", ve_lg)])
        vg(lambda v: v.tensor_tensor(r_c[:], lg_all[:], bc32(r_m1), ALU.is_equal))
        vg(lambda v: v.tensor_tensor(r_c[:], r_c[:], imv, ALU.mult))
        vg(lambda v: v.tensor_scalar(r_c[:], r_c[:], 65536.0, None, ALU.add))
        vg(lambda v: v.tensor_reduce(r_i[:], r_c[:], axis=AX.X, op=ALU.min))
        vg(lambda v: v.tensor_tensor(r_c[:], iv, bc32(r_i), ALU.is_equal))
        vg(lambda v: v.scalar_tensor_tensor(r_d[:], r_c[:], -1e30, lg_all[:],
                                            ALU.mult, ALU.add))
        vg(lambda v: v.reduce_max(r_m2[:], r_d[:], axis=AX.X))
        vg(lambda v: v.tensor_tensor(r_d[:], r_d[:], bc32(r_m2), ALU.is_equal))
        vg(lambda v: v.tensor_tensor(r_d[:], r_d[:], imv, ALU.mult))
        vg(lambda v: v.tensor_scalar(r_d[:], r_d[:], 65536.0, None, ALU.add))
        vg(lambda v: v.tensor_reduce(r_i[:], r_d[:], axis=AX.X, op=ALU.min))
        vg(lambda v: v.tensor_tensor(r_d[:], iv, bc32(r_i), ALU.is_equal))
        vg(lambda v: v.tensor_sub(r_p[:], r_m1[:], r_m2[:]))
        s.op("scalar", lambda a: a.activation(r_p[:], r_p[:], AF.Sigmoid, bias=0.0, scale=1.0),
             waits=[("ve", s.val("ve"))], inc=("ac", 1))
        vg(lambda v: v.tensor_tensor(r_c[:], r_c[:], bc32(r_p), ALU.mult),
           [("ac", s.val("ac"))])
        vg(lambda v: v.tensor_scalar(r_p[:], r_p[:], -1.0, 1.0, ALU.mult, ALU.add))
        vg(lambda v: v.tensor_tensor(r_d[:], r_d[:], bc32(r_p), ALU.mult))
        vg(lambda v: v.tensor_add(r_c[:], r_c[:], r_d[:]))
        s.op("gpsimd", lambda e: e.dma_start(
            out=w16.rearrange("(m p) e -> p m e", p=128), in_=r_c[:]),
            waits=[("ve", s.val("ve"))], inc=("so", 16))

        return




def emit_moe(nc, s, dram):
    ap = dram
    xn1 = ap("xn1", [8, 128, S_LEN], BF16, kind="ExternalInput")
    w16 = ap("w16", [S_LEN, 16], F32, kind="ExternalInput")
    eup = ap("eup", [NEXP, DM, EDIM], F32, kind="ExternalInput")
    egt = ap("egt", [NEXP, DM, EDIM], F32, kind="ExternalInput")
    edn = ap("edn", [NEXP, EDIM, DM], F32, kind="ExternalInput")
    shgT = ap("shgT", [DM, DM], F32, kind="ExternalInput")
    shuT = ap("shuT", [DM, DM], F32, kind="ExternalInput")
    shdT = ap("shdT", [DM, DM], F32, kind="ExternalInput")
    moe = ap("moe", [S_LEN, DM], F32, kind="ExternalOutput")
    shb = [ap("shb%d" % i, [8, 128, 1024], BF16, kind="Internal") for i in range(3)]

    es = ExitStack()
    S = lambda n, *a: es.enter_context(nc.sbuf_tensor("b_" + n, *a))
    P = lambda n, *a: es.enter_context(nc.psum_tensor("b_" + n, *a))

    HTOK = S_LEN // 2
    NJM = HTOK // 128
    xh = S("xh", [128, 8, HTOK], BF16)
    acc = S("acc", [128, NJM, DM], F32)
    w16h = S("w16h", [128, NJM, 16], F32)
    gateb = [S("gateb%d" % i, [128, 8, EDIM], BF16) for i in range(2)]
    upb = [S("upb%d" % i, [128, 8, EDIM], BF16) for i in range(2)]
    dnb = [S("dnb%d" % i, [128, 4, DM], BF16) for i in range(2)]
    wstg = [S("wstg%d" % i, [128, 512], F32) for i in range(4)]
    hid = [S("hid0", [128, 4, 512], BF16), S("hid1", [128, 4, 512], BF16)]
    scr = S("scr", [128, 512], F32)
    shw = [S("shw%d_%d" % (i, b), [128, 8, 128], BF16) for i in range(2) for b in range(2)]
    shd = [S("shd%d" % b, [128, 1024], BF16) for b in range(2)]
    shstage = S("shstage", [128, 1024], F32)
    shstg_b = S("shstg_b", [128, 1024], BF16)
    hsh = [S("hsha", [128, 256], BF16), S("hshb", [128, 256], BF16)]
    scrsh = S("scrsh", [128, 256], F32)
    p_g = P("p_g", [128, 512], F32)
    p_u = P("p_u", [128, 512], F32)
    p_d = [P("p_d0m", [128, 512], F32), P("p_d1m", [128, 512], F32),
           P("p_d2m", [128, 512], F32), P("p_d3m", [128, 512], F32)]

    with es:
        for wi, wsrc in enumerate((shgT, shuT, shdT)):
            for hc in range(8):
                if wi < 2:
                    src_ap = wsrc.rearrange("(c p) h -> p c h", p=128)[:, :, hc * 128:(hc + 1) * 128]
                    dst_ap = shstage[:].rearrange("p (c h) -> p c h", c=8)
                else:
                    src_ap = wsrc[hc * 128:(hc + 1) * 128, :]
                    dst_ap = shstage[:]
                s.op("sync", lambda e, src_ap=src_ap, dst_ap=dst_ap: e.dma_start(
                    out=dst_ap, in_=src_ap),
                    waits=s.gate("shstage_free"), inc=("ld", 16))
                s.op("vector", lambda v: v.tensor_copy(shstg_b[:], shstage[:]),
                     waits=[("ld", s.val("ld"))] + s.gate("shstgb_free"), inc=("ve", 1))
                s.setg("shstage_free", "ve")
                s.op("gpsimd", lambda e, wi=wi, hc=hc: e.dma_start(
                    out=shb[wi][hc], in_=shstg_b[:]),
                    waits=[("ve", s.val("ve"))], inc=("wst", 16))
                s.setg("shstgb_free", "wst")
        wst_cast = s.val("wst")

        pend_moe = [None]

        def emit_down_moe(pend):
            kp, ttp, ve_hidp, wbp, ld_xp, gp_accp = pend
            hb = hid[ttp % 2]
            for mt in range(4):
                jm = ttp * 4 + mt
                for dh in range(2):
                    pb = p_d[(mt * 2 + dh) % 2]
                    for ec in range(4):
                        s.op("tensor", lambda t_, ec=ec, mt=mt, dh=dh, pb=pb, wbp=wbp, hb=hb: t_.matmul(
                            pb[:], hb[:, ec, mt * 128:(mt + 1) * 128],
                            dnb[wbp][:, ec, dh * 512:(dh + 1) * 512],
                            start=(ec == 0), stop=(ec == 3)),
                            waits=[("ve", ve_hidp)] + s.gate("pd%d_free" % ((mt * 2 + dh) % 2)),
                            inc=("mm", 1) if ec == 3 else None)
                    s.op("vector", lambda v, jm=jm, dh=dh, pb=pb, kp=kp: v.scalar_tensor_tensor(
                        acc[:, jm, dh * 512:(dh + 1) * 512], pb[:],
                        w16h[:, jm, kp:kp + 1],
                        acc[:, jm, dh * 512:(dh + 1) * 512], ALU.mult, ALU.add),
                        waits=[("mm", s.val("mm")), ("ld", ld_xp), ("gp", gp_accp)],
                        inc=("ve", 1))
                    s.setg("pd%d_free" % ((mt * 2 + dh) % 2), "ve")
            s.setg("mhid%d_free" % (ttp % 2), "mm")
            if ttp == 3:
                s.setg("dnb%d_free" % wbp, "mm")

        wld_map = {}

        def emit_wload_moe(kk, part):
            wbk = kk % 2
            chunks = ([("g", dc) for dc in range(8)] + [("u", dc) for dc in range(8)] +
                      [("d", ec, dh) for ec in range(4) for dh in range(2)])
            sel = chunks[:12] if part == 0 else chunks[12:]
            for j, ch in enumerate(sel):
                sb = (part * 12 + j) % 4
                if ch[0] == "g":
                    dc = ch[1]
                    s.op("sync", lambda e, kk=kk, dc=dc, sb=sb: e.dma_start(
                        out=wstg[sb][:], in_=egt[kk, dc * 128:(dc + 1) * 128, :]),
                        waits=s.gate("wstg%d_free" % sb), inc=("ld", 16))
                    s.op("vector", lambda v, dc=dc, sb=sb, wbk=wbk: v.tensor_copy(
                        gateb[wbk][:, dc, :], wstg[sb][:]),
                        waits=[("ld", s.val("ld"))] + (s.gate("gateb%d_free" % wbk) if dc == 0 else []),
                        inc=("ve", 1))
                elif ch[0] == "u":
                    dc = ch[1]
                    s.op(MOE_Q_U, lambda e, kk=kk, dc=dc, sb=sb: e.dma_start(
                        out=wstg[sb][:], in_=eup[kk, dc * 128:(dc + 1) * 128, :]),
                        waits=s.gate("wstg%d_free" % sb), inc=("ldu", 16))
                    s.op("vector", lambda v, dc=dc, sb=sb, wbk=wbk: v.tensor_copy(
                        upb[wbk][:, dc, :], wstg[sb][:]),
                        waits=[("ldu", s.val("ldu"))] + (s.gate("upb%d_free" % wbk) if dc == 0 else []),
                        inc=("ve", 1))
                else:
                    ec, dh = ch[1], ch[2]
                    s.op(MOE_Q_D, lambda e, kk=kk, ec=ec, dh=dh, sb=sb: e.dma_start(
                        out=wstg[sb][:],
                        in_=edn[kk, ec * 128:(ec + 1) * 128, dh * 512:(dh + 1) * 512]),
                        waits=s.gate("wstg%d_free" % sb), inc=("ldd", 16))
                    s.op("vector", lambda v, ec=ec, dh=dh, sb=sb, wbk=wbk: v.tensor_copy(
                        dnb[wbk][:, ec, dh * 512:(dh + 1) * 512], wstg[sb][:]),
                        waits=[("ldd", s.val("ldd"))] +
                              (s.gate("dnb%d_free" % wbk) if (ec == 0 and dh == 0) else []),
                        inc=("ve", 1))
                s.setg("wstg%d_free" % sb, "ve")
            if part == 1:
                wld_map[kk] = [("ld", s.val("ld")), ("ldu", s.val("ldu")), ("ldd", s.val("ldd"))]

        for h in range(2):
            s.op("sync", lambda e, h=h: e.dma_start(
                out=xh[:], in_=xn1[:, :, h * HTOK:(h + 1) * HTOK].rearrange("c p t -> p c t")),
                waits=s.gate("xh_free"), inc=("ld", 16))
            s.op("sync", lambda e, h=h: e.dma_start(
                out=w16h[:], in_=w16.rearrange("(m p) e -> p m e", p=128)[:, h * NJM:(h + 1) * NJM, :]),
                inc=("ld", 16))
            ld_x = s.val("ld")
            s.op("gpsimd", lambda e: e.memset(acc[:], 0.0),
                 waits=s.gate("acc_free"), inc=("gp", 1))
            gp_acc = s.val("gp")
            emit_wload_moe(0, 0)
            emit_wload_moe(0, 1)
            for k in range(NEXP):
                wb = k % 2
                wld = wld_map[k]
                for tt in range(4):
                    t0 = tt * 512
                    hb = hid[tt % 2]
                    for ec in range(4):
                        for dc in range(8):
                            s.op("tensor", lambda t_, ec=ec, dc=dc, wb=wb, t0=t0: t_.matmul(
                                p_g[:], gateb[wb][:, dc, ec * 128:(ec + 1) * 128],
                                xh[:, dc, t0:t0 + 512],
                                start=(dc == 0), stop=(dc == 7)),
                                waits=wld + [("ld", ld_x)] + s.gate("pg_free"),
                                inc=("mm", 1) if dc == 7 else None)
                        mm_g = s.val("mm")
                        for dc in range(8):
                            s.op("tensor", lambda t_, ec=ec, dc=dc, wb=wb, t0=t0: t_.matmul(
                                p_u[:], upb[wb][:, dc, ec * 128:(ec + 1) * 128],
                                xh[:, dc, t0:t0 + 512],
                                start=(dc == 0), stop=(dc == 7)),
                                waits=s.gate("pu_free"),
                                inc=("mm", 1) if dc == 7 else None)
                        mm_u = s.val("mm")
                        s.op("scalar", lambda a: a.activation(scr[:], p_g[:], AF.Silu,
                                                              bias=0.0, scale=1.0),
                             waits=[("mm", mm_g)] + s.gate("scr_free"), inc=("ac", 1))
                        s.setg("pg_free", "ac")
                        s.op("vector", lambda v, ec=ec, hb=hb: v.tensor_tensor(
                            hb[:, ec, :], scr[:], p_u[:], ALU.mult),
                            waits=[("ac", s.val("ac")), ("mm", mm_u)] + s.gate("mhid%d_free" % (tt % 2)),
                            inc=("ve", 1))
                        s.setg("pu_free", "ve")
                        s.setg("scr_free", "ve")
                    ve_hid = s.val("ve")
                    if pend_moe[0] is not None:
                        emit_down_moe(pend_moe[0])
                    pend_moe[0] = (k, tt, ve_hid, wb, ld_x, gp_acc)
                    if k + 1 < NEXP and tt < 2:
                        emit_wload_moe(k + 1, tt)
                s.setg("gateb%d_free" % wb, "mm")
                s.setg("upb%d_free" % wb, "mm")
            if pend_moe[0] is not None:
                emit_down_moe(pend_moe[0])
                pend_moe[0] = None

            def emit_down_sh(pend):
                hcp, ve_hshp = pend
                sbp = hcp % 2
                for mt in range(2):
                    for dh in range(2):
                        pi = mt * 2 + dh
                        s.op("tensor", lambda t_, mt=mt, dh=dh, pi=pi, sbp=sbp, hcp=hcp: t_.matmul(
                            p_d[pi][:], hsh[sbp][:, mt * 128:(mt + 1) * 128],
                            shd[sbp][:, dh * 512:(dh + 1) * 512],
                            start=(hcp == 0), stop=(hcp == 7)),
                            waits=[("ve", ve_hshp)] +
                                  (s.gate("pd0_free", "pd1_free", "pd2_free", "pd3_free")
                                   if hcp == 0 else []),
                            inc=("mm", 1) if (mt == 1 and dh == 1) else None)
                s.setg("hsh%d_free" % sbp, "mm")
                s.setg("shwg%d_free" % sbp, "mm")

            for tt in range(8):
                t0 = tt * 256
                pend_sh = None
                for hc in range(8):
                    sb = hc % 2
                    wait_cast = [("wst", wst_cast)] if (h == 0 and tt == 0 and hc < 2) else []
                    s.op("sync", lambda e, hc=hc, sb=sb: e.dma_start(
                        out=shw[0 * 2 + sb][:].rearrange("p a b -> p (a b)"), in_=shb[0][hc]),
                        waits=wait_cast + s.gate("shwg%d_free" % sb), inc=("ld", 16))
                    s.op("sync", lambda e, hc=hc, sb=sb: e.dma_start(
                        out=shw[1 * 2 + sb][:].rearrange("p a b -> p (a b)"), in_=shb[1][hc]),
                        inc=("ld", 16))
                    s.op("sync", lambda e, hc=hc, sb=sb: e.dma_start(
                        out=shd[sb][:], in_=shb[2][hc]),
                        inc=("ld", 16))
                    shld = s.val("ld")
                    for dc in range(8):
                        s.op("tensor", lambda t_, dc=dc, sb=sb, t0=t0: t_.matmul(
                            p_g[:, 0:256], shw[sb][:, dc, :], xh[:, dc, t0:t0 + 256],
                            start=(dc == 0), stop=(dc == 7)),
                            waits=[("ld", shld)] + s.gate("pg_free"),
                            inc=("mm", 1) if dc == 7 else None)
                    mm_g = s.val("mm")
                    for dc in range(8):
                        s.op("tensor", lambda t_, dc=dc, sb=sb, t0=t0: t_.matmul(
                            p_u[:, 0:256], shw[2 + sb][:, dc, :], xh[:, dc, t0:t0 + 256],
                            start=(dc == 0), stop=(dc == 7)),
                            waits=s.gate("pu_free"),
                            inc=("mm", 1) if dc == 7 else None)
                    mm_u = s.val("mm")
                    s.op("scalar", lambda a: a.activation(scrsh[:], p_g[:, 0:256], AF.Silu,
                                                          bias=0.0, scale=1.0),
                         waits=[("mm", mm_g)] + s.gate("scrsh_free"), inc=("ac", 1))
                    s.setg("pg_free", "ac")
                    s.op("vector", lambda v, sb=sb: v.tensor_tensor(hsh[sb][:], scrsh[:], p_u[:, 0:256], ALU.mult),
                         waits=[("ac", s.val("ac")), ("mm", mm_u)] + s.gate("hsh%d_free" % sb),
                         inc=("ve", 1))
                    s.setg("pu_free", "ve")
                    s.setg("scrsh_free", "ve")
                    ve_hsh = s.val("ve")
                    if pend_sh is not None:
                        emit_down_sh(pend_sh)
                    pend_sh = (hc, ve_hsh)
                emit_down_sh(pend_sh)
                mm_shd = s.val("mm")
                for mt in range(2):
                    jm = tt * 2 + mt
                    for dh in range(2):
                        pi = mt * 2 + dh
                        s.op("vector", lambda v, jm=jm, dh=dh, pi=pi: v.tensor_tensor(
                            acc[:, jm, dh * 512:(dh + 1) * 512], p_d[pi][:],
                            acc[:, jm, dh * 512:(dh + 1) * 512], ALU.add),
                            waits=[("mm", mm_shd)],
                            inc=("ve", 1) if pi == 3 else None)
                for pi in range(4):
                    s.setg("pd%d_free" % pi, "ve")
            ve_acc = s.val("ve")
            s.op("gpsimd", lambda e, h=h: e.dma_start(
                out=moe.rearrange("(m p) d -> p m d", p=128)[:, h * NJM:(h + 1) * NJM, :],
                in_=acc[:]),
                waits=[("ve", ve_acc)], inc=("so", 16))
            s.setg("acc_free", "so")
            s.setg("xh_free", "so")

        return




def emit_rope_ops(s, ktmp, krop, tmpa, tmpb, cost, sint, jt):
    cosv = cost[:, jt, :].rearrange("p (u i) -> p u i", u=1).broadcast_to([128, 8, 32])
    sinv = sint[:, jt, :].rearrange("p (u i) -> p u i", u=1).broadcast_to([128, 8, 32])
    tv = ktmp[:].rearrange("p (h i z) -> p h i z", h=8, z=2)
    E = tv[:, :, :, 0]
    O = tv[:, :, :, 1]
    ov = krop[:].rearrange("p (h z i) -> p h z i", h=8, z=2)
    outA = ov[:, :, 0, :]
    outB = ov[:, :, 1, :]
    ta = tmpa[:].rearrange("p (h i) -> p h i", h=8)
    tb = tmpb[:].rearrange("p (h i) -> p h i", h=8)
    def vg(emit):
        s.op("vector", emit, waits=[("ve", s.val("ve"))], inc=("ve", 1))
    vg(lambda v: v.tensor_tensor(ta, E, cosv, ALU.mult))
    vg(lambda v: v.tensor_tensor(tb, O, sinv, ALU.mult))
    vg(lambda v: v.tensor_tensor(outA, ta, tb, ALU.subtract))
    vg(lambda v: v.tensor_tensor(ta, E, sinv, ALU.mult))
    vg(lambda v: v.tensor_tensor(tb, O, cosv, ALU.mult))
    vg(lambda v: v.tensor_tensor(outB, ta, tb, ALU.add))


def emit_attn(nc, s, dram):
    ap = dram
    xn1 = ap("xn1", [8, 128, S_LEN], BF16, kind="ExternalInput")
    x1 = ap("x1", [S_LEN, DM], F32, kind="ExternalInput")
    moe = ap("moe", [S_LEN, DM], F32, kind="ExternalInput")
    wqa = ap("wqa", [DM, DM], F32, kind="ExternalInput")
    wka = ap("wka", [DM, DM], F32, kind="ExternalInput")
    wva = ap("wva", [DM, DM], F32, kind="ExternalInput")
    woa = ap("woa", [DM, DM], F32, kind="ExternalInput")
    bq = ap("bq", [1, DM], F32, kind="ExternalInput")
    bk = ap("bk", [1, DM], F32, kind="ExternalInput")
    bv = ap("bv", [1, DM], F32, kind="ExternalInput")
    bo = ap("bo", [1, DM], F32, kind="ExternalInput")
    rcos = ap("rcos", [S_LEN, 32], F32, kind="ExternalInput")
    rsin = ap("rsin", [S_LEN, 32], F32, kind="ExternalInput")
    idb = ap("idb", [128, 128], BF16, kind="ExternalInput")
    dmask = ap("dmask", [4, 128, 512], BF16, kind="ExternalInput")
    out = ap("out", [S_LEN, DM], F32, kind="ExternalOutput")
    att0 = ap("att0", [8, 128, 4, 512], BF16, kind="Internal")

    es = ExitStack()
    S = lambda n, *a: es.enter_context(nc.sbuf_tensor("c_" + n, *a))
    P = lambda n, *a: es.enter_context(nc.psum_tensor("c_" + n, *a))

    NQT = 8
    xn1t = [S("xn1t0", [128, 8, 512], BF16), S("xn1t1", [128, 8, 512], BF16)]
    kfm = S("kfm", [128, 4, S_LEN], BF16)
    vaug = S("vaug", [128, 32, 520], BF16)
    qfm = S("qfm", [128, 4, 512], BF16)
    attg0t = [S("attg0t0", [128, 4, 512], BF16), S("attg0t1", [128, 4, 512], BF16)]
    atile = S("atile", [128, 4, 512], BF16)
    wqb = S("wqb", [128, 8, 512], BF16)
    wkv = S("wkv", [128, 8, 512], BF16)
    wob = S("wob", [128, 8, DM], BF16)
    wstg = [S("wstg0", [128, 512], F32), S("wstg1", [128, 512], F32)]
    cost = S("cost", [128, 32, 32], F32)
    sint = S("sint", [128, 32, 32], F32)
    maskt = S("maskt", [128, 4, 512], BF16)
    ktmp = S("ktmp", [128, 512], F32)
    krop = S("krop", [128, 512], F32)
    tmpa = S("tmpa", [128, 256], F32)
    tmpb = S("tmpb", [128, 256], F32)
    krb = S("krb", [128, 512], BF16)
    sqs = S("sqs", [128, 512], F32)
    knall = S("knall", [128, 32, 8], F32)
    qnall = S("qnall", [128, 32, 8], F32)
    knpad = S("knpad", [128, 128], BF16)
    mstat = S("mstat", [128, 8], F32)
    negMb = S("negMb", [128, 1], F32)
    rs_t = S("rs_t", [128, 512], F32)
    rb_s = S("rb_s", [64, 512], F32)
    atnb = S("atnb", [64, 512], BF16)
    pbuf = [S("pbuf%d" % i, [128, 512], BF16) for i in range(3)]
    bqb = S("bqb", [128, 512], F32)
    bkb = S("bkb", [128, 512], F32)
    bvb = S("bvb", [128, 512], F32)
    bob = S("bob", [128, DM], F32)
    x1m = S("x1m", [128, DM], F32)
    moem = S("moem", [128, DM], F32)
    base = S("base", [128, DM], F32)
    outt = S("outt", [128, DM], F32)
    onesb = S("onesb", [128, 128], F32)
    idbt = S("idbt", [128, 128], BF16)
    id64 = S("id64", [128, 128], BF16)

    p_sc = [P("p_sc%d" % i, [128, 512], F32) for i in range(3)]
    p_at = P("p_at", [128, 512], F32)
    p_aux = P("p_auxa", [128, 512], F32)
    p_pr = P("p_pr", [128, 512], F32)
    p_tr = P("p_tra", [128, 1024], BF16)

    with es:
        s.op("sync", lambda e: e.dma_start(out=idbt[:], in_=idb[:]), inc=("ld", 16))
        s.op("sync", lambda e: e.dma_start(
            out=cost[:], in_=rcos.rearrange("(j p) i -> p j i", p=128)), inc=("ld", 16))
        s.op("sync", lambda e: e.dma_start(
            out=sint[:], in_=rsin.rearrange("(j p) i -> p j i", p=128)), inc=("ld", 16))
        s.op("sync", lambda e: e.dma_start(
            out=maskt[:], in_=dmask.rearrange("m p f -> p m f")), inc=("ld", 16))
        s.op("sync", lambda e: e.dma_start(out=bob[0:1, :], in_=bo[:]), inc=("ld", 16))
        ld_c = s.val("ld")
        s.op("gpsimd", lambda e: e.memset(onesb[:], 1.0), inc=("gp", 1))
        s.op("gpsimd", lambda e: e.memset(knpad[:], 0.0), inc=("gp", 1))
        s.op("gpsimd", lambda e: e.memset(id64[:], 0.0), inc=("gp", 1))
        gp_c = s.val("gp")
        s.op("vector", lambda v: v.tensor_copy(id64[0:64, 0:64], idbt[0:64, 0:64]),
             waits=[("ld", ld_c), ("gp", gp_c)], inc=("ve", 1))
        for half in range(2):
            s.op("tensor", lambda t_, half=half: t_.matmul(
                p_aux[:], onesb[0:1, :], bob[0:1, half * 512:(half + 1) * 512],
                start=True, stop=True),
                waits=[("ld", ld_c), ("gp", gp_c)] + s.gate("paux_free"), inc=("mm", 1))
            s.op("vector", lambda v, half=half: v.tensor_copy(
                bob[:, half * 512:(half + 1) * 512], p_aux[:]),
                waits=[("mm", s.val("mm"))], inc=("ve", 1))
            s.setg("paux_free", "ve")

        def load_whalf(whbm, dst, tag, gc, ncols=512):
            for dc in range(8):
                sb = dc % 2
                s.op("sync", lambda e, whbm=whbm, dc=dc, sb=sb, gc=gc, ncols=ncols: e.dma_start(
                    out=wstg[sb][:, 0:ncols], in_=whbm[dc * 128:(dc + 1) * 128, gc:gc + ncols]),
                    waits=s.gate("wstg%d_free" % sb), inc=("ld", 16))
                s.op("vector", lambda v, dc=dc, sb=sb, dst=dst, ncols=ncols: v.tensor_copy(
                    dst[:, dc, 0:ncols], wstg[sb][:, 0:ncols]),
                    waits=[("ld", s.val("ld"))] + (s.gate(tag + "_free") if dc == 0 else []),
                    inc=("ve", 1))
                s.setg("wstg%d_free" % sb, "ve")
            s.setg(tag + "_ready", "ve")

        def proj_psum(src_tile, wtile, mt, wait_extra):
            for dc in range(8):
                s.op("tensor", lambda t_, dc=dc, src_tile=src_tile, wtile=wtile, mt=mt: t_.matmul(
                    p_pr[:], src_tile[:, dc, mt * 128:(mt + 1) * 128], wtile[:, dc, 0:512],
                    start=(dc == 0), stop=(dc == 7)),
                    waits=wait_extra + (s.gate("ppr_free") if dc == 0 else []),
                    inc=("mm", 1) if dc == 7 else None)
            return s.val("mm")

        def pmax_to_scalar(src_col_f32, dst_slot):
            s.op("vector", lambda v: v.tensor_copy(knpad[:, 0:1], src_col_f32),
                 waits=[("ve", s.val("ve"))] + s.gate("knpad_free"), inc=("ve", 1))
            s.op("tensor", lambda t_: t_.transpose(p_tr[:, 0:128], knpad[:], idbt[:]),
                 waits=[("ve", s.val("ve")), ("ld", ld_c)] + s.gate("ptr_free"),
                 inc=("mm", 1))
            s.op("vector", lambda v, dst_slot=dst_slot: v.reduce_max(
                mstat[0:1, dst_slot:dst_slot + 1], p_tr[0:1, 0:128], axis=AX.X),
                waits=[("mm", s.val("mm"))], inc=("ve", 1))
            s.setg("ptr_free", "ve")
            s.setg("knpad_free", "ve")

        for g in range(2):
            gc = g * 512
            for (bsrc, btile) in ((bq, bqb), (bk, bkb), (bv, bvb)):
                s.op("sync", lambda e, bsrc=bsrc, btile=btile, gc=gc: e.dma_start(
                    out=btile[0:1, :], in_=bsrc[:, gc:gc + 512]),
                    inc=("ld", 16))
                s.op("tensor", lambda t_, btile=btile: t_.matmul(
                    p_aux[:], onesb[0:1, :], btile[0:1, :], start=True, stop=True),
                    waits=[("ld", s.val("ld")), ("gp", gp_c)] + s.gate("paux_free"),
                    inc=("mm", 1))
                s.op("vector", lambda v, btile=btile: v.tensor_copy(btile[:], p_aux[:]),
                     waits=[("mm", s.val("mm"))], inc=("ve", 1))
                s.setg("paux_free", "ve")
            load_whalf(wka, wkv, "wkv", gc)
            s.op("gpsimd", lambda e: e.memset(vaug[:], 1.0),
                 waits=s.gate("vaug_free"), inc=("gp", 1))
            gp_vaug = s.val("gp")

            for tt in range(NQT):
                t0 = tt * 512
                xb = xn1t[tt % 2]
                s.op("sync", lambda e, t0=t0, xb=xb: e.dma_start(
                    out=xb[:], in_=xn1[:, :, t0:t0 + 512].rearrange("c p t -> p c t")),
                    waits=s.gate("xn1t%d_free" % (tt % 2)), inc=("ld", 16))
                ld_x = s.val("ld")
                for mt in range(4):
                    jt = tt * 4 + mt
                    mm_k = proj_psum(xb, wkv, mt, [("ld", ld_x)] + s.gate("wkv_ready"))
                    s.op("vector", lambda v: v.tensor_tensor(ktmp[:], p_pr[:], bkb[:], ALU.add),
                         waits=[("mm", mm_k)] + s.gate("ktmp_free"), inc=("ve", 1))
                    s.setg("ppr_free", "ve")
                    ve_ktmp = s.val("ve")
                    s.op("scalar", lambda a: a.activation(sqs[:], ktmp[:], AF.Square,
                                                          bias=0.0, scale=1.0),
                         waits=[("ve", ve_ktmp)] + s.gate("sqs_free"), inc=("ac", 1))
                    s.op("vector", lambda v, jt=jt: v.reduce_sum(
                        knall[:, jt, :], sqs[:].rearrange("p (h d) -> p h d", h=8), axis=AX.X),
                        waits=[("ac", s.val("ac"))], inc=("ve", 1))
                    s.setg("sqs_free", "ve")
                    emit_rope_ops(s, ktmp, krop, tmpa, tmpb, cost, sint, jt)
                    s.op("vector", lambda v: v.tensor_copy(krb[:], krop[:]),
                         waits=[("ve", s.val("ve"))], inc=("ve", 1))
                    s.setg("ktmp_free", "ve")
                    ve_krb = s.val("ve")
                    for c in range(4):
                        s.op("tensor", lambda t_, c=c: t_.transpose(
                            p_tr[:, 0:128], krb[:, c * 128:(c + 1) * 128], idbt[:]),
                            waits=[("ve", ve_krb), ("ld", ld_c)] + s.gate("ptr_free"),
                            inc=("mm", 1))
                        s.op("vector", lambda v, c=c, jt=jt: v.tensor_copy(
                            kfm[:, c, jt * 128:(jt + 1) * 128], p_tr[:, 0:128]),
                            waits=[("mm", s.val("mm"))], inc=("ve", 1))
                        s.setg("ptr_free", "ve")
                s.setg("xn1t%d_free" % (tt % 2), "mm")
            s.op("vector", lambda v: v.reduce_max(
                mstat[:, 2:3], knall[:].rearrange("p a b -> p (a b)"), axis=AX.X),
                waits=[("ve", s.val("ve"))], inc=("ve", 1))
            pmax_to_scalar(mstat[:, 2:3], 0)
            s.setg("wkv_free", "mm")
            load_whalf(wva, wkv, "wkv", gc)
            for tt in range(NQT):
                t0 = tt * 512
                xb = xn1t[tt % 2]
                s.op("sync", lambda e, t0=t0, xb=xb: e.dma_start(
                    out=xb[:], in_=xn1[:, :, t0:t0 + 512].rearrange("c p t -> p c t")),
                    waits=s.gate("xn1t%d_free" % (tt % 2)), inc=("ld", 16))
                ld_x = s.val("ld")
                for mt in range(4):
                    jt = tt * 4 + mt
                    mm_v = proj_psum(xb, wkv, mt, [("ld", ld_x)] + s.gate("wkv_ready"))
                    vv = vaug[:, jt, :].rearrange("p (h e) -> p h e", h=8)[:, :, 0:64]
                    pv = p_pr[:].rearrange("p (h e) -> p h e", h=8)
                    bb = bvb[:].rearrange("p (h e) -> p h e", h=8)
                    s.op("vector", lambda v, vv=vv, pv=pv, bb=bb: v.tensor_tensor(
                        vv, pv, bb, ALU.add),
                        waits=[("mm", mm_v), ("gp", gp_vaug)], inc=("ve", 1))
                    s.setg("ppr_free", "ve")
                s.setg("xn1t%d_free" % (tt % 2), "mm")
            s.setg("wkv_free", "mm")
            load_whalf(wqa, wqb, "wqb", gc)
            if g == 1:
                for dc in range(8):
                    for half in range(2):
                        sb = half
                        s.op("sync", lambda e, dc=dc, half=half, sb=sb: e.dma_start(
                            out=wstg[sb][:], in_=woa[dc * 128:(dc + 1) * 128,
                                                    half * 512:(half + 1) * 512]),
                            waits=s.gate("wstg%d_free" % sb), inc=("ld", 16))
                        s.op("vector", lambda v, dc=dc, half=half, sb=sb: v.tensor_copy(
                            wob[:, dc, half * 512:(half + 1) * 512], wstg[sb][:]),
                            waits=[("ld", s.val("ld"))], inc=("ve", 1))
                        s.setg("wstg%d_free" % sb, "ve")
            ve_wo = s.val("ve")

            for qt in range(NQT):
                t0 = qt * 512
                nkc = 4 * (qt + 1)
                xb = xn1t[qt % 2]
                s.op("sync", lambda e, t0=t0, xb=xb: e.dma_start(
                    out=xb[:], in_=xn1[:, :, t0:t0 + 512].rearrange("c p t -> p c t")),
                    waits=s.gate("xn1t%d_free" % (qt % 2)), inc=("ld", 16))
                ld_x = s.val("ld")
                if g == 1:
                    ab = attg0t[qt % 2]
                    s.op("sync", lambda e, qt=qt, ab=ab: e.dma_start(
                        out=ab[:], in_=att0[qt]),
                        waits=s.gate("attg0t%d_free" % (qt % 2)), inc=("ld", 16))
                else:
                    ab = None
                ld_a = s.val("ld")
                for mt in range(4):
                    jt = qt * 4 + mt
                    mm_q = proj_psum(xb, wqb, mt, [("ld", ld_x)] + s.gate("wqb_ready"))
                    s.op("vector", lambda v: v.tensor_tensor(ktmp[:], p_pr[:], bqb[:], ALU.add),
                         waits=[("mm", mm_q)] + s.gate("ktmp_free"), inc=("ve", 1))
                    s.setg("ppr_free", "ve")
                    ve_ktmp = s.val("ve")
                    s.op("scalar", lambda a: a.activation(sqs[:], ktmp[:], AF.Square,
                                                          bias=0.0, scale=1.0),
                         waits=[("ve", ve_ktmp)] + s.gate("sqs_free"), inc=("ac", 1))
                    s.op("vector", lambda v, jt=jt: v.reduce_sum(
                        qnall[:, jt, :], sqs[:].rearrange("p (h d) -> p h d", h=8), axis=AX.X),
                        waits=[("ac", s.val("ac"))], inc=("ve", 1))
                    s.setg("sqs_free", "ve")
                    emit_rope_ops(s, ktmp, krop, tmpa, tmpb, cost, sint, jt)
                    s.op("vector", lambda v: v.tensor_copy(krb[:], krop[:]),
                         waits=[("ve", s.val("ve"))], inc=("ve", 1))
                    s.setg("ktmp_free", "ve")
                    ve_krb = s.val("ve")
                    for c in range(4):
                        s.op("tensor", lambda t_, c=c, mt=mt: t_.transpose(
                            p_tr[:, 0:128], krb[:, c * 128:(c + 1) * 128], idbt[:]),
                            waits=[("ve", ve_krb)] + s.gate("ptr_free"),
                            inc=("mm", 1))
                        s.op("vector", lambda v, c=c, mt=mt: v.tensor_copy(
                            qfm[:, c, mt * 128:(mt + 1) * 128], p_tr[:, 0:128]),
                            waits=[("mm", s.val("mm"))], inc=("ve", 1))
                        s.setg("ptr_free", "ve")
                s.op("vector", lambda v, qt=qt: v.reduce_max(
                    mstat[:, 3:4],
                    qnall[:, qt * 4:(qt + 1) * 4, :].rearrange("p a b -> p (a b)"), axis=AX.X),
                    waits=[("ve", s.val("ve"))], inc=("ve", 1))
                pmax_to_scalar(mstat[:, 3:4], 1)
                def vg1(emit):
                    s.op("vector", emit, waits=[("ve", s.val("ve"))], inc=("ve", 1))
                vg1(lambda v: v.tensor_mul(mstat[0:1, 4:5], mstat[0:1, 0:1], mstat[0:1, 1:2]))
                vg1(lambda v: v.tensor_scalar(mstat[0:1, 4:5], mstat[0:1, 4:5], 1e-30, None, ALU.add))
                s.op("scalar", lambda a: a.activation(mstat[0:1, 5:6], mstat[0:1, 4:5],
                                                      AF.Ln, bias=0.0, scale=1.0),
                     waits=[("ve", s.val("ve"))], inc=("ac", 1))
                s.op("scalar", lambda a: a.activation(mstat[0:1, 6:7], mstat[0:1, 5:6],
                                                      AF.Exp, bias=0.0, scale=0.5),
                     waits=[("ac", s.val("ac"))], inc=("ac", 1))
                vg1(lambda v: v.tensor_scalar(mstat[0:1, 7:8], mstat[0:1, 6:7], -0.125, None, ALU.mult))
                s.op("tensor", lambda t_: t_.matmul(
                    p_aux[:, 0:1], onesb[0:1, :], mstat[0:1, 7:8], start=True, stop=True),
                    waits=[("ve", s.val("ve")), ("ac", s.val("ac"))] + s.gate("paux_free"),
                    inc=("mm", 1))
                s.op("vector", lambda v: v.tensor_copy(negMb[:], p_aux[:, 0:1]),
                     waits=[("mm", s.val("mm"))] + s.gate("negM_inuse"), inc=("ve", 1))
                s.setg("paux_free", "ve")
                s.setg("negM_ready", "ve")
                ve_qfm = s.val("ve")
                for hl in range(8):
                    cc, ro = hl // 2, 64 * (hl % 2)

                    def emit_pv(prev, hl=hl, nkc=nkc):
                        kc, ac_exp, ve_pb = prev
                        pp = kc % (ATTN_LAG + 1)
                        s.op("tensor", lambda t_, kc=kc, pp=pp, hl=hl, nkc=nkc: t_.matmul(
                            p_at[0:65, :], vaug[:, kc, hl * 65:hl * 65 + 65],
                            pbuf[pp][:],
                            start=(kc == 0), stop=(kc == nkc - 1)),
                            waits=[("ac", ac_exp), ("ve", ve_pb)] +
                                  (s.gate("pat_free") if kc == 0 else []),
                            inc=("mm", 1) if kc == nkc - 1 else None)
                        s.setg("pbuf%d_free" % pp, "mm")

                    pend = []
                    for kc in range(nkc):
                        pp = kc % (ATTN_LAG + 1)
                        s.op("tensor", lambda t_, kc=kc, pp=pp, cc=cc, ro=ro: t_.matmul(
                            p_sc[pp][:],
                            kfm[:, cc, kc * 128:(kc + 1) * 128][ro:ro + 64, :],
                            qfm[:, cc, :][ro:ro + 64, :],
                            start=True, stop=True),
                            waits=[("ve", ve_qfm)] + s.gate("psc%d_free" % pp),
                            inc=("mm", 1))
                        mm_st = s.val("mm")
                        s.op("scalar", lambda a, pp=pp: a.activation(
                            pbuf[pp][:], p_sc[pp][:],
                            AF.Exp, bias=negMb[:, 0:1], scale=0.125),
                            waits=[("mm", mm_st)] + s.gate("pbuf%d_free" % pp) +
                                  s.gate("negM_ready"),
                            inc=("ac", 1))
                        s.setg("psc%d_free" % pp, "ac")
                        ac_exp = s.val("ac")
                        j = kc - 4 * qt
                        if j >= 0:
                            s.op("vector", lambda v, pp=pp, j=j: v.tensor_tensor(
                                pbuf[pp][:], pbuf[pp][:],
                                maskt[:, j, :], ALU.mult),
                                waits=[("ac", ac_exp), ("ld", ld_c)], inc=("ve", 1))
                        ve_pb = s.val("ve")
                        pend.append((kc, ac_exp, ve_pb))
                        if len(pend) > ATTN_LAG:
                            emit_pv(pend.pop(0))
                    for pr in pend:
                        emit_pv(pr)
                    mm_pv = s.val("mm")
                    s.op("vector", lambda v: v.reciprocal(rs_t[64:65, :], p_at[64:65, :]),
                         waits=[("mm", mm_pv)] + s.gate("rst_free"), inc=("ve", 1))
                    s.op("tensor", lambda t_: t_.matmul(
                        p_aux[0:64, :], onesb[64:65, 0:64], rs_t[64:65, :],
                        start=True, stop=True),
                        waits=[("ve", s.val("ve"))] + s.gate("paux_free"), inc=("mm", 1))
                    s.op("vector", lambda v: v.tensor_copy(rb_s[:], p_aux[0:64, :]),
                         waits=[("mm", s.val("mm"))], inc=("ve", 1))
                    s.setg("rst_free", "ve")
                    if ro == 0:
                        s.op("vector", lambda v, cc=cc: v.tensor_tensor(
                            atile[0:64, cc, :], p_at[0:64, :], rb_s[:], ALU.mult),
                            waits=[("ve", s.val("ve"))] +
                                  (s.gate("atile_free") if hl == 0 else []),
                            inc=("ve", 1))
                        s.setg("pat_free", "ve")
                        s.setg("paux_free", "ve")
                        s.setg("negM_inuse", "ve")
                    else:
                        s.op("vector", lambda v: v.tensor_tensor(
                            atnb[:], p_at[0:64, :], rb_s[:], ALU.mult),
                            waits=[("ve", s.val("ve"))], inc=("ve", 1))
                        s.setg("pat_free", "ve")
                        s.op("tensor", lambda t_: t_.matmul(
                            p_aux[64:128, :], id64[0:64, 0:64], atnb[:],
                            start=True, stop=True),
                            waits=[("ve", s.val("ve"))], inc=("mm", 1))
                        s.op("vector", lambda v, cc=cc: v.tensor_copy(
                            atile[64:128, cc, :], p_aux[64:128, :]),
                            waits=[("mm", s.val("mm"))], inc=("ve", 1))
                        s.setg("paux_free", "ve")
                        s.setg("negM_inuse", "ve")
                ve_atile = s.val("ve")
                if g == 0:
                    s.op("gpsimd", lambda e, qt=qt: e.dma_start(out=att0[qt], in_=atile[:]),
                         waits=[("ve", ve_atile)], inc=("so", 16))
                    s.setg("atile_free", "so")
                else:
                    for mt in range(4):
                        jt = qt * 4 + mt
                        s.op("sync", lambda e, jt=jt: e.dma_start(
                            out=x1m[:], in_=x1.rearrange("(m p) d -> p m d", p=128)[:, jt, :]),
                            waits=s.gate("x1m_free"), inc=("ld", 16))
                        s.op("sync", lambda e, jt=jt: e.dma_start(
                            out=moem[:], in_=moe.rearrange("(m p) d -> p m d", p=128)[:, jt, :]),
                            inc=("ld", 16))
                        ld_e = s.val("ld")
                        s.op("vector", lambda v: v.tensor_add(base[:], x1m[:], moem[:]),
                             waits=[("ld", ld_e)] + s.gate("base_free"), inc=("ve", 1))
                        s.op("vector", lambda v: v.tensor_add(base[:], base[:], bob[:]),
                             waits=[("ve", s.val("ve"))], inc=("ve", 1))
                        s.setg("x1m_free", "ve")
                        ve_base = s.val("ve")
                        for dh in range(2):
                            pw = p_pr if dh == 0 else p_at
                            gkey = "ppr_free" if dh == 0 else "pat_free"
                            for c in range(4):
                                s.op("tensor", lambda t_, c=c, mt=mt, dh=dh, pw=pw, ab=ab: t_.matmul(
                                    pw[:], ab[:, c, mt * 128:(mt + 1) * 128],
                                    wob[:, c, dh * 512:(dh + 1) * 512],
                                    start=(c == 0), stop=False),
                                    waits=([("ld", ld_a), ("ve", ve_atile), ("ve", ve_wo)] +
                                           (s.gate(gkey) if c == 0 else [])))
                            for c in range(4):
                                s.op("tensor", lambda t_, c=c, mt=mt, dh=dh, pw=pw: t_.matmul(
                                    pw[:], atile[:, c, mt * 128:(mt + 1) * 128],
                                    wob[:, 4 + c, dh * 512:(dh + 1) * 512],
                                    start=False, stop=(c == 3)),
                                    inc=("mm", 1) if c == 3 else None)
                            s.op("vector", lambda v, dh=dh, pw=pw: v.tensor_tensor(
                                outt[:, dh * 512:(dh + 1) * 512], pw[:],
                                base[:, dh * 512:(dh + 1) * 512], ALU.add),
                                waits=[("mm", s.val("mm")), ("ve", ve_base)], inc=("ve", 1))
                            s.setg("ppr_free" if dh == 0 else "pat_free", "ve")
                        s.setg("base_free", "ve")
                        ve_out = s.val("ve")
                        s.op("gpsimd", lambda e, jt=jt: e.dma_start(
                            out=out.rearrange("(m p) d -> p m d", p=128)[:, jt, :], in_=outt[:]),
                            waits=[("ve", ve_out)], inc=("so", 16))
                    s.setg("attg0t%d_free" % (qt % 2), "mm")
                    s.setg("atile_free", "mm")
                s.setg("xn1t%d_free" % (qt % 2), "mm")
            s.setg("vaug_free", "mm")
            s.setg("wqb_free", "mm")
            s.setg("wkv_free", "mm")

        return



def _l0_host_inputs(inputs, core):
    f32 = np.float32
    d = {}
    d["x"] = np.ascontiguousarray(np.asarray(inputs["L0_attn_x"])[core], dtype=f32)
    d["wq"] = np.ascontiguousarray(np.asarray(inputs["L0_attn_W_Q"], dtype=f32))
    d["wk"] = np.ascontiguousarray(np.asarray(inputs["L0_attn_W_K"], dtype=f32))
    d["wkp"] = np.ascontiguousarray(np.asarray(inputs["L0_attn_W_Kp"], dtype=f32))
    d["wv"] = np.ascontiguousarray(np.asarray(inputs["L0_attn_W_V"], dtype=f32))
    d["wo"] = np.ascontiguousarray(np.asarray(inputs["L0_attn_W_O"], dtype=f32))
    d["gam"] = np.ascontiguousarray(np.asarray(inputs["L0_attn_ln_gamma"], dtype=f32)[None, :])
    d["bet"] = np.ascontiguousarray(np.asarray(inputs["L0_attn_ln_beta"], dtype=f32)[None, :])
    d["wgT"] = np.ascontiguousarray(np.asarray(inputs["L0_ffn_w_gate"], dtype=np.int32).T)
    d["wuT"] = np.ascontiguousarray(np.asarray(inputs["L0_ffn_w_up"], dtype=np.int32).T)
    d["wdT"] = np.ascontiguousarray(np.asarray(inputs["L0_ffn_w_down"], dtype=np.int32).T)
    d["rwT"] = np.ascontiguousarray(np.asarray(inputs["L1_ffn_router_w"], dtype=f32).T)
    d["rbias"] = np.ascontiguousarray(np.asarray(inputs["L1_ffn_router_bias"], dtype=f32)[None, :])
    d["idb"] = np.eye(128, dtype=BFNP)
    q = np.arange(32)[:, None, None]
    j = np.arange(32)[None, :, None]
    k = np.arange(32)[None, None, :]
    wm = np.where((j > q) | (k > q), np.float32(-1e9), np.float32(0.0))
    d["wmask"] = np.ascontiguousarray(wm.reshape(32, 1024))
    d["idx16"] = np.ascontiguousarray(np.broadcast_to(np.arange(16, dtype=f32), (128, 16)))
    return d


def _host_consts_l0():
    f32 = np.float32
    q = np.arange(32)[:, None, None]
    j = np.arange(32)[None, :, None]
    k = np.arange(32)[None, None, :]
    wm = np.where((j > q) | (k > q), np.float32(-1e9), np.float32(0.0))
    return {
        "idb": np.eye(128, dtype=BFNP),
        "wmask": np.ascontiguousarray(wm.reshape(32, 1024).astype(f32)),
        "idx16": np.ascontiguousarray(np.broadcast_to(np.arange(16, dtype=f32), (128, 16))),
    }


def _host_consts_attn():
    f32 = np.float32
    inv = (10000.0 ** (-2.0 * np.arange(32, dtype=f32) / 64.0)).astype(f32)
    ang = np.arange(S_LEN, dtype=f32)[:, None] * inv[None, :]
    kk = np.arange(128)[:, None]
    qq = np.arange(512)[None, :]
    dm = np.stack([(kk <= qq - 128 * j).astype(BFNP) for j in range(4)])
    return {
        "rcos": np.cos(ang).astype(f32),
        "rsin": np.sin(ang).astype(f32),
        "idb": np.eye(128, dtype=BFNP),
        "dmask": np.ascontiguousarray(dm),
    }


def kernel(**inputs):
    f32 = np.float32
    A = lambda k: np.asarray(inputs[k], f32)
    x_all = A("L0_attn_x")

    l0_shared = {
        "wq": np.ascontiguousarray(A("L0_attn_W_Q")),
        "wk": np.ascontiguousarray(A("L0_attn_W_K")),
        "wkp": np.ascontiguousarray(A("L0_attn_W_Kp")),
        "wv": np.ascontiguousarray(A("L0_attn_W_V")),
        "wo": np.ascontiguousarray(A("L0_attn_W_O")),
        "gam": np.ascontiguousarray(A("L0_attn_ln_gamma")[None, :]),
        "bet": np.ascontiguousarray(A("L0_attn_ln_beta")[None, :]),
        "wgT": np.ascontiguousarray(np.asarray(inputs["L0_ffn_w_gate"], np.int32).T),
        "wuT": np.ascontiguousarray(np.asarray(inputs["L0_ffn_w_up"], np.int32).T),
        "wdT": np.ascontiguousarray(np.asarray(inputs["L0_ffn_w_down"], np.int32).T),
        "rwT": np.ascontiguousarray(A("L1_ffn_router_w").T),
        "rbias": np.ascontiguousarray(A("L1_ffn_router_bias")[None, :]),
    }
    l0_shared.update(_host_consts_l0())
    moe_shared = {
        "eup": np.ascontiguousarray(A("L1_ffn_up_proj")),
        "egt": np.ascontiguousarray(A("L1_ffn_gate_proj")),
        "edn": np.ascontiguousarray(A("L1_ffn_down_proj")),
        "shgT": np.ascontiguousarray(A("L1_ffn_sh_gate").T),
        "shuT": np.ascontiguousarray(A("L1_ffn_sh_up").T),
        "shdT": np.ascontiguousarray(A("L1_ffn_sh_down").T),
    }
    attn_shared = {
        "wqa": np.ascontiguousarray(A("L1_attn_Wq")),
        "wka": np.ascontiguousarray(A("L1_attn_Wk")),
        "wva": np.ascontiguousarray(A("L1_attn_Wv")),
        "woa": np.ascontiguousarray(A("L1_attn_Wo")),
        "bq": np.ascontiguousarray(A("L1_attn_bq")[None, :]),
        "bk": np.ascontiguousarray(A("L1_attn_bk")[None, :]),
        "bv": np.ascontiguousarray(A("L1_attn_bv")[None, :]),
        "bo": np.ascontiguousarray(A("L1_attn_bo")[None, :]),
    }
    attn_shared.update(_host_consts_attn())

    cores = list(range(NCORES))
    nc = build_fused()
    shared = {}
    shared.update(l0_shared)
    shared.update(moe_shared)
    shared.update(attn_shared)
    maps = []
    for c in cores:
        m = dict(shared)
        m["x"] = np.ascontiguousarray(x_all[c])
        maps.append(m)
    r3 = run_bass_kernel_spmd(nc, maps, core_ids=cores)
    return np.stack([np.asarray(r3.results[c]["out"], f32) for c in cores], axis=0)


# version 09
# speedup vs baseline: 1.2244x; 1.2244x over previous
import numpy as np
import ml_dtypes
import concourse.bass as bass
import concourse.mybir as mybir
from concourse.bass_utils import run_bass_kernel_spmd
from contextlib import ExitStack

F32 = mybir.dt.float32
BF16 = mybir.dt.bfloat16
I32 = mybir.dt.int32
AF = mybir.ActivationFunctionType
ALU = mybir.AluOpType
AX = mybir.AxisListType
BFNP = ml_dtypes.bfloat16

S_LEN = 4096
DM = 1024
NH = 16
HD = 64
HID0 = 3072
NEXP = 16
EDIM = 512
NCORES = 8
SCL = 1.0 / 127.0
EPS_LN = 1e-5
EPS_NORM = 1e-7

TT = 256
NJ = TT // 128
NT0 = S_LEN // TT
NHC = HID0 // 128
ATTN_LAG = 2
LDQ_U = "gpsimd"
LDQ_D = "scalar"
MOE_Q_U = "gpsimd"
MOE_Q_D = "scalar"


class Sched:

    def __init__(self):
        self.ops = []
        self.c = {}
        self.g = {}
        self.bar = {}

    def barrier(self):
        snap = tuple((k, v) for k, v in self.c.items() if v > 0)
        self.bar = {e: snap for e in ("sync", "tensor", "vector", "scalar", "gpsimd")}
        self.g = {}

    def val(self, sem):
        return self.c.get(sem, 0)

    def gate(self, *keys):
        out = []
        for k in keys:
            t = self.g.get(k)
            if t is not None:
                out.append(t)
        return out

    def setg(self, key, sem):
        self.g[key] = (sem, self.val(sem))

    def op(self, eng, emit, waits=(), inc=None):
        if eng in self.bar:
            waits = list(self.bar.pop(eng)) + list(waits)
        if inc is not None:
            self.c[inc[0]] = self.c.get(inc[0], 0) + inc[1]
        self.ops.append((eng, emit, tuple(waits), inc))

    def realize(self, block, sems):
        for name in ("sync", "tensor", "vector", "scalar", "gpsimd"):
            sub = [o for o in self.ops if o[0] == name]
            if not sub:
                continue

            def make(sub):
                def f(eng):
                    seen = {}
                    for _, emit, waits, inc in sub:
                        for sname, v in waits:
                            if v > 0 and seen.get(sname, -1) < v:
                                eng.wait_ge(sems[sname], v)
                                seen[sname] = v
                        inst = emit(eng)
                        if inc is not None:
                            inst.then_inc(sems[inc[0]], inc[1])
                return f

            getattr(block, name)(make(sub))


def _bc(apv, reps, unit_axis=1):
    return apv.rearrange("p (u n) -> p u n", u=1).broadcast_to(
        [apv.shape[0], reps, apv.shape[1]])


def _ln_stats(s, xt_view, nj, st, sqscr, in_waits):
    msum, ss, mean, var, rstd, negmr = (st["msum"], st["ss"], st["mean"],
                                        st["var"], st["rstd"], st["negmr"])
    s.op("vector", lambda v: v.reduce_sum(msum[:, 0:nj], xt_view, axis=AX.X),
         waits=in_waits, inc=("ve", 1))
    ve_msum = s.val("ve")
    for j in range(nj):
        s.op("scalar", lambda a, j=j: a.activation(
            sqscr[:, j, :], xt_view[:, j, :], AF.Square, bias=0.0, scale=1.0),
            waits=in_waits + s.gate("sqscr_free"),
            inc=("ac", 1) if j == nj - 1 else None)
    ac0 = s.val("ac")
    s.op("vector", lambda v: v.reduce_sum(ss[:, 0:nj], sqscr[:, 0:nj, :], axis=AX.X),
         waits=[("ac", ac0)], inc=("ve", 1))
    s.setg("sqscr_free", "ve")
    s.op("vector", lambda v: v.tensor_scalar(mean[:, 0:nj], msum[:, 0:nj], 1.0 / DM, None, ALU.mult),
         waits=[("ve", ve_msum)], inc=("ve", 1))
    s.op("vector", lambda v: v.tensor_tensor(negmr[:, 0:nj], mean[:, 0:nj], mean[:, 0:nj], ALU.mult),
         waits=[("ve", s.val("ve"))], inc=("ve", 1))
    s.op("vector", lambda v: v.scalar_tensor_tensor(var[:, 0:nj], ss[:, 0:nj], 1.0 / DM,
                                                    negmr[:, 0:nj], ALU.mult, ALU.subtract),
         waits=[("ve", s.val("ve"))], inc=("ve", 1))
    s.op("vector", lambda v: v.tensor_scalar(var[:, 0:nj], var[:, 0:nj], EPS_LN, None, ALU.add),
         waits=[("ve", s.val("ve"))], inc=("ve", 1))
    s.op("scalar", lambda a: a.activation(rstd[:, 0:nj], var[:, 0:nj], AF.Sqrt,
                                          bias=0.0, scale=1.0),
         waits=[("ve", s.val("ve"))], inc=("ac", 1))
    s.op("vector", lambda v: v.reciprocal(rstd[:, 0:nj], rstd[:, 0:nj]),
         waits=[("ac", s.val("ac"))], inc=("ve", 1))
    s.op("vector", lambda v: v.tensor_mul(negmr[:, 0:nj], mean[:, 0:nj], rstd[:, 0:nj]),
         waits=[("ve", s.val("ve"))], inc=("ve", 1))
    s.op("vector", lambda v: v.tensor_scalar(negmr[:, 0:nj], negmr[:, 0:nj], -1.0, None, ALU.mult),
         waits=[("ve", s.val("ve"))], inc=("ve", 1))


def _mk_ctx():
    nc = bass.Bass("TRN2", target_bir_lowering=False, debug=False)
    s = Sched()
    cache = {}

    def dram(name, shape, dtype, kind="ExternalInput"):
        if name not in cache:
            cache[name] = nc.dram_tensor(name, shape, dtype, kind=kind).ap()
        return cache[name]

    return nc, s, dram


def _realize(nc, s):
    with nc.sbuf_tensor("fin", [1, 1], F32) as fin:
        s.op("gpsimd", lambda e: e.memset(fin[:], 0.0),
             waits=[(k, v) for k, v in s.c.items() if v > 0])
        sems = {}
        with ExitStack() as es2:
            for n in sorted(s.c):
                sems[n] = es2.enter_context(nc.semaphore(n))
            with nc.Block() as block:
                s.realize(block, sems)
    return nc


def build_l0(stop_after="all", ntiles=NT0, dbg_scratch=False):
    nc, s, dram = _mk_ctx()
    emit_l0(nc, s, dram, stop_after, ntiles, dbg_scratch)
    return _realize(nc, s)


def build_moe():
    nc, s, dram = _mk_ctx()
    emit_moe(nc, s, dram)
    return _realize(nc, s)


def build_attn():
    nc, s, dram = _mk_ctx()
    emit_attn(nc, s, dram)
    return _realize(nc, s)


def build_fused():
    nc, s, dram0 = _mk_ctx()
    handoff = {"x1": None, "xn1": None, "w16": None, "moe": None}

    def dram(name, shape, dtype, kind="ExternalInput"):
        if name in handoff:
            kind = "Internal"
        return dram0(name, shape, dtype, kind)

    emit_l0(nc, s, dram, "all", NT0, False)
    s.barrier()
    emit_moe(nc, s, dram)
    s.barrier()
    emit_attn(nc, s, dram)
    return _realize(nc, s)


def emit_l0(nc, s, dram, stop_after="all", ntiles=NT0, dbg_scratch=False):
    ap = dram
    x = ap("x", [S_LEN, DM], F32, kind="ExternalInput")
    wq = ap("wq", [DM, DM], F32, kind="ExternalInput")
    wk = ap("wk", [DM, DM], F32, kind="ExternalInput")
    wkp = ap("wkp", [DM, DM], F32, kind="ExternalInput")
    wv = ap("wv", [DM, DM], F32, kind="ExternalInput")
    wo = ap("wo", [DM, DM], F32, kind="ExternalInput")
    gam = ap("gam", [1, DM], F32, kind="ExternalInput")
    bet = ap("bet", [1, DM], F32, kind="ExternalInput")
    wgT = ap("wgT", [DM, HID0], I32, kind="ExternalInput")
    wuT = ap("wuT", [DM, HID0], I32, kind="ExternalInput")
    wdT = ap("wdT", [HID0, DM], I32, kind="ExternalInput")
    rwT = ap("rwT", [DM, 16], F32, kind="ExternalInput")
    rbias = ap("rbias", [1, 16], F32, kind="ExternalInput")
    idb = ap("idb", [128, 128], BF16, kind="ExternalInput")
    wmask = ap("wmask", [32, DM], F32, kind="ExternalInput")
    idx16 = ap("idx16", [128, 16], F32, kind="ExternalInput")
    x1 = ap("x1", [S_LEN, DM], F32, kind="ExternalOutput")
    xn1 = ap("xn1", [8, 128, S_LEN], BF16, kind="ExternalOutput")
    w16 = ap("w16", [S_LEN, 16], F32, kind="ExternalOutput")
    scratch_kind = "ExternalOutput" if dbg_scratch else "Internal"
    if dbg_scratch:
        dbg_xn = ap("dbg_xn", [128, NJ, DM], F32, kind="ExternalOutput")
        dbg_fm = ap("dbg_fm", [128, 8, TT], BF16, kind="ExternalOutput")
        dbg_st = ap("dbg_st", [6, 128, NJ], F32, kind="ExternalOutput")
    wgb = ap("wgb", [NHC, 128, 1024], BF16, kind=scratch_kind)
    wub = ap("wub", [NHC, 128, 1024], BF16, kind=scratch_kind)
    wdb = ap("wdb", [NHC, 128, 1024], BF16, kind=scratch_kind)

    es = ExitStack()
    S = lambda n, *a: es.enter_context(nc.sbuf_tensor("a_" + n, *a))
    P = lambda n, *a: es.enter_context(nc.psum_tensor("a_" + n, *a))

    xbuf = [S("xbuf0", [128, NJ, DM], F32), S("xbuf1", [128, NJ, DM], F32)]
    xn = S("xn", [128, NJ, DM], F32)
    t1 = S("t1", [128, NJ, DM], F32)
    Cc = S("Cc", [128, NJ, DM], F32)
    xnb = S("xnb", [128, NJ, DM], BF16)
    xnfm = S("xnfm", [128, 8, TT], BF16)
    gslf = S("gslf", [128, TT], F32)
    hidb = [S("hidb0", [128, TT], BF16), S("hidb1", [128, TT], BF16)]
    gamb = S("gamb", [128, DM], F32)
    betb = S("betb", [128, DM], F32)
    rbb = S("rbb", [128, 16], F32)
    idxt = S("idxt", [128, 16], F32)
    idxm = S("idxm", [128, 16], F32)
    onesf = S("onesf", [1, 128], F32)
    idbt = S("idbt", [128, 128], BF16)
    sqscr = S("sqscr", [128, NJ, DM], F32)
    dbg_snap = S("dbg_snap", [128, 6, NJ], F32)
    sqmini = S("sqmini", [32, DM], F32)
    st = {k: S("st_" + k, [128, NJ], F32)
          for k in ("msum", "ss", "mean", "var", "rstd", "negmr")}
    wgc = [S("wgc%d" % i, [128, 8, 128], BF16) for i in range(4)]
    wuc = [S("wuc%d" % i, [128, 8, 128], BF16) for i in range(4)]
    wdc = [S("wdc%d" % i, [128, 1024], BF16) for i in range(4)]
    wstage = [S("wstage0", [128, 1024], I32), S("wstage1", [128, 1024], I32)]
    wsf = S("wsf", [128, 1024], F32)
    lg_all = S("lg_all", [128, 32, 16], F32)
    r_m1 = S("r_m1", [128, 32], F32)
    r_m2 = S("r_m2", [128, 32], F32)
    r_i = S("r_i", [128, 32], F32)
    r_p = S("r_p", [128, 32], F32)
    r_c = S("r_c", [128, 32, 16], F32)
    r_d = S("r_d", [128, 32, 16], F32)
    rwb = S("rwb", [128, 8, 16], BF16)
    rwf = S("rwf", [128, 8, 16], F32)
    mst = S("mst", [128, DM], BF16)
    wmt = S("wmt", [32, DM], F32)
    xn32 = S("xn32", [32, DM], F32)
    m_q = S("m_q", [32, DM], F32)
    m_k = S("m_k", [32, DM], F32)
    m_kp = S("m_kp", [32, DM], F32)
    m_v = S("m_v", [32, DM], F32)
    m_sc = S("m_sc", [32, DM], F32)
    m_st = S("m_st", [32, 16], F32)
    m_st2 = S("m_st2", [32, 16], F32)
    m_fm = S("m_fm", [128, 8, 32], BF16)
    q_fm = S("q_fm", [128, 8, 32], BF16)
    k_fm = S("k_fm", [128, 8, 32], BF16)
    kp_fm = S("kp_fm", [128, 8, 32], BF16)
    v_fm = S("v_fm", [128, 8, 32], BF16)
    wfm = S("wfm", [128, DM], BF16)
    ptsb = S("ptsb", [128, 8, 32], BF16)
    utsb = S("utsb", [128, 8, 64], BF16)
    z_tm = S("z_tm", [32, DM], BF16)
    zwo = S("zwo", [32, DM], F32)
    wbig = S("wbig", [128, 8, DM], BF16)

    p_tr_full = [P("p_tr0", [128, 1024], BF16), P("p_tr1", [128, 1024], BF16)]
    p_tr = [t[:, 0:128] for t in p_tr_full]
    p_g_full = P("p_g", [128, 512], F32)
    p_g = p_g_full[:, 0:TT]
    p_u_full = P("p_u", [128, 512], F32)
    p_u = p_u_full[:, 0:TT]
    p_d = [P("p_d0", [128, 512], F32), P("p_d1", [128, 512], F32),
           P("p_d2", [128, 512], F32), P("p_d3", [128, 512], F32)]

    with es:
        s.op("sync", lambda e: e.dma_start(out=idbt[:], in_=idb[:]), inc=("ld", 16))
        s.op("sync", lambda e: e.dma_start(out=idxt[:], in_=idx16[:]), inc=("ld", 16))
        s.op("sync", lambda e: e.dma_start(out=wmt[:], in_=wmask[:]), inc=("ld", 16))
        s.op("sync", lambda e: e.dma_start(out=rwf[:], in_=rwT.rearrange("(c p) e -> p c e", p=128)),
             inc=("ld", 16))
        s.op("sync", lambda e: e.dma_start(out=gamb[0:1, :], in_=gam[:]), inc=("ld", 16))
        s.op("sync", lambda e: e.dma_start(out=betb[0:1, :], in_=bet[:]), inc=("ld", 16))
        s.op("sync", lambda e: e.dma_start(out=rbb[0:1, :], in_=rbias[:]), inc=("ld", 16))
        ld_c = s.val("ld")
        if stop_after == "c1":
            return
        s.op("gpsimd", lambda e: e.memset(onesf[:], 1.0), inc=("gp", 1))
        s.op("gpsimd", lambda e: e.memset(Cc[:], 0.0), inc=("gp", 1))
        s.op("gpsimd", lambda e: e.memset(mst[:], 0.0), inc=("gp", 1))
        gp_ms = s.val("gp")
        s.op("vector", lambda v: v.tensor_scalar(idxm[:], idxt[:], 65536.0, None, ALU.subtract),
             waits=[("ld", ld_c)], inc=("ve", 1))
        s.op("vector", lambda v: v.tensor_copy(rwb[:], rwf[:]),
             waits=[("ld", ld_c)], inc=("ve", 1))
        if stop_after == "c2":
            return
        for (tile, psum, pname) in ((gamb, p_d[0], "pd0"), (betb, p_d[1], "pd1")):
            for half in range(2):
                s.op("tensor", lambda t_, tile=tile, psum=psum, half=half: t_.matmul(
                    psum[:], onesf[0:1, :], tile[0:1, half * 512:(half + 1) * 512],
                    start=True, stop=True),
                    waits=[("ld", ld_c), ("gp", gp_ms)] + s.gate(pname + "_free"),
                    inc=("mm", 1))
                s.op("vector", lambda v, tile=tile, psum=psum, half=half: v.tensor_copy(
                    tile[:, half * 512:(half + 1) * 512], psum[:]),
                    waits=[("mm", s.val("mm"))], inc=("ve", 1))
                s.setg(pname + "_free", "ve")
        s.op("tensor", lambda t_: t_.matmul(p_d[2][:, 0:16], onesf[0:1, :], rbb[0:1, 0:16],
                                            start=True, stop=True),
             waits=[("ld", ld_c), ("gp", gp_ms)], inc=("mm", 1))
        s.op("vector", lambda v: v.tensor_copy(rbb[:], p_d[2][:, 0:16]),
             waits=[("mm", s.val("mm"))], inc=("ve", 1))
        s.setg("gb_ready", "ve")
        s.setg("pd0_free", "ve")
        s.setg("pd1_free", "ve")
        s.setg("pd2_free", "ve")

        if stop_after == "cast":
            return
        s.op("sync", lambda e: e.dma_start(out=xn32[:], in_=x[S_LEN - 32:S_LEN, :]),
             inc=("ld", 16))
        ld32 = s.val("ld")
        s.op("vector", lambda v: v.reduce_sum(m_st[:, 0:1], xn32[:], axis=AX.X),
             waits=[("ld", ld32)], inc=("ve", 1))
        s.op("scalar", lambda a: a.activation(sqmini[:], xn32[:], AF.Square,
                                              bias=0.0, scale=1.0),
             waits=[("ld", ld32)], inc=("ac", 1))
        s.op("vector", lambda v: v.reduce_sum(m_st[:, 1:2], sqmini[:], axis=AX.X),
             waits=[("ac", s.val("ac"))], inc=("ve", 1))
        w0 = [("ve", s.val("ve")), ("ac", s.val("ac"))]
        s.op("vector", lambda v: v.tensor_scalar(m_st[:, 2:3], m_st[:, 0:1], 1.0 / DM, None, ALU.mult),
             waits=w0, inc=("ve", 1))
        s.op("vector", lambda v: v.tensor_scalar(m_st[:, 3:4], m_st[:, 1:2], 1.0 / DM, None, ALU.mult),
             waits=[("ve", s.val("ve"))], inc=("ve", 1))
        s.op("vector", lambda v: v.tensor_mul(m_st[:, 4:5], m_st[:, 2:3], m_st[:, 2:3]),
             waits=[("ve", s.val("ve"))], inc=("ve", 1))
        s.op("vector", lambda v: v.tensor_sub(m_st[:, 3:4], m_st[:, 3:4], m_st[:, 4:5]),
             waits=[("ve", s.val("ve"))], inc=("ve", 1))
        s.op("vector", lambda v: v.tensor_scalar(m_st[:, 3:4], m_st[:, 3:4], EPS_LN, None, ALU.add),
             waits=[("ve", s.val("ve"))], inc=("ve", 1))
        s.op("scalar", lambda a: a.activation(m_st[:, 5:6], m_st[:, 3:4], AF.Sqrt, bias=0.0, scale=1.0),
             waits=[("ve", s.val("ve"))], inc=("ac", 1))
        s.op("vector", lambda v: v.reciprocal(m_st[:, 5:6], m_st[:, 5:6]),
             waits=[("ac", s.val("ac"))], inc=("ve", 1))
        s.op("vector", lambda v: v.tensor_mul(m_st[:, 6:7], m_st[:, 2:3], m_st[:, 5:6]),
             waits=[("ve", s.val("ve"))], inc=("ve", 1))
        s.op("vector", lambda v: v.tensor_scalar(m_st[:, 6:7], m_st[:, 6:7], -1.0, None, ALU.mult),
             waits=[("ve", s.val("ve"))], inc=("ve", 1))
        s.op("scalar", lambda a: a.activation(xn32[:], xn32[:], AF.Identity,
                                              bias=m_st[:, 6:7], scale=m_st[:, 5:6]),
             waits=[("ve", s.val("ve"))], inc=("ac", 1))
        ac_xn32 = s.val("ac")

        def mini_tr(src_f32_or_bf16, rows, out_fm, nchunk=8, outrows=None):
            outrows = outrows or rows
            s.op("vector", lambda v: v.tensor_copy(mst[0:rows, :], src_f32_or_bf16),
                 waits=s.gate("mst_free"), inc=("ve", 1))
            ve_stg = s.val("ve")
            for dc in range(nchunk):
                pt = p_tr[dc % 2]
                s.op("tensor", lambda t_, dc=dc, pt=pt: t_.transpose(
                    pt[:], mst[:, dc * 128:(dc + 1) * 128], idbt[:]),
                    waits=[("ve", ve_stg)] + s.gate("ptr%d_free" % (dc % 2)) + [("ld", ld_c)],
                    inc=("mm", 1))
                s.op("vector", lambda v, dc=dc, pt=pt: v.tensor_copy(
                    out_fm[:, dc, 0:outrows], pt[:, 0:outrows]),
                    waits=[("mm", s.val("mm"))], inc=("ve", 1))
                s.setg("ptr%d_free" % (dc % 2), "ve")
            s.setg("mst_free", "ve")

        s.op("vector", lambda v: v.tensor_copy(mst[0:32, :], xn32[:]),
             waits=[("ac", ac_xn32), ("gp", gp_ms)], inc=("ve", 1))
        ve_stg = s.val("ve")
        for dc in range(8):
            pt = p_tr[dc % 2]
            s.op("tensor", lambda t_, dc=dc, pt=pt: t_.transpose(
                pt[:], mst[:, dc * 128:(dc + 1) * 128], idbt[:]),
                waits=[("ve", ve_stg)] + s.gate("ptr%d_free" % (dc % 2)) + [("ld", ld_c)],
                inc=("mm", 1))
            s.op("vector", lambda v, dc=dc, pt=pt: v.tensor_copy(m_fm[:, dc, :], pt[:, 0:32]),
                 waits=[("mm", s.val("mm"))], inc=("ve", 1))
            s.setg("ptr%d_free" % (dc % 2), "ve")
        s.setg("mst_free", "ve")

        for (whbm, out_tm, out_fm, do_norm) in ((wq, m_q, q_fm, True), (wk, m_k, k_fm, True),
                                                (wkp, m_kp, kp_fm, True), (wv, m_v, v_fm, False)):
            for dc in range(8):
                s.op("sync", lambda e, whbm=whbm, dc=dc: e.dma_start(
                    out=wsf[:], in_=whbm.rearrange("(c p) f -> p c f", p=128)[:, dc, :]),
                    waits=s.gate("wsf_free"), inc=("ld", 16))
                s.op("vector", lambda v, dc=dc: v.tensor_copy(wbig[:, dc, :], wsf[:]),
                     waits=[("ld", s.val("ld"))] + s.gate("wbig_free"), inc=("ve", 1))
                s.setg("wsf_free", "ve")
            ve_w = s.val("ve")
            for half in range(2):
                pp = p_d[2 + half]
                for dc in range(8):
                    s.op("tensor", lambda t_, dc=dc, half=half, pp=pp: t_.matmul(
                        pp[0:32, :], m_fm[:, dc, :], wbig[:, dc, half * 512:(half + 1) * 512],
                        start=(dc == 0), stop=(dc == 7)),
                        waits=[("ve", ve_w)] + s.gate("pd%d_free" % (2 + half)),
                        inc=("mm", 1) if dc == 7 else None)
                s.op("vector", lambda v, half=half, pp=pp, out_tm=out_tm: v.tensor_copy(
                    out_tm[:, half * 512:(half + 1) * 512], pp[0:32, :]),
                    waits=[("mm", s.val("mm"))], inc=("ve", 1))
                s.setg("pd%d_free" % (2 + half), "ve")
            s.setg("wbig_free", "ve")
            ve_proj = s.val("ve")
            if do_norm:
                s.op("scalar", lambda a, out_tm=out_tm: a.activation(
                    sqmini[:], out_tm[:], AF.Square, bias=0.0, scale=1.0),
                    waits=[("ve", ve_proj)], inc=("ac", 1))
                s.op("vector", lambda v: v.reduce_sum(
                    m_st[:, 0:16], sqmini[:].rearrange("q (h d) -> q h d", h=NH), axis=AX.X),
                    waits=[("ac", s.val("ac"))], inc=("ve", 1))
                s.op("scalar", lambda a: a.activation(m_st2[:], m_st[:], AF.Sqrt, bias=0.0, scale=1.0),
                     waits=[("ve", s.val("ve"))], inc=("ac", 1))
                s.op("vector", lambda v: v.tensor_scalar(m_st2[:], m_st2[:], EPS_NORM, None, ALU.add),
                     waits=[("ac", s.val("ac"))], inc=("ve", 1))
                s.op("vector", lambda v: v.reciprocal(m_st2[:], m_st2[:]),
                     waits=[("ve", s.val("ve"))], inc=("ve", 1))
                ve_rn = s.val("ve")
                for h in range(NH):
                    s.op("vector", lambda v, h=h, out_tm=out_tm: v.tensor_scalar(
                        out_tm[:, h * 64:(h + 1) * 64], out_tm[:, h * 64:(h + 1) * 64],
                        m_st2[:, h:h + 1], None, ALU.mult),
                        waits=[("ve", ve_rn)],
                        inc=("ve", 1) if h == NH - 1 else None)
            mini_tr(out_tm[:], 32, out_fm)

        for h in range(NH):
            cc, ro = h // 2, 64 * (h % 2)
            qf = q_fm[:, cc, :][ro:ro + 64, :]
            kf = k_fm[:, cc, :][ro:ro + 64, :]
            kpf = kp_fm[:, cc, :][ro:ro + 64, :]
            vf = v_fm[:, cc, :][ro:ro + 64, :]
            kv = kf.rearrange("d (j u) -> d j u", u=1).broadcast_to([64, 32, 32])
            kpv = kpf.rearrange("d (u k) -> d u k", u=1).broadcast_to([64, 32, 32])
            wfm_h = wfm[ro:ro + 64, :]
            s.op("vector", lambda v, kv=kv, kpv=kpv, wfm_h=wfm_h: v.tensor_tensor(
                wfm_h.rearrange("d (j k) -> d j k", k=32), kv, kpv, ALU.mult),
                waits=s.gate("wfm_free"), inc=("ve", 1))
            ve_wfm = s.val("ve")
            for half in range(2):
                pp = p_d[half]
                s.op("tensor", lambda t_, half=half, pp=pp, qf=qf, wfm_h=wfm_h: t_.matmul(
                    pp[0:32, :], qf, wfm_h[:, half * 512:(half + 1) * 512],
                    start=True, stop=True),
                    waits=[("ve", ve_wfm)] + s.gate("pd%d_free" % half),
                    inc=("mm", 1))
                s.op("vector", lambda v, half=half, pp=pp: v.scalar_tensor_tensor(
                    m_sc[:, half * 512:(half + 1) * 512], pp[0:32, :], 0.125,
                    wmt[:, half * 512:(half + 1) * 512], ALU.mult, ALU.add),
                    waits=[("mm", s.val("mm")), ("ld", ld_c)], inc=("ve", 1))
                s.setg("pd%d_free" % half, "ve")
            s.op("vector", lambda v: v.reduce_max(m_st[:, 0:1], m_sc[:], axis=AX.X),
                 inc=("ve", 1))
            s.op("vector", lambda v: v.tensor_scalar(m_st[:, 1:2], m_st[:, 0:1], -1.0, None, ALU.mult),
                 waits=[("ve", s.val("ve"))], inc=("ve", 1))
            s.op("scalar", lambda a: a.activation(m_sc[:], m_sc[:], AF.Exp,
                                                  bias=m_st[:, 1:2], scale=1.0),
                 waits=[("ve", s.val("ve"))], inc=("ac", 1))
            s.op("vector", lambda v: v.reduce_sum(m_st[:, 2:3], m_sc[:], axis=AX.X),
                 waits=[("ac", s.val("ac"))], inc=("ve", 1))
            s.op("vector", lambda v: v.reciprocal(m_st[:, 3:4], m_st[:, 2:3]),
                 waits=[("ve", s.val("ve"))], inc=("ve", 1))
            ve_recip = s.val("ve")
            mini_tr(m_sc[:], 32, ptsb)
            vv1 = vf.rearrange("d (j u) -> d j u", u=1).broadcast_to([64, 32, 32])
            vv2 = vf.rearrange("d (u k) -> d u k", u=1).broadcast_to([64, 32, 32])
            s.op("vector", lambda v, vv1=vv1, vv2=vv2, wfm_h=wfm_h: v.tensor_tensor(
                wfm_h.rearrange("d (j k) -> d j k", k=32), vv1, vv2, ALU.mult),
                inc=("ve", 1))
            mini_tr(wfm_h, 64, utsb)
            s.setg("wfm_free", "ve")
            pz = p_d[2]
            for dc in range(8):
                s.op("tensor", lambda t_, dc=dc, pz=pz: t_.matmul(
                    pz[0:32, 0:64], ptsb[:, dc, :], utsb[:, dc, :],
                    start=(dc == 0), stop=(dc == 7)),
                    waits=[("ve", s.val("ve"))] + s.gate("pd2_free"),
                    inc=("mm", 1) if dc == 7 else None)
            s.op("vector", lambda v, h=h, pz=pz: v.tensor_scalar(
                z_tm[:, h * 64:(h + 1) * 64], pz[0:32, 0:64], m_st[:, 3:4], None, ALU.mult),
                waits=[("mm", s.val("mm")), ("ve", ve_recip)], inc=("ve", 1))
            s.setg("pd2_free", "ve")
        for dc in range(8):
            s.op("sync", lambda e, dc=dc: e.dma_start(
                out=wsf[:], in_=wo.rearrange("(c p) f -> p c f", p=128)[:, dc, :]),
                waits=s.gate("wsf_free"), inc=("ld", 16))
            s.op("vector", lambda v, dc=dc: v.tensor_copy(wbig[:, dc, :], wsf[:]),
                 waits=[("ld", s.val("ld"))] + s.gate("wbig_free"), inc=("ve", 1))
            s.setg("wsf_free", "ve")
        ve_wo = s.val("ve")
        mini_tr(z_tm[:], 32, m_fm)
        ve_zfm = s.val("ve")
        for half in range(2):
            pp = p_d[half]
            for dc in range(8):
                s.op("tensor", lambda t_, dc=dc, half=half, pp=pp: t_.matmul(
                    pp[0:32, :], m_fm[:, dc, :], wbig[:, dc, half * 512:(half + 1) * 512],
                    start=(dc == 0), stop=(dc == 7)),
                    waits=[("ve", ve_wo), ("ve", ve_zfm)] + s.gate("pd%d_free" % half),
                    inc=("mm", 1) if dc == 7 else None)
            s.op("vector", lambda v, half=half, pp=pp: v.tensor_copy(
                zwo[:, half * 512:(half + 1) * 512], pp[0:32, :]),
                waits=[("mm", s.val("mm"))], inc=("ve", 1))
            s.setg("pd%d_free" % half, "ve")
        s.op("gpsimd", lambda e: e.dma_start(out=Cc[96:128, NJ - 1, :], in_=zwo[:]),
             waits=[("ve", s.val("ve"))], inc=("cst", 16))
        cc_done = s.val("cst")

        if stop_after == "mini":
            return
        for t in range(ntiles):
            xb = xbuf[t % 2]
            s.op("sync", lambda e, t=t, xb=xb: e.dma_start(
                out=xb[:], in_=x.rearrange("(t j p) d -> t p j d", j=NJ, p=128)[t]),
                waits=s.gate("xb%d_free" % (t % 2)), inc=("ld", 16))
            ld_x = s.val("ld")
            inw = [("ld", ld_x)]
            _ln_stats(s, xb[:], NJ, st, sqscr, inw)
            ve_rs = s.val("ve")
            if dbg_scratch and t == 0:
                snap = locals().get("dbg_snap")
                for i, k in enumerate(("msum", "ss", "mean", "var", "rstd", "negmr")):
                    s.op("vector", lambda v, i=i, k=k: v.tensor_copy(
                        dbg_snap[:, i, :], st[k][:]),
                        waits=[("ve", ve_rs), ("ac", s.val("ac"))],
                        inc=("ve", 1) if i == 5 else None)
                s.op("gpsimd", lambda e: e.dma_start(
                    out=dbg_st.rearrange("i p j -> p i j"), in_=dbg_snap[:]),
                    waits=[("ve", s.val("ve"))], inc=("so", 16))
            for j in range(NJ):
                s.op("scalar", lambda a, j=j, xb=xb: a.activation(
                    xn[:, j, :], xb[:, j, :], AF.Identity,
                    bias=st["negmr"][:, j:j + 1], scale=st["rstd"][:, j:j + 1]),
                    waits=[("ve", ve_rs)] + s.gate("xn_free"), inc=("ac", 1))
            ac_xn = s.val("ac")
            s.op("vector", lambda v: v.tensor_copy(xnb[:], xn[:]),
                 waits=[("ac", ac_xn)], inc=("ve", 1))
            ve_xnb = s.val("ve")
            for jj in range(NJ):
                for dc in range(8):
                    pt = p_tr[(jj * 8 + dc) % 2]
                    s.op("tensor", lambda t_, jj=jj, dc=dc, pt=pt: t_.transpose(
                        pt[:], xnb[:, jj, dc * 128:(dc + 1) * 128], idbt[:]),
                        waits=[("ve", ve_xnb)] + s.gate("ptr%d_free" % ((jj * 8 + dc) % 2)),
                        inc=("mm", 1))
                    s.op("vector", lambda v, jj=jj, dc=dc, pt=pt: v.tensor_copy(
                        xnfm[:, dc, jj * 128:(jj + 1) * 128], pt[:]),
                        waits=[("mm", s.val("mm"))], inc=("ve", 1))
                    s.setg("ptr%d_free" % ((jj * 8 + dc) % 2), "ve")
            ve_fm = s.val("ve")
            if dbg_scratch and t == 0:
                s.op("gpsimd", lambda e: e.dma_start(out=dbg_xn[:], in_=xn[:]),
                     waits=[("ve", ve_fm)], inc=("so", 16))
                s.op("gpsimd", lambda e: e.dma_start(out=dbg_fm[:], in_=xnfm[:]),
                     waits=[("ve", ve_fm)], inc=("so", 16))
            def emit_down_l0(pend):
                hcp, ve_hidp = pend
                wbp = hcp % 4
                hbp = hcp % 2
                for mt in range(NJ):
                    for half in range(2):
                        pi = mt * 2 + half
                        last = (mt == NJ - 1 and half == 1)
                        s.op("tensor", lambda t_, mt=mt, half=half, pi=pi, hcp=hcp, wbp=wbp, hbp=hbp: t_.matmul(
                            p_d[pi][:], hidb[hbp][:, mt * 128:(mt + 1) * 128],
                            wdc[wbp][:, half * 512:(half + 1) * 512],
                            start=(hcp == 0), stop=(hcp == NHC - 1)),
                            waits=([("ve", ve_hidp)] +
                                   (s.gate("pd0_free", "pd1_free", "pd2_free", "pd3_free")
                                    if hcp == 0 else [])),
                            inc=("mm", 1) if last else None)
                s.setg("hid%d_free" % hbp, "mm")
                s.setg("wbuf%d_free" % wbp, "mm")
                s.setg("wbufu%d_free" % wbp, "mm")
                s.setg("wbufd%d_free" % wbp, "mm")

            pend_down = None
            for hc in range(NHC):
                wb = hc % 4
                if t == 0:
                    for wi, (wsrcT, wdst, ldq, lsem) in enumerate(
                            ((wgT, wgb, "sync", "mld"), (wuT, wub, LDQ_U, "mldu"), (wdT, wdb, LDQ_D, "mldd"))):
                        sj = (hc * 3 + wi) % 2
                        if wi < 2:
                            src = wsrcT.rearrange("(c p) h -> p c h", p=128)[:, :, hc * 128:(hc + 1) * 128]
                            dst = wstage[sj][:].rearrange("p (c h) -> p c h", c=8)
                        else:
                            src = wsrcT[hc * 128:(hc + 1) * 128, :]
                            dst = wstage[sj][:]
                        s.op(ldq, lambda e, src=src, dst=dst: e.dma_start(out=dst, in_=src),
                             waits=s.gate("wstage%d_free" % sj), inc=(lsem, 16))
                        cdst = (wgc[wb][:].rearrange("p a b -> p (a b)") if wi == 0 else
                                (wuc[wb][:].rearrange("p a b -> p (a b)") if wi == 1 else wdc[wb][:]))
                        s.op("vector", lambda v, cdst=cdst, sj=sj: v.tensor_copy(cdst, wstage[sj][:]),
                             waits=[(lsem, s.val(lsem)), ("wst", s.val("wst"))] +
                                   (s.gate("wbuf%d_free" % wb) if wi == 0 else []),
                             inc=("ve", 1))
                        s.setg("wstage%d_free" % sj, "ve")
                        s.op("gpsimd", lambda e, cdst=cdst, wdst=wdst, hc=hc: e.dma_start(
                            out=wdst[hc], in_=cdst),
                            waits=[("ve", s.val("ve"))], inc=("wst", 16))
                    ve_wcast = s.val("ve")
                    mm_waits = [("ve", ve_wcast), ("ve", ve_fm)]
                else:
                    wait_w = [("wst", s.val("wst"))] if t == 1 and hc < 2 else []
                    s.op("sync", lambda e, hc=hc, wb=wb: e.dma_start(
                        out=wgc[wb][:].rearrange("p a b -> p (a b)"), in_=wgb[hc]),
                        waits=wait_w + s.gate("wbuf%d_free" % wb), inc=("mld", 16))
                    s.op(LDQ_U, lambda e, hc=hc, wb=wb: e.dma_start(
                        out=wuc[wb][:].rearrange("p a b -> p (a b)"), in_=wub[hc]),
                        waits=([("wst", s.val("wst"))] if t == 1 and hc < 2 else []) +
                              s.gate("wbufu%d_free" % wb), inc=("mldu", 16))
                    s.op(LDQ_D, lambda e, hc=hc, wb=wb: e.dma_start(
                        out=wdc[wb][:], in_=wdb[hc]),
                        waits=([("wst", s.val("wst"))] if t == 1 and hc < 2 else []) +
                              s.gate("wbufd%d_free" % wb), inc=("mldd", 16))
                    mm_waits = [("mld", s.val("mld")), ("mldu", s.val("mldu")),
                                ("mldd", s.val("mldd")), ("ve", ve_fm)]
                for dc in range(8):
                    s.op("tensor", lambda t_, dc=dc, wb=wb: t_.matmul(
                        p_g[:], wgc[wb][:, dc, :], xnfm[:, dc, :],
                        start=(dc == 0), stop=(dc == 7)),
                        waits=mm_waits + s.gate("pg_free"),
                        inc=("mm", 1) if dc == 7 else None)
                mm_g = s.val("mm")
                for dc in range(8):
                    s.op("tensor", lambda t_, dc=dc, wb=wb: t_.matmul(
                        p_u[:], wuc[wb][:, dc, :], xnfm[:, dc, :],
                        start=(dc == 0), stop=(dc == 7)),
                        waits=s.gate("pu_free"),
                        inc=("mm", 1) if dc == 7 else None)
                mm_u = s.val("mm")
                s.op("scalar", lambda a: a.activation(gslf[:], p_g[:], AF.Silu,
                                                      bias=0.0, scale=SCL),
                     waits=[("mm", mm_g)] + s.gate("gslf_free"), inc=("ac", 1))
                s.setg("pg_free", "ac")
                s.op("vector", lambda v, hb=hc % 2: v.tensor_tensor(hidb[hb][:], gslf[:], p_u[:], ALU.mult),
                     waits=[("ac", s.val("ac")), ("mm", mm_u)] + s.gate("hid%d_free" % (hc % 2)),
                     inc=("ve", 1))
                s.setg("pu_free", "ve")
                s.setg("gslf_free", "ve")
                ve_hid = s.val("ve")
                if pend_down is not None:
                    emit_down_l0(pend_down)
                pend_down = (hc, ve_hid)
            emit_down_l0(pend_down)
            mm_down = s.val("mm")
            if t == NT0 - 1:
                s.op("vector", lambda v: v.tensor_add(xn[:], xn[:], Cc[:]),
                     waits=[("cst", cc_done), ("ve", ve_fm)], inc=("ve", 1))
            _ln_stats(s, xn[:], NJ, st, sqscr,
                      [("ve", s.val("ve")), ("ac", s.val("ac")), ("mm", mm_g)])
            ve_rs2 = s.val("ve")
            for j in range(NJ):
                s.op("scalar", lambda a, j=j: a.activation(
                    t1[:, j, :], xn[:, j, :], AF.Identity,
                    bias=st["negmr"][:, j:j + 1], scale=st["rstd"][:, j:j + 1]),
                    waits=[("ve", ve_rs2)], inc=("ac", 1))
            ac_t1 = s.val("ac")
            s.setg("xn_free", "ac")
            gv = _bc(gamb[:], NJ)
            bv = _bc(betb[:], NJ)
            s.op("vector", lambda v, gv=gv: v.tensor_tensor(t1[:], t1[:], gv, ALU.mult),
                 waits=[("ac", ac_t1)] + s.gate("gb_ready"))
            s.op("vector", lambda v, bv=bv: v.tensor_tensor(t1[:], t1[:], bv, ALU.add))
            s.op("vector", lambda v, xb=xb: v.tensor_add(xb[:], xb[:], t1[:]), inc=("ve", 1))
            ve_s = s.val("ve")
            for mt in range(NJ):
                for half in range(2):
                    pi = mt * 2 + half
                    s.op("vector", lambda v, mt=mt, half=half, pi=pi, xb=xb:
                         v.scalar_tensor_tensor(
                             xb[:, mt, half * 512:(half + 1) * 512], p_d[pi][:],
                             SCL * SCL, xb[:, mt, half * 512:(half + 1) * 512],
                             ALU.mult, ALU.add),
                         waits=[("ve", ve_s), ("mm", mm_down)],
                         inc=("ve", 1) if pi == 3 else None)
            for pi in range(4):
                s.setg("pd%d_free" % pi, "ve")
            ve_x1 = s.val("ve")
            s.op("gpsimd", lambda e, t=t, xb=xb: e.dma_start(
                out=x1.rearrange("(t j p) d -> t p j d", j=NJ, p=128)[t], in_=xb[:]),
                waits=[("ve", ve_x1)], inc=("so", 16))
            _ln_stats(s, xb[:], NJ, st, sqscr, [("ve", ve_x1), ("ac", s.val("ac"))])
            ve_rs3 = s.val("ve")
            for j in range(NJ):
                s.op("scalar", lambda a, j=j, xb=xb: a.activation(
                    xnb[:, j, :], xb[:, j, :], AF.Identity,
                    bias=st["negmr"][:, j:j + 1], scale=st["rstd"][:, j:j + 1]),
                    waits=[("ve", ve_rs3)], inc=("ac", 1))
            ac_xn1 = s.val("ac")
            s.setg("xb%d_free" % (t % 2), "ac")
            for jj in range(NJ):
                for dc in range(8):
                    pt = p_tr[(jj * 8 + dc) % 2]
                    s.op("tensor", lambda t_, jj=jj, dc=dc, pt=pt: t_.transpose(
                        pt[:], xnb[:, jj, dc * 128:(dc + 1) * 128], idbt[:]),
                        waits=[("ac", ac_xn1)] + s.gate("ptr%d_free" % ((jj * 8 + dc) % 2)),
                        inc=("mm", 1))
                    s.op("vector", lambda v, jj=jj, dc=dc, pt=pt: v.tensor_copy(
                        xnfm[:, dc, jj * 128:(jj + 1) * 128], pt[:]),
                        waits=[("mm", s.val("mm"))], inc=("ve", 1))
                    s.setg("ptr%d_free" % ((jj * 8 + dc) % 2), "ve")
            ve_fm1 = s.val("ve")
            s.op("gpsimd", lambda e, t=t: e.dma_start(
                out=xn1[:, :, t * TT:(t + 1) * TT].rearrange("c p u -> p c u"), in_=xnfm[:]),
                waits=[("ve", ve_fm1)], inc=("so", 16))
            for mt in range(NJ):
                mtg = t * NJ + mt
                for dc in range(8):
                    s.op("tensor", lambda t_, mt=mt, dc=dc: t_.matmul(
                        p_g[:, 0:16], xnfm[:, dc, mt * 128:(mt + 1) * 128], rwb[:, dc, :],
                        start=(dc == 0), stop=(dc == 7)),
                        waits=[("ve", ve_fm1)] + s.gate("pg_free"),
                        inc=("mm", 1) if dc == 7 else None)
                s.op("vector", lambda v, mtg=mtg: v.tensor_tensor(
                    lg_all[:, mtg, :], p_g[:, 0:16], rbb[:], ALU.add),
                    waits=[("mm", s.val("mm"))], inc=("ve", 1))
                s.setg("pg_free", "ve")
            s.g["xb%d_free" % (t % 2)] = ("so", s.val("so"))
            s.g["xb%d_acfree" % (t % 2)] = ("ac", ac_xn1)


        if stop_after == "tiles":
            return
        ve_lg = s.val("ve")
        iv = _bc(idxt[:], 32)
        imv = _bc(idxm[:], 32)
        bc32 = lambda m: m[:].rearrange("p (m u) -> p m u", u=1).broadcast_to([128, 32, 16])
        def vg(emit, extra=()):
            s.op("vector", emit, waits=[("ve", s.val("ve"))] + list(extra), inc=("ve", 1))
        vg(lambda v: v.reduce_max(r_m1[:], lg_all[:], axis=AX.X), [("ve", ve_lg)])
        vg(lambda v: v.tensor_tensor(r_c[:], lg_all[:], bc32(r_m1), ALU.is_equal))
        vg(lambda v: v.tensor_tensor(r_c[:], r_c[:], imv, ALU.mult))
        vg(lambda v: v.tensor_scalar(r_c[:], r_c[:], 65536.0, None, ALU.add))
        vg(lambda v: v.tensor_reduce(r_i[:], r_c[:], axis=AX.X, op=ALU.min))
        vg(lambda v: v.tensor_tensor(r_c[:], iv, bc32(r_i), ALU.is_equal))
        vg(lambda v: v.scalar_tensor_tensor(r_d[:], r_c[:], -1e30, lg_all[:],
                                            ALU.mult, ALU.add))
        vg(lambda v: v.reduce_max(r_m2[:], r_d[:], axis=AX.X))
        vg(lambda v: v.tensor_tensor(r_d[:], r_d[:], bc32(r_m2), ALU.is_equal))
        vg(lambda v: v.tensor_tensor(r_d[:], r_d[:], imv, ALU.mult))
        vg(lambda v: v.tensor_scalar(r_d[:], r_d[:], 65536.0, None, ALU.add))
        vg(lambda v: v.tensor_reduce(r_i[:], r_d[:], axis=AX.X, op=ALU.min))
        vg(lambda v: v.tensor_tensor(r_d[:], iv, bc32(r_i), ALU.is_equal))
        vg(lambda v: v.tensor_sub(r_p[:], r_m1[:], r_m2[:]))
        s.op("scalar", lambda a: a.activation(r_p[:], r_p[:], AF.Sigmoid, bias=0.0, scale=1.0),
             waits=[("ve", s.val("ve"))], inc=("ac", 1))
        vg(lambda v: v.tensor_tensor(r_c[:], r_c[:], bc32(r_p), ALU.mult),
           [("ac", s.val("ac"))])
        vg(lambda v: v.tensor_scalar(r_p[:], r_p[:], -1.0, 1.0, ALU.mult, ALU.add))
        vg(lambda v: v.tensor_tensor(r_d[:], r_d[:], bc32(r_p), ALU.mult))
        vg(lambda v: v.tensor_add(r_c[:], r_c[:], r_d[:]))
        s.op("gpsimd", lambda e: e.dma_start(
            out=w16.rearrange("(m p) e -> p m e", p=128), in_=r_c[:]),
            waits=[("ve", s.val("ve"))], inc=("so", 16))

        return




def emit_moe(nc, s, dram):
    ap = dram
    xn1 = ap("xn1", [8, 128, S_LEN], BF16, kind="ExternalInput")
    w16 = ap("w16", [S_LEN, 16], F32, kind="ExternalInput")
    eup = ap("eup", [NEXP, DM, EDIM], F32, kind="ExternalInput")
    egt = ap("egt", [NEXP, DM, EDIM], F32, kind="ExternalInput")
    edn = ap("edn", [NEXP, EDIM, DM], F32, kind="ExternalInput")
    shgT = ap("shgT", [DM, DM], F32, kind="ExternalInput")
    shuT = ap("shuT", [DM, DM], F32, kind="ExternalInput")
    shdT = ap("shdT", [DM, DM], F32, kind="ExternalInput")
    moe = ap("moe", [S_LEN, DM], F32, kind="ExternalOutput")
    shb = [ap("shb%d" % i, [8, 128, 1024], BF16, kind="Internal") for i in range(3)]

    es = ExitStack()
    S = lambda n, *a: es.enter_context(nc.sbuf_tensor("b_" + n, *a))
    P = lambda n, *a: es.enter_context(nc.psum_tensor("b_" + n, *a))

    HTOK = S_LEN // 2
    NJM = HTOK // 128
    xh = S("xh", [128, 8, HTOK], BF16)
    acc = S("acc", [128, NJM, DM], F32)
    w16h = S("w16h", [128, NJM, 16], F32)
    gateb = [S("gateb%d" % i, [128, 8, EDIM], BF16) for i in range(2)]
    upb = [S("upb%d" % i, [128, 8, EDIM], BF16) for i in range(2)]
    dnb = [S("dnb%d" % i, [128, 4, DM], BF16) for i in range(2)]
    wstg = [S("wstg%d" % i, [128, 512], F32) for i in range(2)]
    wstgu = [S("wstgu%d" % i, [128, 512], F32) for i in range(2)]
    wstgd = [S("wstgd%d" % i, [128, 512], F32) for i in range(2)]
    hid = [S("hid0", [128, 4, 512], BF16), S("hid1", [128, 4, 512], BF16)]
    scr = S("scr", [128, 512], F32)
    shw = [S("shw%d_%d" % (i, b), [128, 8, 128], BF16) for i in range(2) for b in range(2)]
    shd = [S("shd%d" % b, [128, 1024], BF16) for b in range(2)]
    shstage = S("shstage", [128, 1024], F32)
    shstg_b = S("shstg_b", [128, 1024], BF16)
    hsh = [S("hsha", [128, 256], BF16), S("hshb", [128, 256], BF16)]
    scrsh = S("scrsh", [128, 256], F32)
    p_g = P("p_g", [128, 512], F32)
    p_u = P("p_u", [128, 512], F32)
    p_d = [P("p_d0m", [128, 512], F32), P("p_d1m", [128, 512], F32),
           P("p_d2m", [128, 512], F32), P("p_d3m", [128, 512], F32)]

    with es:
        for wi, wsrc in enumerate((shgT, shuT, shdT)):
            for hc in range(8):
                if wi < 2:
                    src_ap = wsrc.rearrange("(c p) h -> p c h", p=128)[:, :, hc * 128:(hc + 1) * 128]
                    dst_ap = shstage[:].rearrange("p (c h) -> p c h", c=8)
                else:
                    src_ap = wsrc[hc * 128:(hc + 1) * 128, :]
                    dst_ap = shstage[:]
                s.op("sync", lambda e, src_ap=src_ap, dst_ap=dst_ap: e.dma_start(
                    out=dst_ap, in_=src_ap),
                    waits=s.gate("shstage_free"), inc=("ld", 16))
                s.op("vector", lambda v: v.tensor_copy(shstg_b[:], shstage[:]),
                     waits=[("ld", s.val("ld"))] + s.gate("shstgb_free"), inc=("ve", 1))
                s.setg("shstage_free", "ve")
                s.op("gpsimd", lambda e, wi=wi, hc=hc: e.dma_start(
                    out=shb[wi][hc], in_=shstg_b[:]),
                    waits=[("ve", s.val("ve"))], inc=("wst", 16))
                s.setg("shstgb_free", "wst")
        wst_cast = s.val("wst")

        pend_moe = [None]

        def emit_down_moe(pend):
            kp, ttp, ve_hidp, wbp, ld_xp, gp_accp = pend
            hb = hid[ttp % 2]
            for mt in range(4):
                jm = ttp * 4 + mt
                for dh in range(2):
                    pb = p_d[(mt * 2 + dh) % 2]
                    for ec in range(4):
                        s.op("tensor", lambda t_, ec=ec, mt=mt, dh=dh, pb=pb, wbp=wbp, hb=hb: t_.matmul(
                            pb[:], hb[:, ec, mt * 128:(mt + 1) * 128],
                            dnb[wbp][:, ec, dh * 512:(dh + 1) * 512],
                            start=(ec == 0), stop=(ec == 3)),
                            waits=[("ve", ve_hidp)] + s.gate("pd%d_free" % ((mt * 2 + dh) % 2)),
                            inc=("mm", 1) if ec == 3 else None)
                    s.op("vector", lambda v, jm=jm, dh=dh, pb=pb, kp=kp: v.scalar_tensor_tensor(
                        acc[:, jm, dh * 512:(dh + 1) * 512], pb[:],
                        w16h[:, jm, kp:kp + 1],
                        acc[:, jm, dh * 512:(dh + 1) * 512], ALU.mult, ALU.add),
                        waits=[("mm", s.val("mm")), ("ld", ld_xp), ("gp", gp_accp)],
                        inc=("ve", 1))
                    s.setg("pd%d_free" % ((mt * 2 + dh) % 2), "ve")
            s.setg("mhid%d_free" % (ttp % 2), "mm")
            if ttp == 3:
                s.setg("dnb%d_free" % wbp, "mm")

        wld_map = {}

        def emit_wload_moe(kk, part):
            wbk = kk % 2
            chunks = ([("g", dc) for dc in range(8)] + [("u", dc) for dc in range(8)] +
                      [("d", ec, dh) for ec in range(4) for dh in range(2)])
            sel = chunks[:12] if part == 0 else chunks[12:]
            for j, ch in enumerate(sel):
                if ch[0] == "g":
                    dc = ch[1]
                    sb = dc % 2
                    s.op("sync", lambda e, kk=kk, dc=dc, sb=sb: e.dma_start(
                        out=wstg[sb][:], in_=egt[kk, dc * 128:(dc + 1) * 128, :]),
                        waits=s.gate("wstg%d_free" % sb), inc=("ld", 16))
                    s.op("vector", lambda v, dc=dc, sb=sb, wbk=wbk: v.tensor_copy(
                        gateb[wbk][:, dc, :], wstg[sb][:]),
                        waits=[("ld", s.val("ld"))] + (s.gate("gateb%d_free" % wbk) if dc == 0 else []),
                        inc=("ve", 1))
                    s.setg("wstg%d_free" % sb, "ve")
                elif ch[0] == "u":
                    dc = ch[1]
                    sb = dc % 2
                    s.op(MOE_Q_U, lambda e, kk=kk, dc=dc, sb=sb: e.dma_start(
                        out=wstgu[sb][:], in_=eup[kk, dc * 128:(dc + 1) * 128, :]),
                        waits=s.gate("wstgu%d_free" % sb), inc=("ldu", 16))
                    s.op("vector", lambda v, dc=dc, sb=sb, wbk=wbk: v.tensor_copy(
                        upb[wbk][:, dc, :], wstgu[sb][:]),
                        waits=[("ldu", s.val("ldu"))] + (s.gate("upb%d_free" % wbk) if dc == 0 else []),
                        inc=("ve", 1))
                    s.setg("wstgu%d_free" % sb, "ve")
                else:
                    ec, dh = ch[1], ch[2]
                    sb = (ec * 2 + dh) % 2
                    s.op(MOE_Q_D, lambda e, kk=kk, ec=ec, dh=dh, sb=sb: e.dma_start(
                        out=wstgd[sb][:],
                        in_=edn[kk, ec * 128:(ec + 1) * 128, dh * 512:(dh + 1) * 512]),
                        waits=s.gate("wstgd%d_free" % sb), inc=("ldd", 16))
                    s.op("vector", lambda v, ec=ec, dh=dh, sb=sb, wbk=wbk: v.tensor_copy(
                        dnb[wbk][:, ec, dh * 512:(dh + 1) * 512], wstgd[sb][:]),
                        waits=[("ldd", s.val("ldd"))] +
                              (s.gate("dnb%d_free" % wbk) if (ec == 0 and dh == 0) else []),
                        inc=("ve", 1))
                    s.setg("wstgd%d_free" % sb, "ve")
            if part == 1:
                wld_map[kk] = [("ld", s.val("ld")), ("ldu", s.val("ldu")), ("ldd", s.val("ldd"))]

        for h in range(2):
            s.op("sync", lambda e, h=h: e.dma_start(
                out=xh[:], in_=xn1[:, :, h * HTOK:(h + 1) * HTOK].rearrange("c p t -> p c t")),
                waits=s.gate("xh_free"), inc=("ld", 16))
            s.op("sync", lambda e, h=h: e.dma_start(
                out=w16h[:], in_=w16.rearrange("(m p) e -> p m e", p=128)[:, h * NJM:(h + 1) * NJM, :]),
                inc=("ld", 16))
            ld_x = s.val("ld")
            s.op("gpsimd", lambda e: e.memset(acc[:], 0.0),
                 waits=s.gate("acc_free"), inc=("gp", 1))
            gp_acc = s.val("gp")
            emit_wload_moe(0, 0)
            emit_wload_moe(0, 1)
            for k in range(NEXP):
                wb = k % 2
                wld = wld_map[k]
                for tt in range(4):
                    t0 = tt * 512
                    hb = hid[tt % 2]
                    for ec in range(4):
                        for dc in range(8):
                            s.op("tensor", lambda t_, ec=ec, dc=dc, wb=wb, t0=t0: t_.matmul(
                                p_g[:], gateb[wb][:, dc, ec * 128:(ec + 1) * 128],
                                xh[:, dc, t0:t0 + 512],
                                start=(dc == 0), stop=(dc == 7)),
                                waits=wld + [("ld", ld_x)] + s.gate("pg_free"),
                                inc=("mm", 1) if dc == 7 else None)
                        mm_g = s.val("mm")
                        for dc in range(8):
                            s.op("tensor", lambda t_, ec=ec, dc=dc, wb=wb, t0=t0: t_.matmul(
                                p_u[:], upb[wb][:, dc, ec * 128:(ec + 1) * 128],
                                xh[:, dc, t0:t0 + 512],
                                start=(dc == 0), stop=(dc == 7)),
                                waits=s.gate("pu_free"),
                                inc=("mm", 1) if dc == 7 else None)
                        mm_u = s.val("mm")
                        s.op("scalar", lambda a: a.activation(scr[:], p_g[:], AF.Silu,
                                                              bias=0.0, scale=1.0),
                             waits=[("mm", mm_g)] + s.gate("scr_free"), inc=("ac", 1))
                        s.setg("pg_free", "ac")
                        s.op("vector", lambda v, ec=ec, hb=hb: v.tensor_tensor(
                            hb[:, ec, :], scr[:], p_u[:], ALU.mult),
                            waits=[("ac", s.val("ac")), ("mm", mm_u)] + s.gate("mhid%d_free" % (tt % 2)),
                            inc=("ve", 1))
                        s.setg("pu_free", "ve")
                        s.setg("scr_free", "ve")
                    ve_hid = s.val("ve")
                    if pend_moe[0] is not None:
                        emit_down_moe(pend_moe[0])
                    pend_moe[0] = (k, tt, ve_hid, wb, ld_x, gp_acc)
                    if k + 1 < NEXP and tt < 2:
                        emit_wload_moe(k + 1, tt)
                s.setg("gateb%d_free" % wb, "mm")
                s.setg("upb%d_free" % wb, "mm")
            if pend_moe[0] is not None:
                emit_down_moe(pend_moe[0])
                pend_moe[0] = None

            def emit_down_sh(pend):
                hcp, ve_hshp = pend
                sbp = hcp % 2
                for mt in range(2):
                    for dh in range(2):
                        pi = mt * 2 + dh
                        s.op("tensor", lambda t_, mt=mt, dh=dh, pi=pi, sbp=sbp, hcp=hcp: t_.matmul(
                            p_d[pi][:], hsh[sbp][:, mt * 128:(mt + 1) * 128],
                            shd[sbp][:, dh * 512:(dh + 1) * 512],
                            start=(hcp == 0), stop=(hcp == 7)),
                            waits=[("ve", ve_hshp)] +
                                  (s.gate("pd0_free", "pd1_free", "pd2_free", "pd3_free")
                                   if hcp == 0 else []),
                            inc=("mm", 1) if (mt == 1 and dh == 1) else None)
                s.setg("hsh%d_free" % sbp, "mm")
                s.setg("shwg%d_free" % sbp, "mm")

            for tt in range(8):
                t0 = tt * 256
                pend_sh = None
                for hc in range(8):
                    sb = hc % 2
                    wait_cast = [("wst", wst_cast)] if (h == 0 and tt == 0 and hc < 2) else []
                    s.op("sync", lambda e, hc=hc, sb=sb: e.dma_start(
                        out=shw[0 * 2 + sb][:].rearrange("p a b -> p (a b)"), in_=shb[0][hc]),
                        waits=wait_cast + s.gate("shwg%d_free" % sb), inc=("ld", 16))
                    s.op("sync", lambda e, hc=hc, sb=sb: e.dma_start(
                        out=shw[1 * 2 + sb][:].rearrange("p a b -> p (a b)"), in_=shb[1][hc]),
                        inc=("ld", 16))
                    s.op("sync", lambda e, hc=hc, sb=sb: e.dma_start(
                        out=shd[sb][:], in_=shb[2][hc]),
                        inc=("ld", 16))
                    shld = s.val("ld")
                    for dc in range(8):
                        s.op("tensor", lambda t_, dc=dc, sb=sb, t0=t0: t_.matmul(
                            p_g[:, 0:256], shw[sb][:, dc, :], xh[:, dc, t0:t0 + 256],
                            start=(dc == 0), stop=(dc == 7)),
                            waits=[("ld", shld)] + s.gate("pg_free"),
                            inc=("mm", 1) if dc == 7 else None)
                    mm_g = s.val("mm")
                    for dc in range(8):
                        s.op("tensor", lambda t_, dc=dc, sb=sb, t0=t0: t_.matmul(
                            p_u[:, 0:256], shw[2 + sb][:, dc, :], xh[:, dc, t0:t0 + 256],
                            start=(dc == 0), stop=(dc == 7)),
                            waits=s.gate("pu_free"),
                            inc=("mm", 1) if dc == 7 else None)
                    mm_u = s.val("mm")
                    s.op("scalar", lambda a: a.activation(scrsh[:], p_g[:, 0:256], AF.Silu,
                                                          bias=0.0, scale=1.0),
                         waits=[("mm", mm_g)] + s.gate("scrsh_free"), inc=("ac", 1))
                    s.setg("pg_free", "ac")
                    s.op("vector", lambda v, sb=sb: v.tensor_tensor(hsh[sb][:], scrsh[:], p_u[:, 0:256], ALU.mult),
                         waits=[("ac", s.val("ac")), ("mm", mm_u)] + s.gate("hsh%d_free" % sb),
                         inc=("ve", 1))
                    s.setg("pu_free", "ve")
                    s.setg("scrsh_free", "ve")
                    ve_hsh = s.val("ve")
                    if pend_sh is not None:
                        emit_down_sh(pend_sh)
                    pend_sh = (hc, ve_hsh)
                emit_down_sh(pend_sh)
                mm_shd = s.val("mm")
                for mt in range(2):
                    jm = tt * 2 + mt
                    for dh in range(2):
                        pi = mt * 2 + dh
                        s.op("vector", lambda v, jm=jm, dh=dh, pi=pi: v.tensor_tensor(
                            acc[:, jm, dh * 512:(dh + 1) * 512], p_d[pi][:],
                            acc[:, jm, dh * 512:(dh + 1) * 512], ALU.add),
                            waits=[("mm", mm_shd)],
                            inc=("ve", 1) if pi == 3 else None)
                for pi in range(4):
                    s.setg("pd%d_free" % pi, "ve")
            ve_acc = s.val("ve")
            s.op("gpsimd", lambda e, h=h: e.dma_start(
                out=moe.rearrange("(m p) d -> p m d", p=128)[:, h * NJM:(h + 1) * NJM, :],
                in_=acc[:]),
                waits=[("ve", ve_acc)], inc=("so", 16))
            s.setg("acc_free", "so")
            s.setg("xh_free", "so")

        return




def emit_rope_ops(s, ktmp, krop, tmpa, tmpb, cost, sint, jt):
    cosv = cost[:, jt, :].rearrange("p (u i) -> p u i", u=1).broadcast_to([128, 8, 32])
    sinv = sint[:, jt, :].rearrange("p (u i) -> p u i", u=1).broadcast_to([128, 8, 32])
    tv = ktmp[:].rearrange("p (h i z) -> p h i z", h=8, z=2)
    E = tv[:, :, :, 0]
    O = tv[:, :, :, 1]
    ov = krop[:].rearrange("p (h z i) -> p h z i", h=8, z=2)
    outA = ov[:, :, 0, :]
    outB = ov[:, :, 1, :]
    ta = tmpa[:].rearrange("p (h i) -> p h i", h=8)
    tb = tmpb[:].rearrange("p (h i) -> p h i", h=8)
    def vg(emit):
        s.op("vector", emit, waits=[("ve", s.val("ve"))], inc=("ve", 1))
    vg(lambda v: v.tensor_tensor(ta, E, cosv, ALU.mult))
    vg(lambda v: v.tensor_tensor(tb, O, sinv, ALU.mult))
    vg(lambda v: v.tensor_tensor(outA, ta, tb, ALU.subtract))
    vg(lambda v: v.tensor_tensor(ta, E, sinv, ALU.mult))
    vg(lambda v: v.tensor_tensor(tb, O, cosv, ALU.mult))
    vg(lambda v: v.tensor_tensor(outB, ta, tb, ALU.add))


def emit_attn(nc, s, dram):
    ap = dram
    xn1 = ap("xn1", [8, 128, S_LEN], BF16, kind="ExternalInput")
    x1 = ap("x1", [S_LEN, DM], F32, kind="ExternalInput")
    moe = ap("moe", [S_LEN, DM], F32, kind="ExternalInput")
    wqa = ap("wqa", [DM, DM], F32, kind="ExternalInput")
    wka = ap("wka", [DM, DM], F32, kind="ExternalInput")
    wva = ap("wva", [DM, DM], F32, kind="ExternalInput")
    woa = ap("woa", [DM, DM], F32, kind="ExternalInput")
    bq = ap("bq", [1, DM], F32, kind="ExternalInput")
    bk = ap("bk", [1, DM], F32, kind="ExternalInput")
    bv = ap("bv", [1, DM], F32, kind="ExternalInput")
    bo = ap("bo", [1, DM], F32, kind="ExternalInput")
    rcos = ap("rcos", [S_LEN, 32], F32, kind="ExternalInput")
    rsin = ap("rsin", [S_LEN, 32], F32, kind="ExternalInput")
    idb = ap("idb", [128, 128], BF16, kind="ExternalInput")
    dmask = ap("dmask", [4, 128, 512], BF16, kind="ExternalInput")
    out = ap("out", [S_LEN, DM], F32, kind="ExternalOutput")
    att0 = ap("att0", [8, 128, 4, 512], BF16, kind="Internal")

    es = ExitStack()
    S = lambda n, *a: es.enter_context(nc.sbuf_tensor("c_" + n, *a))
    P = lambda n, *a: es.enter_context(nc.psum_tensor("c_" + n, *a))

    NQT = 8
    xn1t = [S("xn1t0", [128, 8, 512], BF16), S("xn1t1", [128, 8, 512], BF16)]
    kfm = S("kfm", [128, 4, S_LEN], BF16)
    vaug = S("vaug", [128, 32, 520], BF16)
    qfm = S("qfm", [128, 4, 512], BF16)
    attg0t = [S("attg0t0", [128, 4, 512], BF16), S("attg0t1", [128, 4, 512], BF16)]
    atile = S("atile", [128, 4, 512], BF16)
    wqb = S("wqb", [128, 8, 512], BF16)
    wkv = S("wkv", [128, 8, 512], BF16)
    wob = S("wob", [128, 8, DM], BF16)
    wstg = [S("wstg0", [128, 512], F32), S("wstg1", [128, 512], F32)]
    cost = S("cost", [128, 32, 32], F32)
    sint = S("sint", [128, 32, 32], F32)
    maskt = S("maskt", [128, 4, 512], BF16)
    ktmp = S("ktmp", [128, 512], F32)
    krop = S("krop", [128, 512], F32)
    tmpa = S("tmpa", [128, 256], F32)
    tmpb = S("tmpb", [128, 256], F32)
    krb = S("krb", [128, 512], BF16)
    sqs = S("sqs", [128, 512], F32)
    knall = S("knall", [128, 32, 8], F32)
    qnall = S("qnall", [128, 32, 8], F32)
    knpad = S("knpad", [128, 128], BF16)
    mstat = S("mstat", [128, 8], F32)
    negMb = S("negMb", [128, 1], F32)
    rs_t = S("rs_t", [128, 512], F32)
    rb_s = S("rb_s", [64, 512], F32)
    atnb = S("atnb", [64, 512], BF16)
    pbuf = [S("pbuf%d" % i, [128, 512], BF16) for i in range(3)]
    bqb = S("bqb", [128, 512], F32)
    bkb = S("bkb", [128, 512], F32)
    bvb = S("bvb", [128, 512], F32)
    bob = S("bob", [128, DM], F32)
    x1m = S("x1m", [128, DM], F32)
    moem = S("moem", [128, DM], F32)
    base = S("base", [128, DM], F32)
    outt = S("outt", [128, DM], F32)
    onesb = S("onesb", [128, 128], F32)
    idbt = S("idbt", [128, 128], BF16)
    id64 = S("id64", [128, 128], BF16)

    p_sc = [P("p_sc%d" % i, [128, 512], F32) for i in range(3)]
    p_at = P("p_at", [128, 512], F32)
    p_aux = P("p_auxa", [128, 512], F32)
    p_pr = P("p_pr", [128, 512], F32)
    p_tr = P("p_tra", [128, 1024], BF16)

    with es:
        s.op("sync", lambda e: e.dma_start(out=idbt[:], in_=idb[:]), inc=("ld", 16))
        s.op("sync", lambda e: e.dma_start(
            out=cost[:], in_=rcos.rearrange("(j p) i -> p j i", p=128)), inc=("ld", 16))
        s.op("sync", lambda e: e.dma_start(
            out=sint[:], in_=rsin.rearrange("(j p) i -> p j i", p=128)), inc=("ld", 16))
        s.op("sync", lambda e: e.dma_start(
            out=maskt[:], in_=dmask.rearrange("m p f -> p m f")), inc=("ld", 16))
        s.op("sync", lambda e: e.dma_start(out=bob[0:1, :], in_=bo[:]), inc=("ld", 16))
        ld_c = s.val("ld")
        s.op("gpsimd", lambda e: e.memset(onesb[:], 1.0), inc=("gp", 1))
        s.op("gpsimd", lambda e: e.memset(knpad[:], 0.0), inc=("gp", 1))
        s.op("gpsimd", lambda e: e.memset(id64[:], 0.0), inc=("gp", 1))
        gp_c = s.val("gp")
        s.op("vector", lambda v: v.tensor_copy(id64[0:64, 0:64], idbt[0:64, 0:64]),
             waits=[("ld", ld_c), ("gp", gp_c)], inc=("ve", 1))
        for half in range(2):
            s.op("tensor", lambda t_, half=half: t_.matmul(
                p_aux[:], onesb[0:1, :], bob[0:1, half * 512:(half + 1) * 512],
                start=True, stop=True),
                waits=[("ld", ld_c), ("gp", gp_c)] + s.gate("paux_free"), inc=("mm", 1))
            s.op("vector", lambda v, half=half: v.tensor_copy(
                bob[:, half * 512:(half + 1) * 512], p_aux[:]),
                waits=[("mm", s.val("mm"))], inc=("ve", 1))
            s.setg("paux_free", "ve")

        def load_whalf(whbm, dst, tag, gc, ncols=512):
            for dc in range(8):
                sb = dc % 2
                s.op("sync", lambda e, whbm=whbm, dc=dc, sb=sb, gc=gc, ncols=ncols: e.dma_start(
                    out=wstg[sb][:, 0:ncols], in_=whbm[dc * 128:(dc + 1) * 128, gc:gc + ncols]),
                    waits=s.gate("wstg%d_free" % sb), inc=("ld", 16))
                s.op("vector", lambda v, dc=dc, sb=sb, dst=dst, ncols=ncols: v.tensor_copy(
                    dst[:, dc, 0:ncols], wstg[sb][:, 0:ncols]),
                    waits=[("ld", s.val("ld"))] + (s.gate(tag + "_free") if dc == 0 else []),
                    inc=("ve", 1))
                s.setg("wstg%d_free" % sb, "ve")
            s.setg(tag + "_ready", "ve")

        def proj_psum(src_tile, wtile, mt, wait_extra):
            for dc in range(8):
                s.op("tensor", lambda t_, dc=dc, src_tile=src_tile, wtile=wtile, mt=mt: t_.matmul(
                    p_pr[:], src_tile[:, dc, mt * 128:(mt + 1) * 128], wtile[:, dc, 0:512],
                    start=(dc == 0), stop=(dc == 7)),
                    waits=wait_extra + (s.gate("ppr_free") if dc == 0 else []),
                    inc=("mm", 1) if dc == 7 else None)
            return s.val("mm")

        def pmax_to_scalar(src_col_f32, dst_slot):
            s.op("vector", lambda v: v.tensor_copy(knpad[:, 0:1], src_col_f32),
                 waits=[("ve", s.val("ve"))] + s.gate("knpad_free"), inc=("ve", 1))
            s.op("tensor", lambda t_: t_.transpose(p_tr[:, 0:128], knpad[:], idbt[:]),
                 waits=[("ve", s.val("ve")), ("ld", ld_c)] + s.gate("ptr_free"),
                 inc=("mm", 1))
            s.op("vector", lambda v, dst_slot=dst_slot: v.reduce_max(
                mstat[0:1, dst_slot:dst_slot + 1], p_tr[0:1, 0:128], axis=AX.X),
                waits=[("mm", s.val("mm"))], inc=("ve", 1))
            s.setg("ptr_free", "ve")
            s.setg("knpad_free", "ve")

        for g in range(2):
            gc = g * 512
            for (bsrc, btile) in ((bq, bqb), (bk, bkb), (bv, bvb)):
                s.op("sync", lambda e, bsrc=bsrc, btile=btile, gc=gc: e.dma_start(
                    out=btile[0:1, :], in_=bsrc[:, gc:gc + 512]),
                    inc=("ld", 16))
                s.op("tensor", lambda t_, btile=btile: t_.matmul(
                    p_aux[:], onesb[0:1, :], btile[0:1, :], start=True, stop=True),
                    waits=[("ld", s.val("ld")), ("gp", gp_c)] + s.gate("paux_free"),
                    inc=("mm", 1))
                s.op("vector", lambda v, btile=btile: v.tensor_copy(btile[:], p_aux[:]),
                     waits=[("mm", s.val("mm"))], inc=("ve", 1))
                s.setg("paux_free", "ve")
            load_whalf(wka, wkv, "wkv", gc)
            s.op("gpsimd", lambda e: e.memset(vaug[:], 1.0),
                 waits=s.gate("vaug_free"), inc=("gp", 1))
            gp_vaug = s.val("gp")

            for tt in range(NQT):
                t0 = tt * 512
                xb = xn1t[tt % 2]
                s.op("sync", lambda e, t0=t0, xb=xb: e.dma_start(
                    out=xb[:], in_=xn1[:, :, t0:t0 + 512].rearrange("c p t -> p c t")),
                    waits=s.gate("xn1t%d_free" % (tt % 2)), inc=("ld", 16))
                ld_x = s.val("ld")
                for mt in range(4):
                    jt = tt * 4 + mt
                    mm_k = proj_psum(xb, wkv, mt, [("ld", ld_x)] + s.gate("wkv_ready"))
                    s.op("vector", lambda v: v.tensor_tensor(ktmp[:], p_pr[:], bkb[:], ALU.add),
                         waits=[("mm", mm_k)] + s.gate("ktmp_free"), inc=("ve", 1))
                    s.setg("ppr_free", "ve")
                    ve_ktmp = s.val("ve")
                    s.op("scalar", lambda a: a.activation(sqs[:], ktmp[:], AF.Square,
                                                          bias=0.0, scale=1.0),
                         waits=[("ve", ve_ktmp)] + s.gate("sqs_free"), inc=("ac", 1))
                    s.op("vector", lambda v, jt=jt: v.reduce_sum(
                        knall[:, jt, :], sqs[:].rearrange("p (h d) -> p h d", h=8), axis=AX.X),
                        waits=[("ac", s.val("ac"))], inc=("ve", 1))
                    s.setg("sqs_free", "ve")
                    emit_rope_ops(s, ktmp, krop, tmpa, tmpb, cost, sint, jt)
                    s.op("vector", lambda v: v.tensor_copy(krb[:], krop[:]),
                         waits=[("ve", s.val("ve"))], inc=("ve", 1))
                    s.setg("ktmp_free", "ve")
                    ve_krb = s.val("ve")
                    for c in range(4):
                        s.op("tensor", lambda t_, c=c: t_.transpose(
                            p_tr[:, 0:128], krb[:, c * 128:(c + 1) * 128], idbt[:]),
                            waits=[("ve", ve_krb), ("ld", ld_c)] + s.gate("ptr_free"),
                            inc=("mm", 1))
                        s.op("vector", lambda v, c=c, jt=jt: v.tensor_copy(
                            kfm[:, c, jt * 128:(jt + 1) * 128], p_tr[:, 0:128]),
                            waits=[("mm", s.val("mm"))], inc=("ve", 1))
                        s.setg("ptr_free", "ve")
                s.setg("xn1t%d_free" % (tt % 2), "mm")
            s.op("vector", lambda v: v.reduce_max(
                mstat[:, 2:3], knall[:].rearrange("p a b -> p (a b)"), axis=AX.X),
                waits=[("ve", s.val("ve"))], inc=("ve", 1))
            pmax_to_scalar(mstat[:, 2:3], 0)
            s.setg("wkv_free", "mm")
            load_whalf(wva, wkv, "wkv", gc)
            for tt in range(NQT):
                t0 = tt * 512
                xb = xn1t[tt % 2]
                s.op("sync", lambda e, t0=t0, xb=xb: e.dma_start(
                    out=xb[:], in_=xn1[:, :, t0:t0 + 512].rearrange("c p t -> p c t")),
                    waits=s.gate("xn1t%d_free" % (tt % 2)), inc=("ld", 16))
                ld_x = s.val("ld")
                for mt in range(4):
                    jt = tt * 4 + mt
                    mm_v = proj_psum(xb, wkv, mt, [("ld", ld_x)] + s.gate("wkv_ready"))
                    vv = vaug[:, jt, :].rearrange("p (h e) -> p h e", h=8)[:, :, 0:64]
                    pv = p_pr[:].rearrange("p (h e) -> p h e", h=8)
                    bb = bvb[:].rearrange("p (h e) -> p h e", h=8)
                    s.op("vector", lambda v, vv=vv, pv=pv, bb=bb: v.tensor_tensor(
                        vv, pv, bb, ALU.add),
                        waits=[("mm", mm_v), ("gp", gp_vaug)], inc=("ve", 1))
                    s.setg("ppr_free", "ve")
                s.setg("xn1t%d_free" % (tt % 2), "mm")
            s.setg("wkv_free", "mm")
            load_whalf(wqa, wqb, "wqb", gc)
            if g == 1:
                for dc in range(8):
                    for half in range(2):
                        sb = half
                        s.op("sync", lambda e, dc=dc, half=half, sb=sb: e.dma_start(
                            out=wstg[sb][:], in_=woa[dc * 128:(dc + 1) * 128,
                                                    half * 512:(half + 1) * 512]),
                            waits=s.gate("wstg%d_free" % sb), inc=("ld", 16))
                        s.op("vector", lambda v, dc=dc, half=half, sb=sb: v.tensor_copy(
                            wob[:, dc, half * 512:(half + 1) * 512], wstg[sb][:]),
                            waits=[("ld", s.val("ld"))], inc=("ve", 1))
                        s.setg("wstg%d_free" % sb, "ve")
            ve_wo = s.val("ve")

            for qt in range(NQT):
                t0 = qt * 512
                nkc = 4 * (qt + 1)
                xb = xn1t[qt % 2]
                s.op("sync", lambda e, t0=t0, xb=xb: e.dma_start(
                    out=xb[:], in_=xn1[:, :, t0:t0 + 512].rearrange("c p t -> p c t")),
                    waits=s.gate("xn1t%d_free" % (qt % 2)), inc=("ld", 16))
                ld_x = s.val("ld")
                if g == 1:
                    ab = attg0t[qt % 2]
                    s.op("sync", lambda e, qt=qt, ab=ab: e.dma_start(
                        out=ab[:], in_=att0[qt]),
                        waits=s.gate("attg0t%d_free" % (qt % 2)), inc=("ld", 16))
                else:
                    ab = None
                ld_a = s.val("ld")
                for mt in range(4):
                    jt = qt * 4 + mt
                    mm_q = proj_psum(xb, wqb, mt, [("ld", ld_x)] + s.gate("wqb_ready"))
                    s.op("vector", lambda v: v.tensor_tensor(ktmp[:], p_pr[:], bqb[:], ALU.add),
                         waits=[("mm", mm_q)] + s.gate("ktmp_free"), inc=("ve", 1))
                    s.setg("ppr_free", "ve")
                    ve_ktmp = s.val("ve")
                    s.op("scalar", lambda a: a.activation(sqs[:], ktmp[:], AF.Square,
                                                          bias=0.0, scale=1.0),
                         waits=[("ve", ve_ktmp)] + s.gate("sqs_free"), inc=("ac", 1))
                    s.op("vector", lambda v, jt=jt: v.reduce_sum(
                        qnall[:, jt, :], sqs[:].rearrange("p (h d) -> p h d", h=8), axis=AX.X),
                        waits=[("ac", s.val("ac"))], inc=("ve", 1))
                    s.setg("sqs_free", "ve")
                    emit_rope_ops(s, ktmp, krop, tmpa, tmpb, cost, sint, jt)
                    s.op("vector", lambda v: v.tensor_copy(krb[:], krop[:]),
                         waits=[("ve", s.val("ve"))], inc=("ve", 1))
                    s.setg("ktmp_free", "ve")
                    ve_krb = s.val("ve")
                    for c in range(4):
                        s.op("tensor", lambda t_, c=c, mt=mt: t_.transpose(
                            p_tr[:, 0:128], krb[:, c * 128:(c + 1) * 128], idbt[:]),
                            waits=[("ve", ve_krb)] + s.gate("ptr_free"),
                            inc=("mm", 1))
                        s.op("vector", lambda v, c=c, mt=mt: v.tensor_copy(
                            qfm[:, c, mt * 128:(mt + 1) * 128], p_tr[:, 0:128]),
                            waits=[("mm", s.val("mm"))], inc=("ve", 1))
                        s.setg("ptr_free", "ve")
                s.op("vector", lambda v, qt=qt: v.reduce_max(
                    mstat[:, 3:4],
                    qnall[:, qt * 4:(qt + 1) * 4, :].rearrange("p a b -> p (a b)"), axis=AX.X),
                    waits=[("ve", s.val("ve"))], inc=("ve", 1))
                pmax_to_scalar(mstat[:, 3:4], 1)
                def vg1(emit):
                    s.op("vector", emit, waits=[("ve", s.val("ve"))], inc=("ve", 1))
                vg1(lambda v: v.tensor_mul(mstat[0:1, 4:5], mstat[0:1, 0:1], mstat[0:1, 1:2]))
                vg1(lambda v: v.tensor_scalar(mstat[0:1, 4:5], mstat[0:1, 4:5], 1e-30, None, ALU.add))
                s.op("scalar", lambda a: a.activation(mstat[0:1, 5:6], mstat[0:1, 4:5],
                                                      AF.Ln, bias=0.0, scale=1.0),
                     waits=[("ve", s.val("ve"))], inc=("ac", 1))
                s.op("scalar", lambda a: a.activation(mstat[0:1, 6:7], mstat[0:1, 5:6],
                                                      AF.Exp, bias=0.0, scale=0.5),
                     waits=[("ac", s.val("ac"))], inc=("ac", 1))
                vg1(lambda v: v.tensor_scalar(mstat[0:1, 7:8], mstat[0:1, 6:7], -0.125, None, ALU.mult))
                s.op("tensor", lambda t_: t_.matmul(
                    p_aux[:, 0:1], onesb[0:1, :], mstat[0:1, 7:8], start=True, stop=True),
                    waits=[("ve", s.val("ve")), ("ac", s.val("ac"))] + s.gate("paux_free"),
                    inc=("mm", 1))
                s.op("vector", lambda v: v.tensor_copy(negMb[:], p_aux[:, 0:1]),
                     waits=[("mm", s.val("mm"))] + s.gate("negM_inuse"), inc=("ve", 1))
                s.setg("paux_free", "ve")
                s.setg("negM_ready", "ve")
                ve_qfm = s.val("ve")
                for hl in range(8):
                    cc, ro = hl // 2, 64 * (hl % 2)

                    def emit_pv(prev, hl=hl, nkc=nkc):
                        kc, ac_exp, ve_pb = prev
                        pp = kc % (ATTN_LAG + 1)
                        s.op("tensor", lambda t_, kc=kc, pp=pp, hl=hl, nkc=nkc: t_.matmul(
                            p_at[0:65, :], vaug[:, kc, hl * 65:hl * 65 + 65],
                            pbuf[pp][:],
                            start=(kc == 0), stop=(kc == nkc - 1)),
                            waits=[("ac", ac_exp), ("ve", ve_pb)] +
                                  (s.gate("pat_free") if kc == 0 else []),
                            inc=("mm", 1) if kc == nkc - 1 else None)
                        s.setg("pbuf%d_free" % pp, "mm")

                    pend = []
                    for kc in range(nkc):
                        pp = kc % (ATTN_LAG + 1)
                        s.op("tensor", lambda t_, kc=kc, pp=pp, cc=cc, ro=ro: t_.matmul(
                            p_sc[pp][:],
                            kfm[:, cc, kc * 128:(kc + 1) * 128][ro:ro + 64, :],
                            qfm[:, cc, :][ro:ro + 64, :],
                            start=True, stop=True),
                            waits=[("ve", ve_qfm)] + s.gate("psc%d_free" % pp),
                            inc=("mm", 1))
                        mm_st = s.val("mm")
                        s.op("scalar", lambda a, pp=pp: a.activation(
                            pbuf[pp][:], p_sc[pp][:],
                            AF.Exp, bias=negMb[:, 0:1], scale=0.125),
                            waits=[("mm", mm_st)] + s.gate("pbuf%d_free" % pp) +
                                  s.gate("negM_ready"),
                            inc=("ac", 1))
                        s.setg("psc%d_free" % pp, "ac")
                        ac_exp = s.val("ac")
                        j = kc - 4 * qt
                        if j >= 0:
                            s.op("vector", lambda v, pp=pp, j=j: v.tensor_tensor(
                                pbuf[pp][:], pbuf[pp][:],
                                maskt[:, j, :], ALU.mult),
                                waits=[("ac", ac_exp), ("ld", ld_c)], inc=("ve", 1))
                        ve_pb = s.val("ve")
                        pend.append((kc, ac_exp, ve_pb))
                        if len(pend) > ATTN_LAG:
                            emit_pv(pend.pop(0))
                    for pr in pend:
                        emit_pv(pr)
                    mm_pv = s.val("mm")
                    s.op("vector", lambda v: v.reciprocal(rs_t[64:65, :], p_at[64:65, :]),
                         waits=[("mm", mm_pv)] + s.gate("rst_free"), inc=("ve", 1))
                    s.op("tensor", lambda t_: t_.matmul(
                        p_aux[0:64, :], onesb[64:65, 0:64], rs_t[64:65, :],
                        start=True, stop=True),
                        waits=[("ve", s.val("ve"))] + s.gate("paux_free"), inc=("mm", 1))
                    s.op("vector", lambda v: v.tensor_copy(rb_s[:], p_aux[0:64, :]),
                         waits=[("mm", s.val("mm"))], inc=("ve", 1))
                    s.setg("rst_free", "ve")
                    if ro == 0:
                        s.op("vector", lambda v, cc=cc: v.tensor_tensor(
                            atile[0:64, cc, :], p_at[0:64, :], rb_s[:], ALU.mult),
                            waits=[("ve", s.val("ve"))] +
                                  (s.gate("atile_free") if hl == 0 else []),
                            inc=("ve", 1))
                        s.setg("pat_free", "ve")
                        s.setg("paux_free", "ve")
                        s.setg("negM_inuse", "ve")
                    else:
                        s.op("vector", lambda v: v.tensor_tensor(
                            atnb[:], p_at[0:64, :], rb_s[:], ALU.mult),
                            waits=[("ve", s.val("ve"))], inc=("ve", 1))
                        s.setg("pat_free", "ve")
                        s.op("tensor", lambda t_: t_.matmul(
                            p_aux[64:128, :], id64[0:64, 0:64], atnb[:],
                            start=True, stop=True),
                            waits=[("ve", s.val("ve"))], inc=("mm", 1))
                        s.op("vector", lambda v, cc=cc: v.tensor_copy(
                            atile[64:128, cc, :], p_aux[64:128, :]),
                            waits=[("mm", s.val("mm"))], inc=("ve", 1))
                        s.setg("paux_free", "ve")
                        s.setg("negM_inuse", "ve")
                ve_atile = s.val("ve")
                if g == 0:
                    s.op("gpsimd", lambda e, qt=qt: e.dma_start(out=att0[qt], in_=atile[:]),
                         waits=[("ve", ve_atile)], inc=("so", 16))
                    s.setg("atile_free", "so")
                else:
                    for mt in range(4):
                        jt = qt * 4 + mt
                        s.op("sync", lambda e, jt=jt: e.dma_start(
                            out=x1m[:], in_=x1.rearrange("(m p) d -> p m d", p=128)[:, jt, :]),
                            waits=s.gate("x1m_free"), inc=("ld", 16))
                        s.op("sync", lambda e, jt=jt: e.dma_start(
                            out=moem[:], in_=moe.rearrange("(m p) d -> p m d", p=128)[:, jt, :]),
                            inc=("ld", 16))
                        ld_e = s.val("ld")
                        s.op("vector", lambda v: v.tensor_add(base[:], x1m[:], moem[:]),
                             waits=[("ld", ld_e)] + s.gate("base_free"), inc=("ve", 1))
                        s.op("vector", lambda v: v.tensor_add(base[:], base[:], bob[:]),
                             waits=[("ve", s.val("ve"))], inc=("ve", 1))
                        s.setg("x1m_free", "ve")
                        ve_base = s.val("ve")
                        for dh in range(2):
                            pw = p_pr if dh == 0 else p_at
                            gkey = "ppr_free" if dh == 0 else "pat_free"
                            for c in range(4):
                                s.op("tensor", lambda t_, c=c, mt=mt, dh=dh, pw=pw, ab=ab: t_.matmul(
                                    pw[:], ab[:, c, mt * 128:(mt + 1) * 128],
                                    wob[:, c, dh * 512:(dh + 1) * 512],
                                    start=(c == 0), stop=False),
                                    waits=([("ld", ld_a), ("ve", ve_atile), ("ve", ve_wo)] +
                                           (s.gate(gkey) if c == 0 else [])))
                            for c in range(4):
                                s.op("tensor", lambda t_, c=c, mt=mt, dh=dh, pw=pw: t_.matmul(
                                    pw[:], atile[:, c, mt * 128:(mt + 1) * 128],
                                    wob[:, 4 + c, dh * 512:(dh + 1) * 512],
                                    start=False, stop=(c == 3)),
                                    inc=("mm", 1) if c == 3 else None)
                            s.op("vector", lambda v, dh=dh, pw=pw: v.tensor_tensor(
                                outt[:, dh * 512:(dh + 1) * 512], pw[:],
                                base[:, dh * 512:(dh + 1) * 512], ALU.add),
                                waits=[("mm", s.val("mm")), ("ve", ve_base)], inc=("ve", 1))
                            s.setg("ppr_free" if dh == 0 else "pat_free", "ve")
                        s.setg("base_free", "ve")
                        ve_out = s.val("ve")
                        s.op("gpsimd", lambda e, jt=jt: e.dma_start(
                            out=out.rearrange("(m p) d -> p m d", p=128)[:, jt, :], in_=outt[:]),
                            waits=[("ve", ve_out)], inc=("so", 16))
                    s.setg("attg0t%d_free" % (qt % 2), "mm")
                    s.setg("atile_free", "mm")
                s.setg("xn1t%d_free" % (qt % 2), "mm")
            s.setg("vaug_free", "mm")
            s.setg("wqb_free", "mm")
            s.setg("wkv_free", "mm")

        return



def _l0_host_inputs(inputs, core):
    f32 = np.float32
    d = {}
    d["x"] = np.ascontiguousarray(np.asarray(inputs["L0_attn_x"])[core], dtype=f32)
    d["wq"] = np.ascontiguousarray(np.asarray(inputs["L0_attn_W_Q"], dtype=f32))
    d["wk"] = np.ascontiguousarray(np.asarray(inputs["L0_attn_W_K"], dtype=f32))
    d["wkp"] = np.ascontiguousarray(np.asarray(inputs["L0_attn_W_Kp"], dtype=f32))
    d["wv"] = np.ascontiguousarray(np.asarray(inputs["L0_attn_W_V"], dtype=f32))
    d["wo"] = np.ascontiguousarray(np.asarray(inputs["L0_attn_W_O"], dtype=f32))
    d["gam"] = np.ascontiguousarray(np.asarray(inputs["L0_attn_ln_gamma"], dtype=f32)[None, :])
    d["bet"] = np.ascontiguousarray(np.asarray(inputs["L0_attn_ln_beta"], dtype=f32)[None, :])
    d["wgT"] = np.ascontiguousarray(np.asarray(inputs["L0_ffn_w_gate"], dtype=np.int32).T)
    d["wuT"] = np.ascontiguousarray(np.asarray(inputs["L0_ffn_w_up"], dtype=np.int32).T)
    d["wdT"] = np.ascontiguousarray(np.asarray(inputs["L0_ffn_w_down"], dtype=np.int32).T)
    d["rwT"] = np.ascontiguousarray(np.asarray(inputs["L1_ffn_router_w"], dtype=f32).T)
    d["rbias"] = np.ascontiguousarray(np.asarray(inputs["L1_ffn_router_bias"], dtype=f32)[None, :])
    d["idb"] = np.eye(128, dtype=BFNP)
    q = np.arange(32)[:, None, None]
    j = np.arange(32)[None, :, None]
    k = np.arange(32)[None, None, :]
    wm = np.where((j > q) | (k > q), np.float32(-1e9), np.float32(0.0))
    d["wmask"] = np.ascontiguousarray(wm.reshape(32, 1024))
    d["idx16"] = np.ascontiguousarray(np.broadcast_to(np.arange(16, dtype=f32), (128, 16)))
    return d


def _host_consts_l0():
    f32 = np.float32
    q = np.arange(32)[:, None, None]
    j = np.arange(32)[None, :, None]
    k = np.arange(32)[None, None, :]
    wm = np.where((j > q) | (k > q), np.float32(-1e9), np.float32(0.0))
    return {
        "idb": np.eye(128, dtype=BFNP),
        "wmask": np.ascontiguousarray(wm.reshape(32, 1024).astype(f32)),
        "idx16": np.ascontiguousarray(np.broadcast_to(np.arange(16, dtype=f32), (128, 16))),
    }


def _host_consts_attn():
    f32 = np.float32
    inv = (10000.0 ** (-2.0 * np.arange(32, dtype=f32) / 64.0)).astype(f32)
    ang = np.arange(S_LEN, dtype=f32)[:, None] * inv[None, :]
    kk = np.arange(128)[:, None]
    qq = np.arange(512)[None, :]
    dm = np.stack([(kk <= qq - 128 * j).astype(BFNP) for j in range(4)])
    return {
        "rcos": np.cos(ang).astype(f32),
        "rsin": np.sin(ang).astype(f32),
        "idb": np.eye(128, dtype=BFNP),
        "dmask": np.ascontiguousarray(dm),
    }


def kernel(**inputs):
    f32 = np.float32
    A = lambda k: np.asarray(inputs[k], f32)
    x_all = A("L0_attn_x")

    l0_shared = {
        "wq": np.ascontiguousarray(A("L0_attn_W_Q")),
        "wk": np.ascontiguousarray(A("L0_attn_W_K")),
        "wkp": np.ascontiguousarray(A("L0_attn_W_Kp")),
        "wv": np.ascontiguousarray(A("L0_attn_W_V")),
        "wo": np.ascontiguousarray(A("L0_attn_W_O")),
        "gam": np.ascontiguousarray(A("L0_attn_ln_gamma")[None, :]),
        "bet": np.ascontiguousarray(A("L0_attn_ln_beta")[None, :]),
        "wgT": np.ascontiguousarray(np.asarray(inputs["L0_ffn_w_gate"], np.int32).T),
        "wuT": np.ascontiguousarray(np.asarray(inputs["L0_ffn_w_up"], np.int32).T),
        "wdT": np.ascontiguousarray(np.asarray(inputs["L0_ffn_w_down"], np.int32).T),
        "rwT": np.ascontiguousarray(A("L1_ffn_router_w").T),
        "rbias": np.ascontiguousarray(A("L1_ffn_router_bias")[None, :]),
    }
    l0_shared.update(_host_consts_l0())
    moe_shared = {
        "eup": np.ascontiguousarray(A("L1_ffn_up_proj")),
        "egt": np.ascontiguousarray(A("L1_ffn_gate_proj")),
        "edn": np.ascontiguousarray(A("L1_ffn_down_proj")),
        "shgT": np.ascontiguousarray(A("L1_ffn_sh_gate").T),
        "shuT": np.ascontiguousarray(A("L1_ffn_sh_up").T),
        "shdT": np.ascontiguousarray(A("L1_ffn_sh_down").T),
    }
    attn_shared = {
        "wqa": np.ascontiguousarray(A("L1_attn_Wq")),
        "wka": np.ascontiguousarray(A("L1_attn_Wk")),
        "wva": np.ascontiguousarray(A("L1_attn_Wv")),
        "woa": np.ascontiguousarray(A("L1_attn_Wo")),
        "bq": np.ascontiguousarray(A("L1_attn_bq")[None, :]),
        "bk": np.ascontiguousarray(A("L1_attn_bk")[None, :]),
        "bv": np.ascontiguousarray(A("L1_attn_bv")[None, :]),
        "bo": np.ascontiguousarray(A("L1_attn_bo")[None, :]),
    }
    attn_shared.update(_host_consts_attn())

    cores = list(range(NCORES))
    nc = build_fused()
    shared = {}
    shared.update(l0_shared)
    shared.update(moe_shared)
    shared.update(attn_shared)
    maps = []
    for c in cores:
        m = dict(shared)
        m["x"] = np.ascontiguousarray(x_all[c])
        maps.append(m)
    r3 = run_bass_kernel_spmd(nc, maps, core_ids=cores)
    return np.stack([np.asarray(r3.results[c]["out"], f32) for c in cores], axis=0)


# version 10
# speedup vs baseline: 1.3635x; 1.1136x over previous
import numpy as np
import ml_dtypes
import concourse.bass as bass
import concourse.mybir as mybir
from concourse.bass_utils import run_bass_kernel_spmd
from contextlib import ExitStack

F32 = mybir.dt.float32
BF16 = mybir.dt.bfloat16
I32 = mybir.dt.int32
AF = mybir.ActivationFunctionType
ALU = mybir.AluOpType
AX = mybir.AxisListType
BFNP = ml_dtypes.bfloat16

S_LEN = 4096
DM = 1024
NH = 16
HD = 64
HID0 = 3072
NEXP = 16
EDIM = 512
NCORES = 8
SCL = 1.0 / 127.0
EPS_LN = 1e-5
EPS_NORM = 1e-7

TT = 256
NJ = TT // 128
NT0 = S_LEN // TT
NHC = HID0 // 128
ATTN_LAG = 2
LDQ_U = "gpsimd"
LDQ_D = "scalar"
MOE_Q_U = "gpsimd"
MOE_Q_D = "scalar"


class Sched:

    def __init__(self):
        self.ops = []
        self.c = {}
        self.g = {}
        self.bar = {}

    def barrier(self):
        snap = tuple((k, v) for k, v in self.c.items() if v > 0)
        self.bar = {e: snap for e in ("sync", "tensor", "vector", "scalar", "gpsimd")}
        self.g = {}

    def val(self, sem):
        return self.c.get(sem, 0)

    def gate(self, *keys):
        out = []
        for k in keys:
            t = self.g.get(k)
            if t is not None:
                out.append(t)
        return out

    def setg(self, key, sem):
        self.g[key] = (sem, self.val(sem))

    def op(self, eng, emit, waits=(), inc=None):
        if eng in self.bar:
            waits = list(self.bar.pop(eng)) + list(waits)
        if inc is not None:
            self.c[inc[0]] = self.c.get(inc[0], 0) + inc[1]
        self.ops.append((eng, emit, tuple(waits), inc))

    def realize(self, block, sems):
        for name in ("sync", "tensor", "vector", "scalar", "gpsimd"):
            sub = [o for o in self.ops if o[0] == name]
            if not sub:
                continue

            def make(sub):
                def f(eng):
                    seen = {}
                    for _, emit, waits, inc in sub:
                        for sname, v in waits:
                            if v > 0 and seen.get(sname, -1) < v:
                                eng.wait_ge(sems[sname], v)
                                seen[sname] = v
                        inst = emit(eng)
                        if inc is not None:
                            inst.then_inc(sems[inc[0]], inc[1])
                return f

            getattr(block, name)(make(sub))


def _bc(apv, reps, unit_axis=1):
    return apv.rearrange("p (u n) -> p u n", u=1).broadcast_to(
        [apv.shape[0], reps, apv.shape[1]])


def _ln_stats(s, xt_view, nj, st, sqscr, in_waits):
    msum, ss, mean, var, rstd, negmr = (st["msum"], st["ss"], st["mean"],
                                        st["var"], st["rstd"], st["negmr"])
    s.op("vector", lambda v: v.reduce_sum(msum[:, 0:nj], xt_view, axis=AX.X),
         waits=in_waits, inc=("ve", 1))
    ve_msum = s.val("ve")
    for j in range(nj):
        s.op("scalar", lambda a, j=j: a.activation(
            sqscr[:, j, :], xt_view[:, j, :], AF.Square, bias=0.0, scale=1.0),
            waits=in_waits + s.gate("sqscr_free"),
            inc=("ac", 1) if j == nj - 1 else None)
    ac0 = s.val("ac")
    s.op("vector", lambda v: v.reduce_sum(ss[:, 0:nj], sqscr[:, 0:nj, :], axis=AX.X),
         waits=[("ac", ac0)], inc=("ve", 1))
    s.setg("sqscr_free", "ve")
    s.op("vector", lambda v: v.tensor_scalar(mean[:, 0:nj], msum[:, 0:nj], 1.0 / DM, None, ALU.mult),
         waits=[("ve", ve_msum)], inc=("ve", 1))
    s.op("vector", lambda v: v.tensor_tensor(negmr[:, 0:nj], mean[:, 0:nj], mean[:, 0:nj], ALU.mult),
         waits=[("ve", s.val("ve"))], inc=("ve", 1))
    s.op("vector", lambda v: v.scalar_tensor_tensor(var[:, 0:nj], ss[:, 0:nj], 1.0 / DM,
                                                    negmr[:, 0:nj], ALU.mult, ALU.subtract),
         waits=[("ve", s.val("ve"))], inc=("ve", 1))
    s.op("vector", lambda v: v.tensor_scalar(var[:, 0:nj], var[:, 0:nj], EPS_LN, None, ALU.add),
         waits=[("ve", s.val("ve"))], inc=("ve", 1))
    s.op("scalar", lambda a: a.activation(rstd[:, 0:nj], var[:, 0:nj], AF.Sqrt,
                                          bias=0.0, scale=1.0),
         waits=[("ve", s.val("ve"))], inc=("ac", 1))
    s.op("vector", lambda v: v.reciprocal(rstd[:, 0:nj], rstd[:, 0:nj]),
         waits=[("ac", s.val("ac"))], inc=("ve", 1))
    s.op("vector", lambda v: v.tensor_mul(negmr[:, 0:nj], mean[:, 0:nj], rstd[:, 0:nj]),
         waits=[("ve", s.val("ve"))], inc=("ve", 1))
    s.op("vector", lambda v: v.tensor_scalar(negmr[:, 0:nj], negmr[:, 0:nj], -1.0, None, ALU.mult),
         waits=[("ve", s.val("ve"))], inc=("ve", 1))


def _mk_ctx():
    nc = bass.Bass("TRN2", target_bir_lowering=False, debug=False)
    s = Sched()
    cache = {}

    def dram(name, shape, dtype, kind="ExternalInput"):
        if name not in cache:
            cache[name] = nc.dram_tensor(name, shape, dtype, kind=kind).ap()
        return cache[name]

    return nc, s, dram


def _realize(nc, s):
    with nc.sbuf_tensor("fin", [1, 1], F32) as fin:
        s.op("gpsimd", lambda e: e.memset(fin[:], 0.0),
             waits=[(k, v) for k, v in s.c.items() if v > 0])
        sems = {}
        with ExitStack() as es2:
            for n in sorted(s.c):
                sems[n] = es2.enter_context(nc.semaphore(n))
            with nc.Block() as block:
                s.realize(block, sems)
    return nc


def build_l0(stop_after="all", ntiles=NT0, dbg_scratch=False):
    nc, s, dram = _mk_ctx()
    emit_l0(nc, s, dram, stop_after, ntiles, dbg_scratch)
    return _realize(nc, s)


def build_moe():
    nc, s, dram = _mk_ctx()
    emit_moe(nc, s, dram)
    return _realize(nc, s)


def build_attn():
    nc, s, dram = _mk_ctx()
    emit_attn(nc, s, dram)
    return _realize(nc, s)


def build_fused():
    nc, s, dram0 = _mk_ctx()
    handoff = {"x1": None, "xn1": None, "w16": None, "moe": None}

    def dram(name, shape, dtype, kind="ExternalInput"):
        if name in handoff:
            kind = "Internal"
        return dram0(name, shape, dtype, kind)

    emit_l0(nc, s, dram, "all", NT0, False)
    s.barrier()
    emit_moe(nc, s, dram)
    s.barrier()
    emit_attn(nc, s, dram)
    return _realize(nc, s)


def emit_l0(nc, s, dram, stop_after="all", ntiles=NT0, dbg_scratch=False):
    ap = dram
    x = ap("x", [S_LEN, DM], F32, kind="ExternalInput")
    wq = ap("wq", [DM, DM], F32, kind="ExternalInput")
    wk = ap("wk", [DM, DM], F32, kind="ExternalInput")
    wkp = ap("wkp", [DM, DM], F32, kind="ExternalInput")
    wv = ap("wv", [DM, DM], F32, kind="ExternalInput")
    wo = ap("wo", [DM, DM], F32, kind="ExternalInput")
    gam = ap("gam", [1, DM], F32, kind="ExternalInput")
    bet = ap("bet", [1, DM], F32, kind="ExternalInput")
    wgT = ap("wgT", [DM, HID0], I32, kind="ExternalInput")
    wuT = ap("wuT", [DM, HID0], I32, kind="ExternalInput")
    wdT = ap("wdT", [HID0, DM], I32, kind="ExternalInput")
    rwT = ap("rwT", [DM, 16], F32, kind="ExternalInput")
    rbias = ap("rbias", [1, 16], F32, kind="ExternalInput")
    idb = ap("idb", [128, 128], BF16, kind="ExternalInput")
    wmask = ap("wmask", [32, DM], F32, kind="ExternalInput")
    idx16 = ap("idx16", [128, 16], F32, kind="ExternalInput")
    x1 = ap("x1", [S_LEN, DM], F32, kind="ExternalOutput")
    xn1 = ap("xn1", [8, 128, S_LEN], BF16, kind="ExternalOutput")
    w16 = ap("w16", [S_LEN, 16], F32, kind="ExternalOutput")
    scratch_kind = "ExternalOutput" if dbg_scratch else "Internal"
    if dbg_scratch:
        dbg_xn = ap("dbg_xn", [128, NJ, DM], F32, kind="ExternalOutput")
        dbg_fm = ap("dbg_fm", [128, 8, TT], BF16, kind="ExternalOutput")
        dbg_st = ap("dbg_st", [6, 128, NJ], F32, kind="ExternalOutput")
    wall = ap("wall", [NHC, 3, 128, 1024], BF16, kind=scratch_kind)

    es = ExitStack()
    S = lambda n, *a: es.enter_context(nc.sbuf_tensor("a_" + n, *a))
    P = lambda n, *a: es.enter_context(nc.psum_tensor("a_" + n, *a))

    xbuf = [S("xbuf0", [128, NJ, DM], F32), S("xbuf1", [128, NJ, DM], F32)]
    xn = S("xn", [128, NJ, DM], F32)
    t1 = S("t1", [128, NJ, DM], F32)
    Cc = S("Cc", [128, NJ, DM], F32)
    xnb = S("xnb", [128, NJ, DM], BF16)
    xnfm = S("xnfm", [128, 8, TT], BF16)
    gslf = S("gslf", [128, TT], F32)
    hidb = [S("hidb0", [128, TT], BF16), S("hidb1", [128, TT], BF16)]
    gamb = S("gamb", [128, DM], F32)
    betb = S("betb", [128, DM], F32)
    rbb = S("rbb", [128, 16], F32)
    idxt = S("idxt", [128, 16], F32)
    idxm = S("idxm", [128, 16], F32)
    onesf = S("onesf", [1, 128], F32)
    idbt = S("idbt", [128, 128], BF16)
    sqscr = S("sqscr", [128, NJ, DM], F32)
    dbg_snap = S("dbg_snap", [128, 6, NJ], F32)
    sqmini = S("sqmini", [32, DM], F32)
    st = {k: S("st_" + k, [128, NJ], F32)
          for k in ("msum", "ss", "mean", "var", "rstd", "negmr")}
    wcb = [S("wcb%d" % i, [128, 3, 1024], BF16) for i in range(4)]
    wgc = [b[:, 0, :].rearrange("p (c h) -> p c h", c=8) for b in wcb]
    wuc = [b[:, 1, :].rearrange("p (c h) -> p c h", c=8) for b in wcb]
    wdc = [b[:, 2, :] for b in wcb]
    wstage = [S("wstage0", [128, 1024], I32), S("wstage1", [128, 1024], I32)]
    wsf = S("wsf", [128, 1024], F32)
    lg_all = S("lg_all", [128, 32, 16], F32)
    r_m1 = S("r_m1", [128, 32], F32)
    r_m2 = S("r_m2", [128, 32], F32)
    r_i = S("r_i", [128, 32], F32)
    r_p = S("r_p", [128, 32], F32)
    r_c = S("r_c", [128, 32, 16], F32)
    r_d = S("r_d", [128, 32, 16], F32)
    rwb = S("rwb", [128, 8, 16], BF16)
    rwf = S("rwf", [128, 8, 16], F32)
    mst = S("mst", [128, DM], BF16)
    wmt = S("wmt", [32, DM], F32)
    xn32 = S("xn32", [32, DM], F32)
    m_q = S("m_q", [32, DM], F32)
    m_k = S("m_k", [32, DM], F32)
    m_kp = S("m_kp", [32, DM], F32)
    m_v = S("m_v", [32, DM], F32)
    m_sc = S("m_sc", [32, DM], F32)
    m_st = S("m_st", [32, 16], F32)
    m_st2 = S("m_st2", [32, 16], F32)
    m_fm = S("m_fm", [128, 8, 32], BF16)
    q_fm = S("q_fm", [128, 8, 32], BF16)
    k_fm = S("k_fm", [128, 8, 32], BF16)
    kp_fm = S("kp_fm", [128, 8, 32], BF16)
    v_fm = S("v_fm", [128, 8, 32], BF16)
    wfm = S("wfm", [128, DM], BF16)
    ptsb = S("ptsb", [128, 8, 32], BF16)
    utsb = S("utsb", [128, 8, 64], BF16)
    z_tm = S("z_tm", [32, DM], BF16)
    zwo = S("zwo", [32, DM], F32)
    wbig = S("wbig", [128, 8, DM], BF16)

    p_tr_full = [P("p_tr0", [128, 1024], BF16), P("p_tr1", [128, 1024], BF16)]
    p_tr = [t[:, 0:128] for t in p_tr_full]
    p_g_full = P("p_g", [128, 512], F32)
    p_g = p_g_full[:, 0:TT]
    p_u_full = P("p_u", [128, 512], F32)
    p_u = p_u_full[:, 0:TT]
    p_d = [P("p_d0", [128, 512], F32), P("p_d1", [128, 512], F32),
           P("p_d2", [128, 512], F32), P("p_d3", [128, 512], F32)]

    with es:
        s.op("sync", lambda e: e.dma_start(out=idbt[:], in_=idb[:]), inc=("ld", 16))
        s.op("sync", lambda e: e.dma_start(out=idxt[:], in_=idx16[:]), inc=("ld", 16))
        s.op("sync", lambda e: e.dma_start(out=wmt[:], in_=wmask[:]), inc=("ld", 16))
        s.op("sync", lambda e: e.dma_start(out=rwf[:], in_=rwT.rearrange("(c p) e -> p c e", p=128)),
             inc=("ld", 16))
        s.op("sync", lambda e: e.dma_start(out=gamb[0:1, :], in_=gam[:]), inc=("ld", 16))
        s.op("sync", lambda e: e.dma_start(out=betb[0:1, :], in_=bet[:]), inc=("ld", 16))
        s.op("sync", lambda e: e.dma_start(out=rbb[0:1, :], in_=rbias[:]), inc=("ld", 16))
        ld_c = s.val("ld")
        if stop_after == "c1":
            return
        s.op("gpsimd", lambda e: e.memset(onesf[:], 1.0), inc=("gp", 1))
        s.op("gpsimd", lambda e: e.memset(Cc[:], 0.0), inc=("gp", 1))
        s.op("gpsimd", lambda e: e.memset(mst[:], 0.0), inc=("gp", 1))
        gp_ms = s.val("gp")
        s.op("vector", lambda v: v.tensor_scalar(idxm[:], idxt[:], 65536.0, None, ALU.subtract),
             waits=[("ld", ld_c)], inc=("ve", 1))
        s.op("vector", lambda v: v.tensor_copy(rwb[:], rwf[:]),
             waits=[("ld", ld_c)], inc=("ve", 1))
        if stop_after == "c2":
            return
        for (tile, psum, pname) in ((gamb, p_d[0], "pd0"), (betb, p_d[1], "pd1")):
            for half in range(2):
                s.op("tensor", lambda t_, tile=tile, psum=psum, half=half: t_.matmul(
                    psum[:], onesf[0:1, :], tile[0:1, half * 512:(half + 1) * 512],
                    start=True, stop=True),
                    waits=[("ld", ld_c), ("gp", gp_ms)] + s.gate(pname + "_free"),
                    inc=("mm", 1))
                s.op("vector", lambda v, tile=tile, psum=psum, half=half: v.tensor_copy(
                    tile[:, half * 512:(half + 1) * 512], psum[:]),
                    waits=[("mm", s.val("mm"))], inc=("ve", 1))
                s.setg(pname + "_free", "ve")
        s.op("tensor", lambda t_: t_.matmul(p_d[2][:, 0:16], onesf[0:1, :], rbb[0:1, 0:16],
                                            start=True, stop=True),
             waits=[("ld", ld_c), ("gp", gp_ms)], inc=("mm", 1))
        s.op("vector", lambda v: v.tensor_copy(rbb[:], p_d[2][:, 0:16]),
             waits=[("mm", s.val("mm"))], inc=("ve", 1))
        s.setg("gb_ready", "ve")
        s.setg("pd0_free", "ve")
        s.setg("pd1_free", "ve")
        s.setg("pd2_free", "ve")

        if stop_after == "cast":
            return
        s.op("sync", lambda e: e.dma_start(out=xn32[:], in_=x[S_LEN - 32:S_LEN, :]),
             inc=("ld", 16))
        ld32 = s.val("ld")
        s.op("vector", lambda v: v.reduce_sum(m_st[:, 0:1], xn32[:], axis=AX.X),
             waits=[("ld", ld32)], inc=("ve", 1))
        s.op("scalar", lambda a: a.activation(sqmini[:], xn32[:], AF.Square,
                                              bias=0.0, scale=1.0),
             waits=[("ld", ld32)], inc=("ac", 1))
        s.op("vector", lambda v: v.reduce_sum(m_st[:, 1:2], sqmini[:], axis=AX.X),
             waits=[("ac", s.val("ac"))], inc=("ve", 1))
        w0 = [("ve", s.val("ve")), ("ac", s.val("ac"))]
        s.op("vector", lambda v: v.tensor_scalar(m_st[:, 2:3], m_st[:, 0:1], 1.0 / DM, None, ALU.mult),
             waits=w0, inc=("ve", 1))
        s.op("vector", lambda v: v.tensor_scalar(m_st[:, 3:4], m_st[:, 1:2], 1.0 / DM, None, ALU.mult),
             waits=[("ve", s.val("ve"))], inc=("ve", 1))
        s.op("vector", lambda v: v.tensor_mul(m_st[:, 4:5], m_st[:, 2:3], m_st[:, 2:3]),
             waits=[("ve", s.val("ve"))], inc=("ve", 1))
        s.op("vector", lambda v: v.tensor_sub(m_st[:, 3:4], m_st[:, 3:4], m_st[:, 4:5]),
             waits=[("ve", s.val("ve"))], inc=("ve", 1))
        s.op("vector", lambda v: v.tensor_scalar(m_st[:, 3:4], m_st[:, 3:4], EPS_LN, None, ALU.add),
             waits=[("ve", s.val("ve"))], inc=("ve", 1))
        s.op("scalar", lambda a: a.activation(m_st[:, 5:6], m_st[:, 3:4], AF.Sqrt, bias=0.0, scale=1.0),
             waits=[("ve", s.val("ve"))], inc=("ac", 1))
        s.op("vector", lambda v: v.reciprocal(m_st[:, 5:6], m_st[:, 5:6]),
             waits=[("ac", s.val("ac"))], inc=("ve", 1))
        s.op("vector", lambda v: v.tensor_mul(m_st[:, 6:7], m_st[:, 2:3], m_st[:, 5:6]),
             waits=[("ve", s.val("ve"))], inc=("ve", 1))
        s.op("vector", lambda v: v.tensor_scalar(m_st[:, 6:7], m_st[:, 6:7], -1.0, None, ALU.mult),
             waits=[("ve", s.val("ve"))], inc=("ve", 1))
        s.op("scalar", lambda a: a.activation(xn32[:], xn32[:], AF.Identity,
                                              bias=m_st[:, 6:7], scale=m_st[:, 5:6]),
             waits=[("ve", s.val("ve"))], inc=("ac", 1))
        ac_xn32 = s.val("ac")

        def mini_tr(src_f32_or_bf16, rows, out_fm, nchunk=8, outrows=None):
            outrows = outrows or rows
            s.op("vector", lambda v: v.tensor_copy(mst[0:rows, :], src_f32_or_bf16),
                 waits=s.gate("mst_free"), inc=("ve", 1))
            ve_stg = s.val("ve")
            for dc in range(nchunk):
                pt = p_tr[dc % 2]
                s.op("tensor", lambda t_, dc=dc, pt=pt: t_.transpose(
                    pt[:], mst[:, dc * 128:(dc + 1) * 128], idbt[:]),
                    waits=[("ve", ve_stg)] + s.gate("ptr%d_free" % (dc % 2)) + [("ld", ld_c)],
                    inc=("mm", 1))
                s.op("vector", lambda v, dc=dc, pt=pt: v.tensor_copy(
                    out_fm[:, dc, 0:outrows], pt[:, 0:outrows]),
                    waits=[("mm", s.val("mm"))], inc=("ve", 1))
                s.setg("ptr%d_free" % (dc % 2), "ve")
            s.setg("mst_free", "ve")

        s.op("vector", lambda v: v.tensor_copy(mst[0:32, :], xn32[:]),
             waits=[("ac", ac_xn32), ("gp", gp_ms)], inc=("ve", 1))
        ve_stg = s.val("ve")
        for dc in range(8):
            pt = p_tr[dc % 2]
            s.op("tensor", lambda t_, dc=dc, pt=pt: t_.transpose(
                pt[:], mst[:, dc * 128:(dc + 1) * 128], idbt[:]),
                waits=[("ve", ve_stg)] + s.gate("ptr%d_free" % (dc % 2)) + [("ld", ld_c)],
                inc=("mm", 1))
            s.op("vector", lambda v, dc=dc, pt=pt: v.tensor_copy(m_fm[:, dc, :], pt[:, 0:32]),
                 waits=[("mm", s.val("mm"))], inc=("ve", 1))
            s.setg("ptr%d_free" % (dc % 2), "ve")
        s.setg("mst_free", "ve")

        for (whbm, out_tm, out_fm, do_norm) in ((wq, m_q, q_fm, True), (wk, m_k, k_fm, True),
                                                (wkp, m_kp, kp_fm, True), (wv, m_v, v_fm, False)):
            for dc in range(8):
                s.op("sync", lambda e, whbm=whbm, dc=dc: e.dma_start(
                    out=wsf[:], in_=whbm.rearrange("(c p) f -> p c f", p=128)[:, dc, :]),
                    waits=s.gate("wsf_free"), inc=("ld", 16))
                s.op("vector", lambda v, dc=dc: v.tensor_copy(wbig[:, dc, :], wsf[:]),
                     waits=[("ld", s.val("ld"))] + s.gate("wbig_free"), inc=("ve", 1))
                s.setg("wsf_free", "ve")
            ve_w = s.val("ve")
            for half in range(2):
                pp = p_d[2 + half]
                for dc in range(8):
                    s.op("tensor", lambda t_, dc=dc, half=half, pp=pp: t_.matmul(
                        pp[0:32, :], m_fm[:, dc, :], wbig[:, dc, half * 512:(half + 1) * 512],
                        start=(dc == 0), stop=(dc == 7)),
                        waits=[("ve", ve_w)] + s.gate("pd%d_free" % (2 + half)),
                        inc=("mm", 1) if dc == 7 else None)
                s.op("vector", lambda v, half=half, pp=pp, out_tm=out_tm: v.tensor_copy(
                    out_tm[:, half * 512:(half + 1) * 512], pp[0:32, :]),
                    waits=[("mm", s.val("mm"))], inc=("ve", 1))
                s.setg("pd%d_free" % (2 + half), "ve")
            s.setg("wbig_free", "ve")
            ve_proj = s.val("ve")
            if do_norm:
                s.op("scalar", lambda a, out_tm=out_tm: a.activation(
                    sqmini[:], out_tm[:], AF.Square, bias=0.0, scale=1.0),
                    waits=[("ve", ve_proj)], inc=("ac", 1))
                s.op("vector", lambda v: v.reduce_sum(
                    m_st[:, 0:16], sqmini[:].rearrange("q (h d) -> q h d", h=NH), axis=AX.X),
                    waits=[("ac", s.val("ac"))], inc=("ve", 1))
                s.op("scalar", lambda a: a.activation(m_st2[:], m_st[:], AF.Sqrt, bias=0.0, scale=1.0),
                     waits=[("ve", s.val("ve"))], inc=("ac", 1))
                s.op("vector", lambda v: v.tensor_scalar(m_st2[:], m_st2[:], EPS_NORM, None, ALU.add),
                     waits=[("ac", s.val("ac"))], inc=("ve", 1))
                s.op("vector", lambda v: v.reciprocal(m_st2[:], m_st2[:]),
                     waits=[("ve", s.val("ve"))], inc=("ve", 1))
                ve_rn = s.val("ve")
                for h in range(NH):
                    s.op("vector", lambda v, h=h, out_tm=out_tm: v.tensor_scalar(
                        out_tm[:, h * 64:(h + 1) * 64], out_tm[:, h * 64:(h + 1) * 64],
                        m_st2[:, h:h + 1], None, ALU.mult),
                        waits=[("ve", ve_rn)],
                        inc=("ve", 1) if h == NH - 1 else None)
            mini_tr(out_tm[:], 32, out_fm)

        for h in range(NH):
            cc, ro = h // 2, 64 * (h % 2)
            qf = q_fm[:, cc, :][ro:ro + 64, :]
            kf = k_fm[:, cc, :][ro:ro + 64, :]
            kpf = kp_fm[:, cc, :][ro:ro + 64, :]
            vf = v_fm[:, cc, :][ro:ro + 64, :]
            kv = kf.rearrange("d (j u) -> d j u", u=1).broadcast_to([64, 32, 32])
            kpv = kpf.rearrange("d (u k) -> d u k", u=1).broadcast_to([64, 32, 32])
            wfm_h = wfm[ro:ro + 64, :]
            s.op("vector", lambda v, kv=kv, kpv=kpv, wfm_h=wfm_h: v.tensor_tensor(
                wfm_h.rearrange("d (j k) -> d j k", k=32), kv, kpv, ALU.mult),
                waits=s.gate("wfm_free"), inc=("ve", 1))
            ve_wfm = s.val("ve")
            for half in range(2):
                pp = p_d[half]
                s.op("tensor", lambda t_, half=half, pp=pp, qf=qf, wfm_h=wfm_h: t_.matmul(
                    pp[0:32, :], qf, wfm_h[:, half * 512:(half + 1) * 512],
                    start=True, stop=True),
                    waits=[("ve", ve_wfm)] + s.gate("pd%d_free" % half),
                    inc=("mm", 1))
                s.op("vector", lambda v, half=half, pp=pp: v.scalar_tensor_tensor(
                    m_sc[:, half * 512:(half + 1) * 512], pp[0:32, :], 0.125,
                    wmt[:, half * 512:(half + 1) * 512], ALU.mult, ALU.add),
                    waits=[("mm", s.val("mm")), ("ld", ld_c)], inc=("ve", 1))
                s.setg("pd%d_free" % half, "ve")
            s.op("vector", lambda v: v.reduce_max(m_st[:, 0:1], m_sc[:], axis=AX.X),
                 inc=("ve", 1))
            s.op("vector", lambda v: v.tensor_scalar(m_st[:, 1:2], m_st[:, 0:1], -1.0, None, ALU.mult),
                 waits=[("ve", s.val("ve"))], inc=("ve", 1))
            s.op("scalar", lambda a: a.activation(m_sc[:], m_sc[:], AF.Exp,
                                                  bias=m_st[:, 1:2], scale=1.0),
                 waits=[("ve", s.val("ve"))], inc=("ac", 1))
            s.op("vector", lambda v: v.reduce_sum(m_st[:, 2:3], m_sc[:], axis=AX.X),
                 waits=[("ac", s.val("ac"))], inc=("ve", 1))
            s.op("vector", lambda v: v.reciprocal(m_st[:, 3:4], m_st[:, 2:3]),
                 waits=[("ve", s.val("ve"))], inc=("ve", 1))
            ve_recip = s.val("ve")
            mini_tr(m_sc[:], 32, ptsb)
            vv1 = vf.rearrange("d (j u) -> d j u", u=1).broadcast_to([64, 32, 32])
            vv2 = vf.rearrange("d (u k) -> d u k", u=1).broadcast_to([64, 32, 32])
            s.op("vector", lambda v, vv1=vv1, vv2=vv2, wfm_h=wfm_h: v.tensor_tensor(
                wfm_h.rearrange("d (j k) -> d j k", k=32), vv1, vv2, ALU.mult),
                inc=("ve", 1))
            mini_tr(wfm_h, 64, utsb)
            s.setg("wfm_free", "ve")
            pz = p_d[2]
            for dc in range(8):
                s.op("tensor", lambda t_, dc=dc, pz=pz: t_.matmul(
                    pz[0:32, 0:64], ptsb[:, dc, :], utsb[:, dc, :],
                    start=(dc == 0), stop=(dc == 7)),
                    waits=[("ve", s.val("ve"))] + s.gate("pd2_free"),
                    inc=("mm", 1) if dc == 7 else None)
            s.op("vector", lambda v, h=h, pz=pz: v.tensor_scalar(
                z_tm[:, h * 64:(h + 1) * 64], pz[0:32, 0:64], m_st[:, 3:4], None, ALU.mult),
                waits=[("mm", s.val("mm")), ("ve", ve_recip)], inc=("ve", 1))
            s.setg("pd2_free", "ve")
        for dc in range(8):
            s.op("sync", lambda e, dc=dc: e.dma_start(
                out=wsf[:], in_=wo.rearrange("(c p) f -> p c f", p=128)[:, dc, :]),
                waits=s.gate("wsf_free"), inc=("ld", 16))
            s.op("vector", lambda v, dc=dc: v.tensor_copy(wbig[:, dc, :], wsf[:]),
                 waits=[("ld", s.val("ld"))] + s.gate("wbig_free"), inc=("ve", 1))
            s.setg("wsf_free", "ve")
        ve_wo = s.val("ve")
        mini_tr(z_tm[:], 32, m_fm)
        ve_zfm = s.val("ve")
        for half in range(2):
            pp = p_d[half]
            for dc in range(8):
                s.op("tensor", lambda t_, dc=dc, half=half, pp=pp: t_.matmul(
                    pp[0:32, :], m_fm[:, dc, :], wbig[:, dc, half * 512:(half + 1) * 512],
                    start=(dc == 0), stop=(dc == 7)),
                    waits=[("ve", ve_wo), ("ve", ve_zfm)] + s.gate("pd%d_free" % half),
                    inc=("mm", 1) if dc == 7 else None)
            s.op("vector", lambda v, half=half, pp=pp: v.tensor_copy(
                zwo[:, half * 512:(half + 1) * 512], pp[0:32, :]),
                waits=[("mm", s.val("mm"))], inc=("ve", 1))
            s.setg("pd%d_free" % half, "ve")
        s.op("gpsimd", lambda e: e.dma_start(out=Cc[96:128, NJ - 1, :], in_=zwo[:]),
             waits=[("ve", s.val("ve"))], inc=("cst", 16))
        cc_done = s.val("cst")

        if stop_after == "mini":
            return
        for t in range(ntiles):
            xb = xbuf[t % 2]
            s.op("sync", lambda e, t=t, xb=xb: e.dma_start(
                out=xb[:], in_=x.rearrange("(t j p) d -> t p j d", j=NJ, p=128)[t]),
                waits=s.gate("xb%d_free" % (t % 2)), inc=("ld", 16))
            ld_x = s.val("ld")
            inw = [("ld", ld_x)]
            _ln_stats(s, xb[:], NJ, st, sqscr, inw)
            ve_rs = s.val("ve")
            if dbg_scratch and t == 0:
                snap = locals().get("dbg_snap")
                for i, k in enumerate(("msum", "ss", "mean", "var", "rstd", "negmr")):
                    s.op("vector", lambda v, i=i, k=k: v.tensor_copy(
                        dbg_snap[:, i, :], st[k][:]),
                        waits=[("ve", ve_rs), ("ac", s.val("ac"))],
                        inc=("ve", 1) if i == 5 else None)
                s.op("gpsimd", lambda e: e.dma_start(
                    out=dbg_st.rearrange("i p j -> p i j"), in_=dbg_snap[:]),
                    waits=[("ve", s.val("ve"))], inc=("so", 16))
            for j in range(NJ):
                s.op("scalar", lambda a, j=j, xb=xb: a.activation(
                    xn[:, j, :], xb[:, j, :], AF.Identity,
                    bias=st["negmr"][:, j:j + 1], scale=st["rstd"][:, j:j + 1]),
                    waits=[("ve", ve_rs)] + s.gate("xn_free"), inc=("ac", 1))
            ac_xn = s.val("ac")
            s.op("vector", lambda v: v.tensor_copy(xnb[:], xn[:]),
                 waits=[("ac", ac_xn)], inc=("ve", 1))
            ve_xnb = s.val("ve")
            for jj in range(NJ):
                for dc in range(8):
                    pt = p_tr[(jj * 8 + dc) % 2]
                    s.op("tensor", lambda t_, jj=jj, dc=dc, pt=pt: t_.transpose(
                        pt[:], xnb[:, jj, dc * 128:(dc + 1) * 128], idbt[:]),
                        waits=[("ve", ve_xnb)] + s.gate("ptr%d_free" % ((jj * 8 + dc) % 2)),
                        inc=("mm", 1))
                    s.op("vector", lambda v, jj=jj, dc=dc, pt=pt: v.tensor_copy(
                        xnfm[:, dc, jj * 128:(jj + 1) * 128], pt[:]),
                        waits=[("mm", s.val("mm"))], inc=("ve", 1))
                    s.setg("ptr%d_free" % ((jj * 8 + dc) % 2), "ve")
            ve_fm = s.val("ve")
            if dbg_scratch and t == 0:
                s.op("gpsimd", lambda e: e.dma_start(out=dbg_xn[:], in_=xn[:]),
                     waits=[("ve", ve_fm)], inc=("so", 16))
                s.op("gpsimd", lambda e: e.dma_start(out=dbg_fm[:], in_=xnfm[:]),
                     waits=[("ve", ve_fm)], inc=("so", 16))
            def emit_down_l0(pend):
                hcp, ve_hidp = pend
                wbp = hcp % 4
                hbp = hcp % 2
                for mt in range(NJ):
                    for half in range(2):
                        pi = mt * 2 + half
                        last = (mt == NJ - 1 and half == 1)
                        s.op("tensor", lambda t_, mt=mt, half=half, pi=pi, hcp=hcp, wbp=wbp, hbp=hbp: t_.matmul(
                            p_d[pi][:], hidb[hbp][:, mt * 128:(mt + 1) * 128],
                            wdc[wbp][:, half * 512:(half + 1) * 512],
                            start=(hcp == 0), stop=(hcp == NHC - 1)),
                            waits=([("ve", ve_hidp)] +
                                   (s.gate("pd0_free", "pd1_free", "pd2_free", "pd3_free")
                                    if hcp == 0 else [])),
                            inc=("mm", 1) if last else None)
                s.setg("hid%d_free" % hbp, "mm")
                s.setg("wbuf%d_free" % wbp, "mm")

            pend_down = None
            for hc in range(NHC):
                wb = hc % 4
                if t == 0:
                    for wi, wsrcT in enumerate((wgT, wuT, wdT)):
                        sj = (hc * 3 + wi) % 2
                        if wi < 2:
                            src = wsrcT.rearrange("(c p) h -> p c h", p=128)[:, :, hc * 128:(hc + 1) * 128]
                            dst = wstage[sj][:].rearrange("p (c h) -> p c h", c=8)
                        else:
                            src = wsrcT[hc * 128:(hc + 1) * 128, :]
                            dst = wstage[sj][:]
                        s.op("sync", lambda e, src=src, dst=dst: e.dma_start(out=dst, in_=src),
                             waits=s.gate("wstage%d_free" % sj), inc=("mld", 16))
                        cdst = wcb[wb][:, wi, :]
                        s.op("vector", lambda v, cdst=cdst, sj=sj: v.tensor_copy(cdst, wstage[sj][:]),
                             waits=[("mld", s.val("mld")), ("wst", s.val("wst"))] +
                                   (s.gate("wbuf%d_free" % wb) if wi == 0 else []),
                             inc=("ve", 1))
                        s.setg("wstage%d_free" % sj, "ve")
                        s.op("gpsimd", lambda e, cdst=cdst, wi=wi, hc=hc: e.dma_start(
                            out=wall[hc, wi], in_=cdst),
                            waits=[("ve", s.val("ve"))], inc=("wst", 16))
                    ve_wcast = s.val("ve")
                    mm_waits = [("ve", ve_wcast), ("ve", ve_fm)]
                else:
                    wait_w = [("wst", s.val("wst"))] if t == 1 and hc < 2 else []
                    s.op("sync", lambda e, hc=hc, wb=wb: e.dma_start(
                        out=wcb[wb][:], in_=wall[hc].rearrange("w p f -> p w f")),
                        waits=wait_w + s.gate("wbuf%d_free" % wb), inc=("mld", 16))
                    mm_waits = [("mld", s.val("mld")), ("ve", ve_fm)]
                for dc in range(8):
                    s.op("tensor", lambda t_, dc=dc, wb=wb: t_.matmul(
                        p_g[:], wgc[wb][:, dc, :], xnfm[:, dc, :],
                        start=(dc == 0), stop=(dc == 7)),
                        waits=mm_waits + s.gate("pg_free"),
                        inc=("mm", 1) if dc == 7 else None)
                mm_g = s.val("mm")
                for dc in range(8):
                    s.op("tensor", lambda t_, dc=dc, wb=wb: t_.matmul(
                        p_u[:], wuc[wb][:, dc, :], xnfm[:, dc, :],
                        start=(dc == 0), stop=(dc == 7)),
                        waits=s.gate("pu_free"),
                        inc=("mm", 1) if dc == 7 else None)
                mm_u = s.val("mm")
                s.op("scalar", lambda a: a.activation(gslf[:], p_g[:], AF.Silu,
                                                      bias=0.0, scale=SCL),
                     waits=[("mm", mm_g)] + s.gate("gslf_free"), inc=("ac", 1))
                s.setg("pg_free", "ac")
                s.op("vector", lambda v, hb=hc % 2: v.tensor_tensor(hidb[hb][:], gslf[:], p_u[:], ALU.mult),
                     waits=[("ac", s.val("ac")), ("mm", mm_u)] + s.gate("hid%d_free" % (hc % 2)),
                     inc=("ve", 1))
                s.setg("pu_free", "ve")
                s.setg("gslf_free", "ve")
                ve_hid = s.val("ve")
                if pend_down is not None:
                    emit_down_l0(pend_down)
                pend_down = (hc, ve_hid)
            emit_down_l0(pend_down)
            mm_down = s.val("mm")
            if t == NT0 - 1:
                s.op("vector", lambda v: v.tensor_add(xn[:], xn[:], Cc[:]),
                     waits=[("cst", cc_done), ("ve", ve_fm)], inc=("ve", 1))
            _ln_stats(s, xn[:], NJ, st, sqscr,
                      [("ve", s.val("ve")), ("ac", s.val("ac")), ("mm", mm_g)])
            ve_rs2 = s.val("ve")
            for j in range(NJ):
                s.op("scalar", lambda a, j=j: a.activation(
                    t1[:, j, :], xn[:, j, :], AF.Identity,
                    bias=st["negmr"][:, j:j + 1], scale=st["rstd"][:, j:j + 1]),
                    waits=[("ve", ve_rs2)], inc=("ac", 1))
            ac_t1 = s.val("ac")
            s.setg("xn_free", "ac")
            gv = _bc(gamb[:], NJ)
            bv = _bc(betb[:], NJ)
            s.op("vector", lambda v, gv=gv: v.tensor_tensor(t1[:], t1[:], gv, ALU.mult),
                 waits=[("ac", ac_t1)] + s.gate("gb_ready"))
            s.op("vector", lambda v, bv=bv: v.tensor_tensor(t1[:], t1[:], bv, ALU.add))
            s.op("vector", lambda v, xb=xb: v.tensor_add(xb[:], xb[:], t1[:]), inc=("ve", 1))
            ve_s = s.val("ve")
            for mt in range(NJ):
                for half in range(2):
                    pi = mt * 2 + half
                    s.op("vector", lambda v, mt=mt, half=half, pi=pi, xb=xb:
                         v.scalar_tensor_tensor(
                             xb[:, mt, half * 512:(half + 1) * 512], p_d[pi][:],
                             SCL * SCL, xb[:, mt, half * 512:(half + 1) * 512],
                             ALU.mult, ALU.add),
                         waits=[("ve", ve_s), ("mm", mm_down)],
                         inc=("ve", 1) if pi == 3 else None)
            for pi in range(4):
                s.setg("pd%d_free" % pi, "ve")
            ve_x1 = s.val("ve")
            s.op("gpsimd", lambda e, t=t, xb=xb: e.dma_start(
                out=x1.rearrange("(t j p) d -> t p j d", j=NJ, p=128)[t], in_=xb[:]),
                waits=[("ve", ve_x1)], inc=("so", 16))
            _ln_stats(s, xb[:], NJ, st, sqscr, [("ve", ve_x1), ("ac", s.val("ac"))])
            ve_rs3 = s.val("ve")
            for j in range(NJ):
                s.op("scalar", lambda a, j=j, xb=xb: a.activation(
                    xnb[:, j, :], xb[:, j, :], AF.Identity,
                    bias=st["negmr"][:, j:j + 1], scale=st["rstd"][:, j:j + 1]),
                    waits=[("ve", ve_rs3)], inc=("ac", 1))
            ac_xn1 = s.val("ac")
            s.setg("xb%d_free" % (t % 2), "ac")
            for jj in range(NJ):
                for dc in range(8):
                    pt = p_tr[(jj * 8 + dc) % 2]
                    s.op("tensor", lambda t_, jj=jj, dc=dc, pt=pt: t_.transpose(
                        pt[:], xnb[:, jj, dc * 128:(dc + 1) * 128], idbt[:]),
                        waits=[("ac", ac_xn1)] + s.gate("ptr%d_free" % ((jj * 8 + dc) % 2)),
                        inc=("mm", 1))
                    s.op("vector", lambda v, jj=jj, dc=dc, pt=pt: v.tensor_copy(
                        xnfm[:, dc, jj * 128:(jj + 1) * 128], pt[:]),
                        waits=[("mm", s.val("mm"))], inc=("ve", 1))
                    s.setg("ptr%d_free" % ((jj * 8 + dc) % 2), "ve")
            ve_fm1 = s.val("ve")
            s.op("gpsimd", lambda e, t=t: e.dma_start(
                out=xn1[:, :, t * TT:(t + 1) * TT].rearrange("c p u -> p c u"), in_=xnfm[:]),
                waits=[("ve", ve_fm1)], inc=("so", 16))
            for mt in range(NJ):
                mtg = t * NJ + mt
                for dc in range(8):
                    s.op("tensor", lambda t_, mt=mt, dc=dc: t_.matmul(
                        p_g[:, 0:16], xnfm[:, dc, mt * 128:(mt + 1) * 128], rwb[:, dc, :],
                        start=(dc == 0), stop=(dc == 7)),
                        waits=[("ve", ve_fm1)] + s.gate("pg_free"),
                        inc=("mm", 1) if dc == 7 else None)
                s.op("vector", lambda v, mtg=mtg: v.tensor_tensor(
                    lg_all[:, mtg, :], p_g[:, 0:16], rbb[:], ALU.add),
                    waits=[("mm", s.val("mm"))], inc=("ve", 1))
                s.setg("pg_free", "ve")
            s.g["xb%d_free" % (t % 2)] = ("so", s.val("so"))
            s.g["xb%d_acfree" % (t % 2)] = ("ac", ac_xn1)


        if stop_after == "tiles":
            return
        ve_lg = s.val("ve")
        iv = _bc(idxt[:], 32)
        imv = _bc(idxm[:], 32)
        bc32 = lambda m: m[:].rearrange("p (m u) -> p m u", u=1).broadcast_to([128, 32, 16])
        def vg(emit, extra=()):
            s.op("vector", emit, waits=[("ve", s.val("ve"))] + list(extra), inc=("ve", 1))
        vg(lambda v: v.reduce_max(r_m1[:], lg_all[:], axis=AX.X), [("ve", ve_lg)])
        vg(lambda v: v.tensor_tensor(r_c[:], lg_all[:], bc32(r_m1), ALU.is_equal))
        vg(lambda v: v.tensor_tensor(r_c[:], r_c[:], imv, ALU.mult))
        vg(lambda v: v.tensor_scalar(r_c[:], r_c[:], 65536.0, None, ALU.add))
        vg(lambda v: v.tensor_reduce(r_i[:], r_c[:], axis=AX.X, op=ALU.min))
        vg(lambda v: v.tensor_tensor(r_c[:], iv, bc32(r_i), ALU.is_equal))
        vg(lambda v: v.scalar_tensor_tensor(r_d[:], r_c[:], -1e30, lg_all[:],
                                            ALU.mult, ALU.add))
        vg(lambda v: v.reduce_max(r_m2[:], r_d[:], axis=AX.X))
        vg(lambda v: v.tensor_tensor(r_d[:], r_d[:], bc32(r_m2), ALU.is_equal))
        vg(lambda v: v.tensor_tensor(r_d[:], r_d[:], imv, ALU.mult))
        vg(lambda v: v.tensor_scalar(r_d[:], r_d[:], 65536.0, None, ALU.add))
        vg(lambda v: v.tensor_reduce(r_i[:], r_d[:], axis=AX.X, op=ALU.min))
        vg(lambda v: v.tensor_tensor(r_d[:], iv, bc32(r_i), ALU.is_equal))
        vg(lambda v: v.tensor_sub(r_p[:], r_m1[:], r_m2[:]))
        s.op("scalar", lambda a: a.activation(r_p[:], r_p[:], AF.Sigmoid, bias=0.0, scale=1.0),
             waits=[("ve", s.val("ve"))], inc=("ac", 1))
        vg(lambda v: v.tensor_tensor(r_c[:], r_c[:], bc32(r_p), ALU.mult),
           [("ac", s.val("ac"))])
        vg(lambda v: v.tensor_scalar(r_p[:], r_p[:], -1.0, 1.0, ALU.mult, ALU.add))
        vg(lambda v: v.tensor_tensor(r_d[:], r_d[:], bc32(r_p), ALU.mult))
        vg(lambda v: v.tensor_add(r_c[:], r_c[:], r_d[:]))
        s.op("gpsimd", lambda e: e.dma_start(
            out=w16.rearrange("(m p) e -> p m e", p=128), in_=r_c[:]),
            waits=[("ve", s.val("ve"))], inc=("so", 16))

        return




def emit_moe(nc, s, dram):
    ap = dram
    xn1 = ap("xn1", [8, 128, S_LEN], BF16, kind="ExternalInput")
    w16 = ap("w16", [S_LEN, 16], F32, kind="ExternalInput")
    eup = ap("eup", [NEXP, DM, EDIM], F32, kind="ExternalInput")
    egt = ap("egt", [NEXP, DM, EDIM], F32, kind="ExternalInput")
    edn = ap("edn", [NEXP, EDIM, DM], F32, kind="ExternalInput")
    shgT = ap("shgT", [DM, DM], F32, kind="ExternalInput")
    shuT = ap("shuT", [DM, DM], F32, kind="ExternalInput")
    shdT = ap("shdT", [DM, DM], F32, kind="ExternalInput")
    moe = ap("moe", [S_LEN, DM], F32, kind="ExternalOutput")
    shb = [ap("shb%d" % i, [8, 128, 1024], BF16, kind="Internal") for i in range(3)]

    es = ExitStack()
    S = lambda n, *a: es.enter_context(nc.sbuf_tensor("b_" + n, *a))
    P = lambda n, *a: es.enter_context(nc.psum_tensor("b_" + n, *a))

    HTOK = S_LEN // 2
    NJM = HTOK // 128
    xh = S("xh", [128, 8, HTOK], BF16)
    acc = S("acc", [128, NJM, DM], F32)
    w16h = S("w16h", [128, NJM, 16], F32)
    gateb = [S("gateb%d" % i, [128, 8, EDIM], BF16) for i in range(2)]
    upb = [S("upb%d" % i, [128, 8, EDIM], BF16) for i in range(2)]
    dnb = [S("dnb%d" % i, [128, 4, DM], BF16) for i in range(2)]
    wstgB = [S("wstgB%d" % i, [128, 2048], F32) for i in range(2)]
    hid = [S("hid0", [128, 4, 512], BF16), S("hid1", [128, 4, 512], BF16)]
    scr = S("scr", [128, 512], F32)
    shw = [S("shw%d_%d" % (i, b), [128, 8, 128], BF16) for i in range(2) for b in range(2)]
    shd = [S("shd%d" % b, [128, 1024], BF16) for b in range(2)]
    shstage = S("shstage", [128, 1024], F32)
    shstg_b = S("shstg_b", [128, 1024], BF16)
    hsh = [S("hsha", [128, 256], BF16), S("hshb", [128, 256], BF16)]
    scrsh = S("scrsh", [128, 256], F32)
    p_g = P("p_g", [128, 512], F32)
    p_u = P("p_u", [128, 512], F32)
    p_d = [P("p_d0m", [128, 512], F32), P("p_d1m", [128, 512], F32),
           P("p_d2m", [128, 512], F32), P("p_d3m", [128, 512], F32)]

    with es:
        for wi, wsrc in enumerate((shgT, shuT, shdT)):
            for hc in range(8):
                if wi < 2:
                    src_ap = wsrc.rearrange("(c p) h -> p c h", p=128)[:, :, hc * 128:(hc + 1) * 128]
                    dst_ap = shstage[:].rearrange("p (c h) -> p c h", c=8)
                else:
                    src_ap = wsrc[hc * 128:(hc + 1) * 128, :]
                    dst_ap = shstage[:]
                s.op("sync", lambda e, src_ap=src_ap, dst_ap=dst_ap: e.dma_start(
                    out=dst_ap, in_=src_ap),
                    waits=s.gate("shstage_free"), inc=("ld", 16))
                s.op("vector", lambda v: v.tensor_copy(shstg_b[:], shstage[:]),
                     waits=[("ld", s.val("ld"))] + s.gate("shstgb_free"), inc=("ve", 1))
                s.setg("shstage_free", "ve")
                s.op("gpsimd", lambda e, wi=wi, hc=hc: e.dma_start(
                    out=shb[wi][hc], in_=shstg_b[:]),
                    waits=[("ve", s.val("ve"))], inc=("wst", 16))
                s.setg("shstgb_free", "wst")
        wst_cast = s.val("wst")

        pend_moe = [None]

        def emit_down_moe(pend):
            kp, ttp, ve_hidp, wbp, ld_xp, gp_accp = pend
            hb = hid[ttp % 2]
            for mt in range(4):
                jm = ttp * 4 + mt
                for dh in range(2):
                    pb = p_d[(mt * 2 + dh) % 2]
                    for ec in range(4):
                        s.op("tensor", lambda t_, ec=ec, mt=mt, dh=dh, pb=pb, wbp=wbp, hb=hb: t_.matmul(
                            pb[:], hb[:, ec, mt * 128:(mt + 1) * 128],
                            dnb[wbp][:, ec, dh * 512:(dh + 1) * 512],
                            start=(ec == 0), stop=(ec == 3)),
                            waits=[("ve", ve_hidp)] + s.gate("pd%d_free" % ((mt * 2 + dh) % 2)),
                            inc=("mm", 1) if ec == 3 else None)
                    s.op("vector", lambda v, jm=jm, dh=dh, pb=pb, kp=kp: v.scalar_tensor_tensor(
                        acc[:, jm, dh * 512:(dh + 1) * 512], pb[:],
                        w16h[:, jm, kp:kp + 1],
                        acc[:, jm, dh * 512:(dh + 1) * 512], ALU.mult, ALU.add),
                        waits=[("mm", s.val("mm")), ("ld", ld_xp), ("gp", gp_accp)],
                        inc=("ve", 1))
                    s.setg("pd%d_free" % ((mt * 2 + dh) % 2), "ve")
            s.setg("mhid%d_free" % (ttp % 2), "mm")
            if ttp == 3:
                s.setg("dnb%d_free" % wbp, "mm")

        wld_map = {}

        def emit_wload_moe(kk, part):
            wbk = kk % 2
            chunks = [("g", 0), ("g", 1), ("u", 0), ("u", 1), ("d", 0), ("d", 1)]
            sel = chunks[:3] if part == 0 else chunks[3:]
            for jj, (kind, j) in enumerate(sel):
                gi = part * 3 + jj
                sb = gi % 2
                if kind == "g" or kind == "u":
                    wsrc = egt if kind == "g" else eup
                    wdst = gateb if kind == "g" else upb
                    src_ap = wsrc[kk, j * 512:(j + 1) * 512, :].rearrange("(c p) e -> p c e", p=128)
                    stv = wstgB[sb][:].rearrange("p (c e) -> p c e", c=4)
                    dstv = wdst[wbk][:, j * 4:(j + 1) * 4, :]
                    first = (kind == "g" and j == 0) or (kind == "u" and j == 0)
                    gname = ("gateb%d_free" % wbk) if kind == "g" else ("upb%d_free" % wbk)
                else:
                    src_ap = edn[kk, j * 256:(j + 1) * 256, :].rearrange("(c p) d -> p c d", p=128)
                    stv = wstgB[sb][:].rearrange("p (c d) -> p c d", c=2)
                    dstv = dnb[wbk][:, j * 2:(j + 1) * 2, :]
                    first = (j == 0)
                    gname = "dnb%d_free" % wbk
                s.op("sync", lambda e, src_ap=src_ap, stv=stv: e.dma_start(out=stv, in_=src_ap),
                     waits=s.gate("wstgB%d_free" % sb), inc=("ld", 16))
                s.op("vector", lambda v, dstv=dstv, stv=stv: v.tensor_copy(dstv, stv),
                     waits=[("ld", s.val("ld"))] + (s.gate(gname) if first and j == 0 else []),
                     inc=("ve", 1))
                s.setg("wstgB%d_free" % sb, "ve")
            if part == 1:
                wld_map[kk] = [("ld", s.val("ld"))]

        for h in range(2):
            s.op("sync", lambda e, h=h: e.dma_start(
                out=xh[:], in_=xn1[:, :, h * HTOK:(h + 1) * HTOK].rearrange("c p t -> p c t")),
                waits=s.gate("xh_free"), inc=("ld", 16))
            s.op("sync", lambda e, h=h: e.dma_start(
                out=w16h[:], in_=w16.rearrange("(m p) e -> p m e", p=128)[:, h * NJM:(h + 1) * NJM, :]),
                inc=("ld", 16))
            ld_x = s.val("ld")
            s.op("gpsimd", lambda e: e.memset(acc[:], 0.0),
                 waits=s.gate("acc_free"), inc=("gp", 1))
            gp_acc = s.val("gp")
            emit_wload_moe(0, 0)
            emit_wload_moe(0, 1)
            for k in range(NEXP):
                wb = k % 2
                wld = wld_map[k]
                for tt in range(4):
                    t0 = tt * 512
                    hb = hid[tt % 2]
                    for ec in range(4):
                        for dc in range(8):
                            s.op("tensor", lambda t_, ec=ec, dc=dc, wb=wb, t0=t0: t_.matmul(
                                p_g[:], gateb[wb][:, dc, ec * 128:(ec + 1) * 128],
                                xh[:, dc, t0:t0 + 512],
                                start=(dc == 0), stop=(dc == 7)),
                                waits=wld + [("ld", ld_x)] + s.gate("pg_free"),
                                inc=("mm", 1) if dc == 7 else None)
                        mm_g = s.val("mm")
                        for dc in range(8):
                            s.op("tensor", lambda t_, ec=ec, dc=dc, wb=wb, t0=t0: t_.matmul(
                                p_u[:], upb[wb][:, dc, ec * 128:(ec + 1) * 128],
                                xh[:, dc, t0:t0 + 512],
                                start=(dc == 0), stop=(dc == 7)),
                                waits=s.gate("pu_free"),
                                inc=("mm", 1) if dc == 7 else None)
                        mm_u = s.val("mm")
                        s.op("scalar", lambda a: a.activation(scr[:], p_g[:], AF.Silu,
                                                              bias=0.0, scale=1.0),
                             waits=[("mm", mm_g)] + s.gate("scr_free"), inc=("ac", 1))
                        s.setg("pg_free", "ac")
                        s.op("vector", lambda v, ec=ec, hb=hb: v.tensor_tensor(
                            hb[:, ec, :], scr[:], p_u[:], ALU.mult),
                            waits=[("ac", s.val("ac")), ("mm", mm_u)] + s.gate("mhid%d_free" % (tt % 2)),
                            inc=("ve", 1))
                        s.setg("pu_free", "ve")
                        s.setg("scr_free", "ve")
                    ve_hid = s.val("ve")
                    if pend_moe[0] is not None:
                        emit_down_moe(pend_moe[0])
                    pend_moe[0] = (k, tt, ve_hid, wb, ld_x, gp_acc)
                    if k + 1 < NEXP and tt < 2:
                        emit_wload_moe(k + 1, tt)
                s.setg("gateb%d_free" % wb, "mm")
                s.setg("upb%d_free" % wb, "mm")
            if pend_moe[0] is not None:
                emit_down_moe(pend_moe[0])
                pend_moe[0] = None

            def emit_down_sh(pend):
                hcp, ve_hshp = pend
                sbp = hcp % 2
                for mt in range(2):
                    for dh in range(2):
                        pi = mt * 2 + dh
                        s.op("tensor", lambda t_, mt=mt, dh=dh, pi=pi, sbp=sbp, hcp=hcp: t_.matmul(
                            p_d[pi][:], hsh[sbp][:, mt * 128:(mt + 1) * 128],
                            shd[sbp][:, dh * 512:(dh + 1) * 512],
                            start=(hcp == 0), stop=(hcp == 7)),
                            waits=[("ve", ve_hshp)] +
                                  (s.gate("pd0_free", "pd1_free", "pd2_free", "pd3_free")
                                   if hcp == 0 else []),
                            inc=("mm", 1) if (mt == 1 and dh == 1) else None)
                s.setg("hsh%d_free" % sbp, "mm")
                s.setg("shwg%d_free" % sbp, "mm")

            for tt in range(8):
                t0 = tt * 256
                pend_sh = None
                for hc in range(8):
                    sb = hc % 2
                    wait_cast = [("wst", wst_cast)] if (h == 0 and tt == 0 and hc < 2) else []
                    s.op("sync", lambda e, hc=hc, sb=sb: e.dma_start(
                        out=shw[0 * 2 + sb][:].rearrange("p a b -> p (a b)"), in_=shb[0][hc]),
                        waits=wait_cast + s.gate("shwg%d_free" % sb), inc=("ld", 16))
                    s.op("sync", lambda e, hc=hc, sb=sb: e.dma_start(
                        out=shw[1 * 2 + sb][:].rearrange("p a b -> p (a b)"), in_=shb[1][hc]),
                        inc=("ld", 16))
                    s.op("sync", lambda e, hc=hc, sb=sb: e.dma_start(
                        out=shd[sb][:], in_=shb[2][hc]),
                        inc=("ld", 16))
                    shld = s.val("ld")
                    for dc in range(8):
                        s.op("tensor", lambda t_, dc=dc, sb=sb, t0=t0: t_.matmul(
                            p_g[:, 0:256], shw[sb][:, dc, :], xh[:, dc, t0:t0 + 256],
                            start=(dc == 0), stop=(dc == 7)),
                            waits=[("ld", shld)] + s.gate("pg_free"),
                            inc=("mm", 1) if dc == 7 else None)
                    mm_g = s.val("mm")
                    for dc in range(8):
                        s.op("tensor", lambda t_, dc=dc, sb=sb, t0=t0: t_.matmul(
                            p_u[:, 0:256], shw[2 + sb][:, dc, :], xh[:, dc, t0:t0 + 256],
                            start=(dc == 0), stop=(dc == 7)),
                            waits=s.gate("pu_free"),
                            inc=("mm", 1) if dc == 7 else None)
                    mm_u = s.val("mm")
                    s.op("scalar", lambda a: a.activation(scrsh[:], p_g[:, 0:256], AF.Silu,
                                                          bias=0.0, scale=1.0),
                         waits=[("mm", mm_g)] + s.gate("scrsh_free"), inc=("ac", 1))
                    s.setg("pg_free", "ac")
                    s.op("vector", lambda v, sb=sb: v.tensor_tensor(hsh[sb][:], scrsh[:], p_u[:, 0:256], ALU.mult),
                         waits=[("ac", s.val("ac")), ("mm", mm_u)] + s.gate("hsh%d_free" % sb),
                         inc=("ve", 1))
                    s.setg("pu_free", "ve")
                    s.setg("scrsh_free", "ve")
                    ve_hsh = s.val("ve")
                    if pend_sh is not None:
                        emit_down_sh(pend_sh)
                    pend_sh = (hc, ve_hsh)
                emit_down_sh(pend_sh)
                mm_shd = s.val("mm")
                for mt in range(2):
                    jm = tt * 2 + mt
                    for dh in range(2):
                        pi = mt * 2 + dh
                        s.op("vector", lambda v, jm=jm, dh=dh, pi=pi: v.tensor_tensor(
                            acc[:, jm, dh * 512:(dh + 1) * 512], p_d[pi][:],
                            acc[:, jm, dh * 512:(dh + 1) * 512], ALU.add),
                            waits=[("mm", mm_shd)],
                            inc=("ve", 1) if pi == 3 else None)
                for pi in range(4):
                    s.setg("pd%d_free" % pi, "ve")
            ve_acc = s.val("ve")
            s.op("gpsimd", lambda e, h=h: e.dma_start(
                out=moe.rearrange("(m p) d -> p m d", p=128)[:, h * NJM:(h + 1) * NJM, :],
                in_=acc[:]),
                waits=[("ve", ve_acc)], inc=("so", 16))
            s.setg("acc_free", "so")
            s.setg("xh_free", "so")

        return




def emit_rope_ops(s, ktmp, krop, tmpa, tmpb, cost, sint, jt):
    cosv = cost[:, jt, :].rearrange("p (u i) -> p u i", u=1).broadcast_to([128, 8, 32])
    sinv = sint[:, jt, :].rearrange("p (u i) -> p u i", u=1).broadcast_to([128, 8, 32])
    tv = ktmp[:].rearrange("p (h i z) -> p h i z", h=8, z=2)
    E = tv[:, :, :, 0]
    O = tv[:, :, :, 1]
    ov = krop[:].rearrange("p (h z i) -> p h z i", h=8, z=2)
    outA = ov[:, :, 0, :]
    outB = ov[:, :, 1, :]
    ta = tmpa[:].rearrange("p (h i) -> p h i", h=8)
    tb = tmpb[:].rearrange("p (h i) -> p h i", h=8)
    def vg(emit):
        s.op("vector", emit, waits=[("ve", s.val("ve"))], inc=("ve", 1))
    vg(lambda v: v.tensor_tensor(ta, E, cosv, ALU.mult))
    vg(lambda v: v.tensor_tensor(tb, O, sinv, ALU.mult))
    vg(lambda v: v.tensor_tensor(outA, ta, tb, ALU.subtract))
    vg(lambda v: v.tensor_tensor(ta, E, sinv, ALU.mult))
    vg(lambda v: v.tensor_tensor(tb, O, cosv, ALU.mult))
    vg(lambda v: v.tensor_tensor(outB, ta, tb, ALU.add))


def emit_attn(nc, s, dram):
    ap = dram
    xn1 = ap("xn1", [8, 128, S_LEN], BF16, kind="ExternalInput")
    x1 = ap("x1", [S_LEN, DM], F32, kind="ExternalInput")
    moe = ap("moe", [S_LEN, DM], F32, kind="ExternalInput")
    wqa = ap("wqa", [DM, DM], F32, kind="ExternalInput")
    wka = ap("wka", [DM, DM], F32, kind="ExternalInput")
    wva = ap("wva", [DM, DM], F32, kind="ExternalInput")
    woa = ap("woa", [DM, DM], F32, kind="ExternalInput")
    bq = ap("bq", [1, DM], F32, kind="ExternalInput")
    bk = ap("bk", [1, DM], F32, kind="ExternalInput")
    bv = ap("bv", [1, DM], F32, kind="ExternalInput")
    bo = ap("bo", [1, DM], F32, kind="ExternalInput")
    rcos = ap("rcos", [S_LEN, 32], F32, kind="ExternalInput")
    rsin = ap("rsin", [S_LEN, 32], F32, kind="ExternalInput")
    idb = ap("idb", [128, 128], BF16, kind="ExternalInput")
    dmask = ap("dmask", [4, 128, 512], BF16, kind="ExternalInput")
    out = ap("out", [S_LEN, DM], F32, kind="ExternalOutput")
    att0 = ap("att0", [8, 128, 4, 512], BF16, kind="Internal")

    es = ExitStack()
    S = lambda n, *a: es.enter_context(nc.sbuf_tensor("c_" + n, *a))
    P = lambda n, *a: es.enter_context(nc.psum_tensor("c_" + n, *a))

    NQT = 8
    xn1t = [S("xn1t0", [128, 8, 512], BF16), S("xn1t1", [128, 8, 512], BF16)]
    kfm = S("kfm", [128, 4, S_LEN], BF16)
    vaug = S("vaug", [128, 32, 520], BF16)
    qfm = S("qfm", [128, 4, 512], BF16)
    attg0t = [S("attg0t0", [128, 4, 512], BF16), S("attg0t1", [128, 4, 512], BF16)]
    atile = S("atile", [128, 4, 512], BF16)
    wqb = S("wqb", [128, 8, 512], BF16)
    wkv = S("wkv", [128, 8, 512], BF16)
    wob = S("wob", [128, 8, DM], BF16)
    wstg = [S("wstg0", [128, 512], F32), S("wstg1", [128, 512], F32)]
    cost = S("cost", [128, 32, 32], F32)
    sint = S("sint", [128, 32, 32], F32)
    maskt = S("maskt", [128, 4, 512], BF16)
    ktmp = S("ktmp", [128, 512], F32)
    krop = S("krop", [128, 512], F32)
    tmpa = S("tmpa", [128, 256], F32)
    tmpb = S("tmpb", [128, 256], F32)
    krb = S("krb", [128, 512], BF16)
    sqs = S("sqs", [128, 512], F32)
    knall = S("knall", [128, 32, 8], F32)
    qnall = S("qnall", [128, 32, 8], F32)
    knpad = S("knpad", [128, 128], BF16)
    mstat = S("mstat", [128, 8], F32)
    negMb = S("negMb", [128, 1], F32)
    rs_t = S("rs_t", [128, 512], F32)
    rb_s = S("rb_s", [64, 512], F32)
    atnb = S("atnb", [64, 512], BF16)
    pbuf = [S("pbuf%d" % i, [128, 512], BF16) for i in range(3)]
    bqb = S("bqb", [128, 512], F32)
    bkb = S("bkb", [128, 512], F32)
    bvb = S("bvb", [128, 512], F32)
    bob = S("bob", [128, DM], F32)
    x1m = S("x1m", [128, DM], F32)
    moem = S("moem", [128, DM], F32)
    base = S("base", [128, DM], F32)
    outt = S("outt", [128, DM], F32)
    onesb = S("onesb", [128, 128], F32)
    idbt = S("idbt", [128, 128], BF16)
    id64 = S("id64", [128, 128], BF16)

    p_sc = [P("p_sc%d" % i, [128, 512], F32) for i in range(3)]
    p_at = P("p_at", [128, 512], F32)
    p_aux = P("p_auxa", [128, 512], F32)
    p_pr = P("p_pr", [128, 512], F32)
    p_tr = P("p_tra", [128, 1024], BF16)

    with es:
        s.op("sync", lambda e: e.dma_start(out=idbt[:], in_=idb[:]), inc=("ld", 16))
        s.op("sync", lambda e: e.dma_start(
            out=cost[:], in_=rcos.rearrange("(j p) i -> p j i", p=128)), inc=("ld", 16))
        s.op("sync", lambda e: e.dma_start(
            out=sint[:], in_=rsin.rearrange("(j p) i -> p j i", p=128)), inc=("ld", 16))
        s.op("sync", lambda e: e.dma_start(
            out=maskt[:], in_=dmask.rearrange("m p f -> p m f")), inc=("ld", 16))
        s.op("sync", lambda e: e.dma_start(out=bob[0:1, :], in_=bo[:]), inc=("ld", 16))
        ld_c = s.val("ld")
        s.op("gpsimd", lambda e: e.memset(onesb[:], 1.0), inc=("gp", 1))
        s.op("gpsimd", lambda e: e.memset(knpad[:], 0.0), inc=("gp", 1))
        s.op("gpsimd", lambda e: e.memset(id64[:], 0.0), inc=("gp", 1))
        gp_c = s.val("gp")
        s.op("vector", lambda v: v.tensor_copy(id64[0:64, 0:64], idbt[0:64, 0:64]),
             waits=[("ld", ld_c), ("gp", gp_c)], inc=("ve", 1))
        for half in range(2):
            s.op("tensor", lambda t_, half=half: t_.matmul(
                p_aux[:], onesb[0:1, :], bob[0:1, half * 512:(half + 1) * 512],
                start=True, stop=True),
                waits=[("ld", ld_c), ("gp", gp_c)] + s.gate("paux_free"), inc=("mm", 1))
            s.op("vector", lambda v, half=half: v.tensor_copy(
                bob[:, half * 512:(half + 1) * 512], p_aux[:]),
                waits=[("mm", s.val("mm"))], inc=("ve", 1))
            s.setg("paux_free", "ve")

        def load_whalf(whbm, dst, tag, gc, ncols=512):
            for dc in range(8):
                sb = dc % 2
                s.op("sync", lambda e, whbm=whbm, dc=dc, sb=sb, gc=gc, ncols=ncols: e.dma_start(
                    out=wstg[sb][:, 0:ncols], in_=whbm[dc * 128:(dc + 1) * 128, gc:gc + ncols]),
                    waits=s.gate("wstg%d_free" % sb), inc=("ld", 16))
                s.op("vector", lambda v, dc=dc, sb=sb, dst=dst, ncols=ncols: v.tensor_copy(
                    dst[:, dc, 0:ncols], wstg[sb][:, 0:ncols]),
                    waits=[("ld", s.val("ld"))] + (s.gate(tag + "_free") if dc == 0 else []),
                    inc=("ve", 1))
                s.setg("wstg%d_free" % sb, "ve")
            s.setg(tag + "_ready", "ve")

        def proj_psum(src_tile, wtile, mt, wait_extra):
            for dc in range(8):
                s.op("tensor", lambda t_, dc=dc, src_tile=src_tile, wtile=wtile, mt=mt: t_.matmul(
                    p_pr[:], src_tile[:, dc, mt * 128:(mt + 1) * 128], wtile[:, dc, 0:512],
                    start=(dc == 0), stop=(dc == 7)),
                    waits=wait_extra + (s.gate("ppr_free") if dc == 0 else []),
                    inc=("mm", 1) if dc == 7 else None)
            return s.val("mm")

        def pmax_to_scalar(src_col_f32, dst_slot):
            s.op("vector", lambda v: v.tensor_copy(knpad[:, 0:1], src_col_f32),
                 waits=[("ve", s.val("ve"))] + s.gate("knpad_free"), inc=("ve", 1))
            s.op("tensor", lambda t_: t_.transpose(p_tr[:, 0:128], knpad[:], idbt[:]),
                 waits=[("ve", s.val("ve")), ("ld", ld_c)] + s.gate("ptr_free"),
                 inc=("mm", 1))
            s.op("vector", lambda v, dst_slot=dst_slot: v.reduce_max(
                mstat[0:1, dst_slot:dst_slot + 1], p_tr[0:1, 0:128], axis=AX.X),
                waits=[("mm", s.val("mm"))], inc=("ve", 1))
            s.setg("ptr_free", "ve")
            s.setg("knpad_free", "ve")

        for g in range(2):
            gc = g * 512
            for (bsrc, btile) in ((bq, bqb), (bk, bkb), (bv, bvb)):
                s.op("sync", lambda e, bsrc=bsrc, btile=btile, gc=gc: e.dma_start(
                    out=btile[0:1, :], in_=bsrc[:, gc:gc + 512]),
                    inc=("ld", 16))
                s.op("tensor", lambda t_, btile=btile: t_.matmul(
                    p_aux[:], onesb[0:1, :], btile[0:1, :], start=True, stop=True),
                    waits=[("ld", s.val("ld")), ("gp", gp_c)] + s.gate("paux_free"),
                    inc=("mm", 1))
                s.op("vector", lambda v, btile=btile: v.tensor_copy(btile[:], p_aux[:]),
                     waits=[("mm", s.val("mm"))], inc=("ve", 1))
                s.setg("paux_free", "ve")
            load_whalf(wka, wkv, "wkv", gc)
            s.op("gpsimd", lambda e: e.memset(vaug[:], 1.0),
                 waits=s.gate("vaug_free"), inc=("gp", 1))
            gp_vaug = s.val("gp")

            for tt in range(NQT):
                t0 = tt * 512
                xb = xn1t[tt % 2]
                s.op("sync", lambda e, t0=t0, xb=xb: e.dma_start(
                    out=xb[:], in_=xn1[:, :, t0:t0 + 512].rearrange("c p t -> p c t")),
                    waits=s.gate("xn1t%d_free" % (tt % 2)), inc=("ld", 16))
                ld_x = s.val("ld")
                for mt in range(4):
                    jt = tt * 4 + mt
                    mm_k = proj_psum(xb, wkv, mt, [("ld", ld_x)] + s.gate("wkv_ready"))
                    s.op("vector", lambda v: v.tensor_tensor(ktmp[:], p_pr[:], bkb[:], ALU.add),
                         waits=[("mm", mm_k)] + s.gate("ktmp_free"), inc=("ve", 1))
                    s.setg("ppr_free", "ve")
                    ve_ktmp = s.val("ve")
                    s.op("scalar", lambda a: a.activation(sqs[:], ktmp[:], AF.Square,
                                                          bias=0.0, scale=1.0),
                         waits=[("ve", ve_ktmp)] + s.gate("sqs_free"), inc=("ac", 1))
                    s.op("vector", lambda v, jt=jt: v.reduce_sum(
                        knall[:, jt, :], sqs[:].rearrange("p (h d) -> p h d", h=8), axis=AX.X),
                        waits=[("ac", s.val("ac"))], inc=("ve", 1))
                    s.setg("sqs_free", "ve")
                    emit_rope_ops(s, ktmp, krop, tmpa, tmpb, cost, sint, jt)
                    s.op("vector", lambda v: v.tensor_copy(krb[:], krop[:]),
                         waits=[("ve", s.val("ve"))], inc=("ve", 1))
                    s.setg("ktmp_free", "ve")
                    ve_krb = s.val("ve")
                    for c in range(4):
                        s.op("tensor", lambda t_, c=c: t_.transpose(
                            p_tr[:, 0:128], krb[:, c * 128:(c + 1) * 128], idbt[:]),
                            waits=[("ve", ve_krb), ("ld", ld_c)] + s.gate("ptr_free"),
                            inc=("mm", 1))
                        s.op("vector", lambda v, c=c, jt=jt: v.tensor_copy(
                            kfm[:, c, jt * 128:(jt + 1) * 128], p_tr[:, 0:128]),
                            waits=[("mm", s.val("mm"))], inc=("ve", 1))
                        s.setg("ptr_free", "ve")
                s.setg("xn1t%d_free" % (tt % 2), "mm")
            s.op("vector", lambda v: v.reduce_max(
                mstat[:, 2:3], knall[:].rearrange("p a b -> p (a b)"), axis=AX.X),
                waits=[("ve", s.val("ve"))], inc=("ve", 1))
            pmax_to_scalar(mstat[:, 2:3], 0)
            s.setg("wkv_free", "mm")
            load_whalf(wva, wkv, "wkv", gc)
            for tt in range(NQT):
                t0 = tt * 512
                xb = xn1t[tt % 2]
                s.op("sync", lambda e, t0=t0, xb=xb: e.dma_start(
                    out=xb[:], in_=xn1[:, :, t0:t0 + 512].rearrange("c p t -> p c t")),
                    waits=s.gate("xn1t%d_free" % (tt % 2)), inc=("ld", 16))
                ld_x = s.val("ld")
                for mt in range(4):
                    jt = tt * 4 + mt
                    mm_v = proj_psum(xb, wkv, mt, [("ld", ld_x)] + s.gate("wkv_ready"))
                    vv = vaug[:, jt, :].rearrange("p (h e) -> p h e", h=8)[:, :, 0:64]
                    pv = p_pr[:].rearrange("p (h e) -> p h e", h=8)
                    bb = bvb[:].rearrange("p (h e) -> p h e", h=8)
                    s.op("vector", lambda v, vv=vv, pv=pv, bb=bb: v.tensor_tensor(
                        vv, pv, bb, ALU.add),
                        waits=[("mm", mm_v), ("gp", gp_vaug)], inc=("ve", 1))
                    s.setg("ppr_free", "ve")
                s.setg("xn1t%d_free" % (tt % 2), "mm")
            s.setg("wkv_free", "mm")
            load_whalf(wqa, wqb, "wqb", gc)
            if g == 1:
                for dc in range(8):
                    for half in range(2):
                        sb = half
                        s.op("sync", lambda e, dc=dc, half=half, sb=sb: e.dma_start(
                            out=wstg[sb][:], in_=woa[dc * 128:(dc + 1) * 128,
                                                    half * 512:(half + 1) * 512]),
                            waits=s.gate("wstg%d_free" % sb), inc=("ld", 16))
                        s.op("vector", lambda v, dc=dc, half=half, sb=sb: v.tensor_copy(
                            wob[:, dc, half * 512:(half + 1) * 512], wstg[sb][:]),
                            waits=[("ld", s.val("ld"))], inc=("ve", 1))
                        s.setg("wstg%d_free" % sb, "ve")
            ve_wo = s.val("ve")

            for qt in range(NQT):
                t0 = qt * 512
                nkc = 4 * (qt + 1)
                xb = xn1t[qt % 2]
                s.op("sync", lambda e, t0=t0, xb=xb: e.dma_start(
                    out=xb[:], in_=xn1[:, :, t0:t0 + 512].rearrange("c p t -> p c t")),
                    waits=s.gate("xn1t%d_free" % (qt % 2)), inc=("ld", 16))
                ld_x = s.val("ld")
                if g == 1:
                    ab = attg0t[qt % 2]
                    s.op("sync", lambda e, qt=qt, ab=ab: e.dma_start(
                        out=ab[:], in_=att0[qt]),
                        waits=s.gate("attg0t%d_free" % (qt % 2)), inc=("ld", 16))
                else:
                    ab = None
                ld_a = s.val("ld")
                for mt in range(4):
                    jt = qt * 4 + mt
                    mm_q = proj_psum(xb, wqb, mt, [("ld", ld_x)] + s.gate("wqb_ready"))
                    s.op("vector", lambda v: v.tensor_tensor(ktmp[:], p_pr[:], bqb[:], ALU.add),
                         waits=[("mm", mm_q)] + s.gate("ktmp_free"), inc=("ve", 1))
                    s.setg("ppr_free", "ve")
                    ve_ktmp = s.val("ve")
                    s.op("scalar", lambda a: a.activation(sqs[:], ktmp[:], AF.Square,
                                                          bias=0.0, scale=1.0),
                         waits=[("ve", ve_ktmp)] + s.gate("sqs_free"), inc=("ac", 1))
                    s.op("vector", lambda v, jt=jt: v.reduce_sum(
                        qnall[:, jt, :], sqs[:].rearrange("p (h d) -> p h d", h=8), axis=AX.X),
                        waits=[("ac", s.val("ac"))], inc=("ve", 1))
                    s.setg("sqs_free", "ve")
                    emit_rope_ops(s, ktmp, krop, tmpa, tmpb, cost, sint, jt)
                    s.op("vector", lambda v: v.tensor_copy(krb[:], krop[:]),
                         waits=[("ve", s.val("ve"))], inc=("ve", 1))
                    s.setg("ktmp_free", "ve")
                    ve_krb = s.val("ve")
                    for c in range(4):
                        s.op("tensor", lambda t_, c=c, mt=mt: t_.transpose(
                            p_tr[:, 0:128], krb[:, c * 128:(c + 1) * 128], idbt[:]),
                            waits=[("ve", ve_krb)] + s.gate("ptr_free"),
                            inc=("mm", 1))
                        s.op("vector", lambda v, c=c, mt=mt: v.tensor_copy(
                            qfm[:, c, mt * 128:(mt + 1) * 128], p_tr[:, 0:128]),
                            waits=[("mm", s.val("mm"))], inc=("ve", 1))
                        s.setg("ptr_free", "ve")
                s.op("vector", lambda v, qt=qt: v.reduce_max(
                    mstat[:, 3:4],
                    qnall[:, qt * 4:(qt + 1) * 4, :].rearrange("p a b -> p (a b)"), axis=AX.X),
                    waits=[("ve", s.val("ve"))], inc=("ve", 1))
                pmax_to_scalar(mstat[:, 3:4], 1)
                def vg1(emit):
                    s.op("vector", emit, waits=[("ve", s.val("ve"))], inc=("ve", 1))
                vg1(lambda v: v.tensor_mul(mstat[0:1, 4:5], mstat[0:1, 0:1], mstat[0:1, 1:2]))
                vg1(lambda v: v.tensor_scalar(mstat[0:1, 4:5], mstat[0:1, 4:5], 1e-30, None, ALU.add))
                s.op("scalar", lambda a: a.activation(mstat[0:1, 5:6], mstat[0:1, 4:5],
                                                      AF.Ln, bias=0.0, scale=1.0),
                     waits=[("ve", s.val("ve"))], inc=("ac", 1))
                s.op("scalar", lambda a: a.activation(mstat[0:1, 6:7], mstat[0:1, 5:6],
                                                      AF.Exp, bias=0.0, scale=0.5),
                     waits=[("ac", s.val("ac"))], inc=("ac", 1))
                vg1(lambda v: v.tensor_scalar(mstat[0:1, 7:8], mstat[0:1, 6:7], -0.125, None, ALU.mult))
                s.op("tensor", lambda t_: t_.matmul(
                    p_aux[:, 0:1], onesb[0:1, :], mstat[0:1, 7:8], start=True, stop=True),
                    waits=[("ve", s.val("ve")), ("ac", s.val("ac"))] + s.gate("paux_free"),
                    inc=("mm", 1))
                s.op("vector", lambda v: v.tensor_copy(negMb[:], p_aux[:, 0:1]),
                     waits=[("mm", s.val("mm"))] + s.gate("negM_inuse"), inc=("ve", 1))
                s.setg("paux_free", "ve")
                s.setg("negM_ready", "ve")
                ve_qfm = s.val("ve")
                for hl in range(8):
                    cc, ro = hl // 2, 64 * (hl % 2)

                    def emit_pv(prev, hl=hl, nkc=nkc):
                        kc, ac_exp, ve_pb = prev
                        pp = kc % (ATTN_LAG + 1)
                        s.op("tensor", lambda t_, kc=kc, pp=pp, hl=hl, nkc=nkc: t_.matmul(
                            p_at[0:65, :], vaug[:, kc, hl * 65:hl * 65 + 65],
                            pbuf[pp][:],
                            start=(kc == 0), stop=(kc == nkc - 1)),
                            waits=[("ac", ac_exp), ("ve", ve_pb)] +
                                  (s.gate("pat_free") if kc == 0 else []),
                            inc=("mm", 1) if kc == nkc - 1 else None)
                        s.setg("pbuf%d_free" % pp, "mm")

                    pend = []
                    for kc in range(nkc):
                        pp = kc % (ATTN_LAG + 1)
                        s.op("tensor", lambda t_, kc=kc, pp=pp, cc=cc, ro=ro: t_.matmul(
                            p_sc[pp][:],
                            kfm[:, cc, kc * 128:(kc + 1) * 128][ro:ro + 64, :],
                            qfm[:, cc, :][ro:ro + 64, :],
                            start=True, stop=True),
                            waits=[("ve", ve_qfm)] + s.gate("psc%d_free" % pp),
                            inc=("mm", 1))
                        mm_st = s.val("mm")
                        s.op("scalar", lambda a, pp=pp: a.activation(
                            pbuf[pp][:], p_sc[pp][:],
                            AF.Exp, bias=negMb[:, 0:1], scale=0.125),
                            waits=[("mm", mm_st)] + s.gate("pbuf%d_free" % pp) +
                                  s.gate("negM_ready"),
                            inc=("ac", 1))
                        s.setg("psc%d_free" % pp, "ac")
                        ac_exp = s.val("ac")
                        j = kc - 4 * qt
                        if j >= 0:
                            s.op("vector", lambda v, pp=pp, j=j: v.tensor_tensor(
                                pbuf[pp][:], pbuf[pp][:],
                                maskt[:, j, :], ALU.mult),
                                waits=[("ac", ac_exp), ("ld", ld_c)], inc=("ve", 1))
                        ve_pb = s.val("ve")
                        pend.append((kc, ac_exp, ve_pb))
                        if len(pend) > ATTN_LAG:
                            emit_pv(pend.pop(0))
                    for pr in pend:
                        emit_pv(pr)
                    mm_pv = s.val("mm")
                    s.op("vector", lambda v: v.reciprocal(rs_t[64:65, :], p_at[64:65, :]),
                         waits=[("mm", mm_pv)] + s.gate("rst_free"), inc=("ve", 1))
                    s.op("tensor", lambda t_: t_.matmul(
                        p_aux[0:64, :], onesb[64:65, 0:64], rs_t[64:65, :],
                        start=True, stop=True),
                        waits=[("ve", s.val("ve"))] + s.gate("paux_free"), inc=("mm", 1))
                    s.op("vector", lambda v: v.tensor_copy(rb_s[:], p_aux[0:64, :]),
                         waits=[("mm", s.val("mm"))], inc=("ve", 1))
                    s.setg("rst_free", "ve")
                    if ro == 0:
                        s.op("vector", lambda v, cc=cc: v.tensor_tensor(
                            atile[0:64, cc, :], p_at[0:64, :], rb_s[:], ALU.mult),
                            waits=[("ve", s.val("ve"))] +
                                  (s.gate("atile_free") if hl == 0 else []),
                            inc=("ve", 1))
                        s.setg("pat_free", "ve")
                        s.setg("paux_free", "ve")
                        s.setg("negM_inuse", "ve")
                    else:
                        s.op("vector", lambda v: v.tensor_tensor(
                            atnb[:], p_at[0:64, :], rb_s[:], ALU.mult),
                            waits=[("ve", s.val("ve"))], inc=("ve", 1))
                        s.setg("pat_free", "ve")
                        s.op("tensor", lambda t_: t_.matmul(
                            p_aux[64:128, :], id64[0:64, 0:64], atnb[:],
                            start=True, stop=True),
                            waits=[("ve", s.val("ve"))], inc=("mm", 1))
                        s.op("vector", lambda v, cc=cc: v.tensor_copy(
                            atile[64:128, cc, :], p_aux[64:128, :]),
                            waits=[("mm", s.val("mm"))], inc=("ve", 1))
                        s.setg("paux_free", "ve")
                        s.setg("negM_inuse", "ve")
                ve_atile = s.val("ve")
                if g == 0:
                    s.op("gpsimd", lambda e, qt=qt: e.dma_start(out=att0[qt], in_=atile[:]),
                         waits=[("ve", ve_atile)], inc=("so", 16))
                    s.setg("atile_free", "so")
                else:
                    for mt in range(4):
                        jt = qt * 4 + mt
                        s.op("sync", lambda e, jt=jt: e.dma_start(
                            out=x1m[:], in_=x1.rearrange("(m p) d -> p m d", p=128)[:, jt, :]),
                            waits=s.gate("x1m_free"), inc=("ld", 16))
                        s.op("sync", lambda e, jt=jt: e.dma_start(
                            out=moem[:], in_=moe.rearrange("(m p) d -> p m d", p=128)[:, jt, :]),
                            inc=("ld", 16))
                        ld_e = s.val("ld")
                        s.op("vector", lambda v: v.tensor_add(base[:], x1m[:], moem[:]),
                             waits=[("ld", ld_e)] + s.gate("base_free"), inc=("ve", 1))
                        s.op("vector", lambda v: v.tensor_add(base[:], base[:], bob[:]),
                             waits=[("ve", s.val("ve"))], inc=("ve", 1))
                        s.setg("x1m_free", "ve")
                        ve_base = s.val("ve")
                        for dh in range(2):
                            pw = p_pr if dh == 0 else p_at
                            gkey = "ppr_free" if dh == 0 else "pat_free"
                            for c in range(4):
                                s.op("tensor", lambda t_, c=c, mt=mt, dh=dh, pw=pw, ab=ab: t_.matmul(
                                    pw[:], ab[:, c, mt * 128:(mt + 1) * 128],
                                    wob[:, c, dh * 512:(dh + 1) * 512],
                                    start=(c == 0), stop=False),
                                    waits=([("ld", ld_a), ("ve", ve_atile), ("ve", ve_wo)] +
                                           (s.gate(gkey) if c == 0 else [])))
                            for c in range(4):
                                s.op("tensor", lambda t_, c=c, mt=mt, dh=dh, pw=pw: t_.matmul(
                                    pw[:], atile[:, c, mt * 128:(mt + 1) * 128],
                                    wob[:, 4 + c, dh * 512:(dh + 1) * 512],
                                    start=False, stop=(c == 3)),
                                    inc=("mm", 1) if c == 3 else None)
                            s.op("vector", lambda v, dh=dh, pw=pw: v.tensor_tensor(
                                outt[:, dh * 512:(dh + 1) * 512], pw[:],
                                base[:, dh * 512:(dh + 1) * 512], ALU.add),
                                waits=[("mm", s.val("mm")), ("ve", ve_base)], inc=("ve", 1))
                            s.setg("ppr_free" if dh == 0 else "pat_free", "ve")
                        s.setg("base_free", "ve")
                        ve_out = s.val("ve")
                        s.op("gpsimd", lambda e, jt=jt: e.dma_start(
                            out=out.rearrange("(m p) d -> p m d", p=128)[:, jt, :], in_=outt[:]),
                            waits=[("ve", ve_out)], inc=("so", 16))
                    s.setg("attg0t%d_free" % (qt % 2), "mm")
                    s.setg("atile_free", "mm")
                s.setg("xn1t%d_free" % (qt % 2), "mm")
            s.setg("vaug_free", "mm")
            s.setg("wqb_free", "mm")
            s.setg("wkv_free", "mm")

        return



def _l0_host_inputs(inputs, core):
    f32 = np.float32
    d = {}
    d["x"] = np.ascontiguousarray(np.asarray(inputs["L0_attn_x"])[core], dtype=f32)
    d["wq"] = np.ascontiguousarray(np.asarray(inputs["L0_attn_W_Q"], dtype=f32))
    d["wk"] = np.ascontiguousarray(np.asarray(inputs["L0_attn_W_K"], dtype=f32))
    d["wkp"] = np.ascontiguousarray(np.asarray(inputs["L0_attn_W_Kp"], dtype=f32))
    d["wv"] = np.ascontiguousarray(np.asarray(inputs["L0_attn_W_V"], dtype=f32))
    d["wo"] = np.ascontiguousarray(np.asarray(inputs["L0_attn_W_O"], dtype=f32))
    d["gam"] = np.ascontiguousarray(np.asarray(inputs["L0_attn_ln_gamma"], dtype=f32)[None, :])
    d["bet"] = np.ascontiguousarray(np.asarray(inputs["L0_attn_ln_beta"], dtype=f32)[None, :])
    d["wgT"] = np.ascontiguousarray(np.asarray(inputs["L0_ffn_w_gate"], dtype=np.int32).T)
    d["wuT"] = np.ascontiguousarray(np.asarray(inputs["L0_ffn_w_up"], dtype=np.int32).T)
    d["wdT"] = np.ascontiguousarray(np.asarray(inputs["L0_ffn_w_down"], dtype=np.int32).T)
    d["rwT"] = np.ascontiguousarray(np.asarray(inputs["L1_ffn_router_w"], dtype=f32).T)
    d["rbias"] = np.ascontiguousarray(np.asarray(inputs["L1_ffn_router_bias"], dtype=f32)[None, :])
    d["idb"] = np.eye(128, dtype=BFNP)
    q = np.arange(32)[:, None, None]
    j = np.arange(32)[None, :, None]
    k = np.arange(32)[None, None, :]
    wm = np.where((j > q) | (k > q), np.float32(-1e9), np.float32(0.0))
    d["wmask"] = np.ascontiguousarray(wm.reshape(32, 1024))
    d["idx16"] = np.ascontiguousarray(np.broadcast_to(np.arange(16, dtype=f32), (128, 16)))
    return d


def _host_consts_l0():
    f32 = np.float32
    q = np.arange(32)[:, None, None]
    j = np.arange(32)[None, :, None]
    k = np.arange(32)[None, None, :]
    wm = np.where((j > q) | (k > q), np.float32(-1e9), np.float32(0.0))
    return {
        "idb": np.eye(128, dtype=BFNP),
        "wmask": np.ascontiguousarray(wm.reshape(32, 1024).astype(f32)),
        "idx16": np.ascontiguousarray(np.broadcast_to(np.arange(16, dtype=f32), (128, 16))),
    }


def _host_consts_attn():
    f32 = np.float32
    inv = (10000.0 ** (-2.0 * np.arange(32, dtype=f32) / 64.0)).astype(f32)
    ang = np.arange(S_LEN, dtype=f32)[:, None] * inv[None, :]
    kk = np.arange(128)[:, None]
    qq = np.arange(512)[None, :]
    dm = np.stack([(kk <= qq - 128 * j).astype(BFNP) for j in range(4)])
    return {
        "rcos": np.cos(ang).astype(f32),
        "rsin": np.sin(ang).astype(f32),
        "idb": np.eye(128, dtype=BFNP),
        "dmask": np.ascontiguousarray(dm),
    }


def kernel(**inputs):
    f32 = np.float32
    A = lambda k: np.asarray(inputs[k], f32)
    x_all = A("L0_attn_x")

    l0_shared = {
        "wq": np.ascontiguousarray(A("L0_attn_W_Q")),
        "wk": np.ascontiguousarray(A("L0_attn_W_K")),
        "wkp": np.ascontiguousarray(A("L0_attn_W_Kp")),
        "wv": np.ascontiguousarray(A("L0_attn_W_V")),
        "wo": np.ascontiguousarray(A("L0_attn_W_O")),
        "gam": np.ascontiguousarray(A("L0_attn_ln_gamma")[None, :]),
        "bet": np.ascontiguousarray(A("L0_attn_ln_beta")[None, :]),
        "wgT": np.ascontiguousarray(np.asarray(inputs["L0_ffn_w_gate"], np.int32).T),
        "wuT": np.ascontiguousarray(np.asarray(inputs["L0_ffn_w_up"], np.int32).T),
        "wdT": np.ascontiguousarray(np.asarray(inputs["L0_ffn_w_down"], np.int32).T),
        "rwT": np.ascontiguousarray(A("L1_ffn_router_w").T),
        "rbias": np.ascontiguousarray(A("L1_ffn_router_bias")[None, :]),
    }
    l0_shared.update(_host_consts_l0())
    moe_shared = {
        "eup": np.ascontiguousarray(A("L1_ffn_up_proj")),
        "egt": np.ascontiguousarray(A("L1_ffn_gate_proj")),
        "edn": np.ascontiguousarray(A("L1_ffn_down_proj")),
        "shgT": np.ascontiguousarray(A("L1_ffn_sh_gate").T),
        "shuT": np.ascontiguousarray(A("L1_ffn_sh_up").T),
        "shdT": np.ascontiguousarray(A("L1_ffn_sh_down").T),
    }
    attn_shared = {
        "wqa": np.ascontiguousarray(A("L1_attn_Wq")),
        "wka": np.ascontiguousarray(A("L1_attn_Wk")),
        "wva": np.ascontiguousarray(A("L1_attn_Wv")),
        "woa": np.ascontiguousarray(A("L1_attn_Wo")),
        "bq": np.ascontiguousarray(A("L1_attn_bq")[None, :]),
        "bk": np.ascontiguousarray(A("L1_attn_bk")[None, :]),
        "bv": np.ascontiguousarray(A("L1_attn_bv")[None, :]),
        "bo": np.ascontiguousarray(A("L1_attn_bo")[None, :]),
    }
    attn_shared.update(_host_consts_attn())

    cores = list(range(NCORES))
    nc = build_fused()
    shared = {}
    shared.update(l0_shared)
    shared.update(moe_shared)
    shared.update(attn_shared)
    maps = []
    for c in cores:
        m = dict(shared)
        m["x"] = np.ascontiguousarray(x_all[c])
        maps.append(m)
    r3 = run_bass_kernel_spmd(nc, maps, core_ids=cores)
    return np.stack([np.asarray(r3.results[c]["out"], f32) for c in cores], axis=0)


# version 17
# speedup vs baseline: 1.4879x; 1.0912x over previous
import numpy as np
import ml_dtypes
import concourse.bass as bass
import concourse.mybir as mybir
from concourse.bass_utils import run_bass_kernel_spmd
from contextlib import ExitStack

F32 = mybir.dt.float32
BF16 = mybir.dt.bfloat16
I32 = mybir.dt.int32
AF = mybir.ActivationFunctionType
ALU = mybir.AluOpType
AX = mybir.AxisListType
BFNP = ml_dtypes.bfloat16

S_LEN = 4096
DM = 1024
NH = 16
HD = 64
HID0 = 3072
NEXP = 16
EDIM = 512
NCORES = 8
SCL = 1.0 / 127.0
EPS_LN = 1e-5
EPS_NORM = 1e-7

TT = 256
NJ = TT // 128
NT0 = S_LEN // TT
NHC = HID0 // 128
ATTN_LAG = 2
LDQ_U = "gpsimd"
LDQ_D = "scalar"
MOE_Q_U = "gpsimd"
MOE_Q_D = "scalar"


class Sched:

    def __init__(self):
        self.ops = []
        self.c = {}
        self.g = {}
        self.bar = {}

    def barrier(self):
        snap = tuple((k, v) for k, v in self.c.items() if v > 0)
        self.bar = {e: snap for e in ("sync", "tensor", "vector", "scalar", "gpsimd")}
        self.g = {}

    def val(self, sem):
        return self.c.get(sem, 0)

    def gate(self, *keys):
        out = []
        for k in keys:
            t = self.g.get(k)
            if t is not None:
                out.append(t)
        return out

    def setg(self, key, sem):
        self.g[key] = (sem, self.val(sem))

    def op(self, eng, emit, waits=(), inc=None):
        if eng in self.bar:
            waits = list(self.bar.pop(eng)) + list(waits)
        if inc is not None:
            self.c[inc[0]] = self.c.get(inc[0], 0) + inc[1]
        self.ops.append((eng, emit, tuple(waits), inc))

    def realize(self, block, sems):
        for name in ("sync", "tensor", "vector", "scalar", "gpsimd"):
            sub = [o for o in self.ops if o[0] == name]
            if not sub:
                continue

            def make(sub):
                def f(eng):
                    seen = {}
                    for _, emit, waits, inc in sub:
                        for sname, v in waits:
                            if v > 0 and seen.get(sname, -1) < v:
                                eng.wait_ge(sems[sname], v)
                                seen[sname] = v
                        inst = emit(eng)
                        if inc is not None:
                            inst.then_inc(sems[inc[0]], inc[1])
                return f

            getattr(block, name)(make(sub))


def _bc(apv, reps, unit_axis=1):
    return apv.rearrange("p (u n) -> p u n", u=1).broadcast_to(
        [apv.shape[0], reps, apv.shape[1]])


def _ln_stats(s, xt_view, nj, st, sqscr, in_waits, gkey="sqscr_free"):
    stb, mv, rstd, negmr = st["bn"], st["mv"], st["rstd"], st["negmr"]
    for j in range(nj):
        for h in range(2):
            s.op("vector", lambda v, j=j, h=h: v.bn_stats(
                stb[:, j, h * 6:(h + 1) * 6], xt_view[:, j, h * 512:(h + 1) * 512]),
                waits=(in_waits + s.gate(gkey)) if (j == 0 and h == 0) else [],
                inc=("ve", 1) if (j == nj - 1 and h == 1) else None)
    ve_bn = s.val("ve")
    for j in range(nj):
        s.op("vector", lambda v, j=j: v.bn_aggr(mv[:, j, :], stb[:, j, :]),
             waits=[("ve", ve_bn)] if j == 0 else [],
             inc=("ve", 1) if j == nj - 1 else None)
    s.setg(gkey, "ve")
    s.op("vector", lambda v: v.tensor_scalar(rstd[:, 0:nj], mv[:, :, 1], EPS_LN,
                                             None, ALU.add),
         waits=[("ve", s.val("ve"))], inc=("ve", 1))
    s.op("scalar", lambda a: a.activation(rstd[:, 0:nj], rstd[:, 0:nj], AF.Sqrt,
                                          bias=0.0, scale=1.0),
         waits=[("ve", s.val("ve"))], inc=("ac", 1))
    s.op("vector", lambda v: v.reciprocal(rstd[:, 0:nj], rstd[:, 0:nj]),
         waits=[("ac", s.val("ac"))], inc=("ve", 1))
    s.op("vector", lambda v: v.scalar_tensor_tensor(
        negmr[:, 0:nj], mv[:, :, 0], -1.0, rstd[:, 0:nj], ALU.mult, ALU.mult),
        waits=[("ve", s.val("ve"))], inc=("ve", 1))


def _mk_ctx():
    nc = bass.Bass("TRN2", target_bir_lowering=False, debug=False)
    s = Sched()
    cache = {}

    def dram(name, shape, dtype, kind="ExternalInput"):
        if name not in cache:
            cache[name] = nc.dram_tensor(name, shape, dtype, kind=kind).ap()
        return cache[name]

    return nc, s, dram


def _realize(nc, s):
    with nc.sbuf_tensor("fin", [1, 1], F32) as fin:
        s.op("gpsimd", lambda e: e.memset(fin[:], 0.0),
             waits=[(k, v) for k, v in s.c.items() if v > 0])
        sems = {}
        with ExitStack() as es2:
            for n in sorted(s.c):
                sems[n] = es2.enter_context(nc.semaphore(n))
            with nc.Block() as block:
                s.realize(block, sems)
    return nc


def build_l0(stop_after="all", ntiles=NT0, dbg_scratch=False):
    nc, s, dram = _mk_ctx()
    emit_l0(nc, s, dram, stop_after, ntiles, dbg_scratch)
    return _realize(nc, s)


def build_moe():
    nc, s, dram = _mk_ctx()
    emit_moe(nc, s, dram)
    return _realize(nc, s)


def build_attn():
    nc, s, dram = _mk_ctx()
    emit_attn(nc, s, dram)
    return _realize(nc, s)


def build_fused():
    nc, s, dram0 = _mk_ctx()
    handoff = {"x1": None, "xn1": None, "w16": None, "moe": None}

    def dram(name, shape, dtype, kind="ExternalInput"):
        if name in handoff:
            kind = "Internal"
        return dram0(name, shape, dtype, kind)

    emit_l0(nc, s, dram, "all", NT0, False)
    s.barrier()
    emit_moe(nc, s, dram)
    s.barrier()
    emit_attn(nc, s, dram)
    return _realize(nc, s)


def emit_l0(nc, s, dram, stop_after="all", ntiles=NT0, dbg_scratch=False):
    ap = dram
    x = ap("x", [S_LEN, DM], F32, kind="ExternalInput")
    wq = ap("wq", [DM, DM], F32, kind="ExternalInput")
    wk = ap("wk", [DM, DM], F32, kind="ExternalInput")
    wkp = ap("wkp", [DM, DM], F32, kind="ExternalInput")
    wv = ap("wv", [DM, DM], F32, kind="ExternalInput")
    wo = ap("wo", [DM, DM], F32, kind="ExternalInput")
    gam = ap("gam", [1, DM], F32, kind="ExternalInput")
    bet = ap("bet", [1, DM], F32, kind="ExternalInput")
    wgT = ap("wgT", [DM, HID0], I32, kind="ExternalInput")
    wuT = ap("wuT", [DM, HID0], I32, kind="ExternalInput")
    wdT = ap("wdT", [HID0, DM], I32, kind="ExternalInput")
    rwT = ap("rwT", [DM, 16], F32, kind="ExternalInput")
    rbias = ap("rbias", [1, 16], F32, kind="ExternalInput")
    idb = ap("idb", [128, 128], BF16, kind="ExternalInput")
    wmask = ap("wmask", [32, DM], F32, kind="ExternalInput")
    idx16 = ap("idx16", [128, 16], F32, kind="ExternalInput")
    x1 = ap("x1", [S_LEN, DM], F32, kind="ExternalOutput")
    xn1 = ap("xn1", [8, 128, S_LEN], BF16, kind="ExternalOutput")
    w16 = ap("w16", [S_LEN, 16], F32, kind="ExternalOutput")
    scratch_kind = "ExternalOutput" if dbg_scratch else "Internal"
    if dbg_scratch:
        dbg_xn = ap("dbg_xn", [128, NJ, DM], F32, kind="ExternalOutput")
        dbg_fm = ap("dbg_fm", [128, 8, TT], BF16, kind="ExternalOutput")
        dbg_st = ap("dbg_st", [6, 128, NJ], F32, kind="ExternalOutput")
    wall = ap("wall", [NHC, 3, 128, 1024], BF16, kind=scratch_kind)

    es = ExitStack()
    S = lambda n, *a: es.enter_context(nc.sbuf_tensor("a_" + n, *a))
    P = lambda n, *a: es.enter_context(nc.psum_tensor("a_" + n, *a))

    xbuf = [S("xbuf0", [128, NJ, DM], F32), S("xbuf1", [128, NJ, DM], F32)]
    xn2 = [S("xna", [128, NJ, DM], F32), S("xnb_", [128, NJ, DM], F32)]
    t1 = S("t1", [128, NJ, DM], F32)
    Cc = S("Cc", [128, NJ, DM], F32)
    xnb = S("xnb", [128, NJ, DM], BF16)
    xnb2 = S("xnb2", [128, NJ, DM], BF16)
    xnfm2 = [S("xnfma", [128, 8, TT], BF16), S("xnfmb", [128, 8, TT], BF16)]
    gslf = S("gslf", [128, TT], F32)
    hidb = [S("hidb0", [128, TT], BF16), S("hidb1", [128, TT], BF16)]
    gamb = S("gamb", [128, DM], F32)
    betb = S("betb", [128, DM], F32)
    rbb = S("rbb", [128, 16], F32)
    idxt = S("idxt", [128, 16], F32)
    idxm = S("idxm", [128, 16], F32)
    onesf = S("onesf", [1, 128], F32)
    idbt = S("idbt", [128, 128], BF16)
    sqscr = S("sqscr", [128, NJ, DM], F32)
    dbg_snap = S("dbg_snap", [128, 6, NJ], F32)
    sqmini = S("sqmini", [32, DM], F32)
    st = {k: S("st_" + k, [128, NJ], F32)
          for k in ("msum", "ss", "mean", "var", "rstd", "negmr")}
    st["bn"] = S("st_bn", [128, NJ, 12], F32)
    st["mv"] = S("st_mv", [128, NJ, 2], F32)
    st2 = {k: S("s2_" + k, [128, NJ], F32)
           for k in ("msum", "ss", "mean", "var", "rstd", "negmr")}
    st2["bn"] = S("s2_bn", [128, NJ, 12], F32)
    st2["mv"] = S("s2_mv", [128, NJ, 2], F32)
    sqscr2 = S("sqscr2", [128, NJ, DM], F32)
    wcb = [S("wcb%d" % i, [128, 3, 1024], BF16) for i in range(4)]
    wgc = [b[:, 0, :].rearrange("p (c h) -> p c h", c=8) for b in wcb]
    wuc = [b[:, 1, :].rearrange("p (c h) -> p c h", c=8) for b in wcb]
    wdc = [b[:, 2, :] for b in wcb]
    wstage = [S("wstage0", [128, 1024], I32), S("wstage1", [128, 1024], I32)]
    wsf = S("wsf", [128, 1024], F32)
    lg_all = S("lg_all", [128, 32, 16], F32)
    r_m1 = S("r_m1", [128, 32], F32)
    r_m2 = S("r_m2", [128, 32], F32)
    r_i = S("r_i", [128, 32], F32)
    r_p = S("r_p", [128, 32], F32)
    r_c = S("r_c", [128, 32, 16], F32)
    r_d = S("r_d", [128, 32, 16], F32)
    rwb = S("rwb", [128, 8, 16], BF16)
    rwf = S("rwf", [128, 8, 16], F32)
    mst = S("mst", [128, DM], BF16)
    wmt = S("wmt", [32, DM], F32)
    xn32 = S("xn32", [32, DM], F32)
    m_q = S("m_q", [32, DM], F32)
    m_k = S("m_k", [32, DM], F32)
    m_kp = S("m_kp", [32, DM], F32)
    m_v = S("m_v", [32, DM], F32)
    m_sc = S("m_sc", [32, DM], F32)
    m_st = S("m_st", [32, 16], F32)
    m_st2 = S("m_st2", [32, 16], F32)
    m_fm = S("m_fm", [128, 8, 32], BF16)
    q_fm = S("q_fm", [128, 8, 32], BF16)
    k_fm = S("k_fm", [128, 8, 32], BF16)
    kp_fm = S("kp_fm", [128, 8, 32], BF16)
    v_fm = S("v_fm", [128, 8, 32], BF16)
    wfm = S("wfm", [128, DM], BF16)
    ptsb = S("ptsb", [128, 8, 32], BF16)
    utsb = S("utsb", [128, 8, 64], BF16)
    z_tm = S("z_tm", [32, DM], BF16)
    zwo = S("zwo", [32, DM], F32)
    wbig = S("wbig", [128, 8, DM], BF16)

    p_tr_full = [P("p_tr0", [128, 1024], BF16), P("p_tr1", [128, 1024], BF16)]
    p_tr = [t[:, 0:128] for t in p_tr_full]
    p_gu = [P("p_gu0", [128, 512], F32), P("p_gu1", [128, 512], F32)]
    p_g = p_gu[0]
    p_d = [P("p_d0", [128, 512], F32), P("p_d1", [128, 512], F32),
           P("p_d2", [128, 512], F32), P("p_d3", [128, 512], F32)]

    with es:
        s.op("sync", lambda e: e.dma_start(out=idbt[:], in_=idb[:]), inc=("ld", 16))
        s.op("sync", lambda e: e.dma_start(out=idxt[:], in_=idx16[:]), inc=("ld", 16))
        s.op("sync", lambda e: e.dma_start(out=wmt[:], in_=wmask[:]), inc=("ld", 16))
        s.op("sync", lambda e: e.dma_start(out=rwf[:], in_=rwT.rearrange("(c p) e -> p c e", p=128)),
             inc=("ld", 16))
        s.op("sync", lambda e: e.dma_start(out=gamb[0:1, :], in_=gam[:]), inc=("ld", 16))
        s.op("sync", lambda e: e.dma_start(out=betb[0:1, :], in_=bet[:]), inc=("ld", 16))
        s.op("sync", lambda e: e.dma_start(out=rbb[0:1, :], in_=rbias[:]), inc=("ld", 16))
        ld_c = s.val("ld")
        if stop_after == "c1":
            return
        s.op("gpsimd", lambda e: e.memset(onesf[:], 1.0), inc=("gp", 1))
        s.op("gpsimd", lambda e: e.memset(Cc[:], 0.0), inc=("gp", 1))
        s.op("gpsimd", lambda e: e.memset(mst[:], 0.0), inc=("gp", 1))
        gp_ms = s.val("gp")
        s.op("vector", lambda v: v.tensor_scalar(idxm[:], idxt[:], 65536.0, None, ALU.subtract),
             waits=[("ld", ld_c)], inc=("ve", 1))
        s.op("vector", lambda v: v.tensor_copy(rwb[:], rwf[:]),
             waits=[("ld", ld_c)], inc=("ve", 1))
        if stop_after == "c2":
            return
        for (tile, psum, pname) in ((gamb, p_d[0], "pd0"), (betb, p_d[1], "pd1")):
            for half in range(2):
                s.op("tensor", lambda t_, tile=tile, psum=psum, half=half: t_.matmul(
                    psum[:], onesf[0:1, :], tile[0:1, half * 512:(half + 1) * 512],
                    start=True, stop=True),
                    waits=[("ld", ld_c), ("gp", gp_ms)] + s.gate(pname + "_free"),
                    inc=("mm", 1))
                s.op("vector", lambda v, tile=tile, psum=psum, half=half: v.tensor_copy(
                    tile[:, half * 512:(half + 1) * 512], psum[:]),
                    waits=[("mm", s.val("mm"))], inc=("ve", 1))
                s.setg(pname + "_free", "ve")
        s.op("tensor", lambda t_: t_.matmul(p_d[2][:, 0:16], onesf[0:1, :], rbb[0:1, 0:16],
                                            start=True, stop=True),
             waits=[("ld", ld_c), ("gp", gp_ms)], inc=("mm", 1))
        s.op("vector", lambda v: v.tensor_copy(rbb[:], p_d[2][:, 0:16]),
             waits=[("mm", s.val("mm"))], inc=("ve", 1))
        s.setg("gb_ready", "ve")
        s.setg("pd0_free", "ve")
        s.setg("pd1_free", "ve")
        s.setg("pd2_free", "ve")

        if stop_after == "cast":
            return
        s.op("sync", lambda e: e.dma_start(out=xn32[:], in_=x[S_LEN - 32:S_LEN, :]),
             inc=("ld", 16))
        ld32 = s.val("ld")
        s.op("vector", lambda v: v.reduce_sum(m_st[:, 0:1], xn32[:], axis=AX.X),
             waits=[("ld", ld32)], inc=("ve", 1))
        s.op("scalar", lambda a: a.activation(sqmini[:], xn32[:], AF.Square,
                                              bias=0.0, scale=1.0),
             waits=[("ld", ld32)], inc=("ac", 1))
        s.op("vector", lambda v: v.reduce_sum(m_st[:, 1:2], sqmini[:], axis=AX.X),
             waits=[("ac", s.val("ac"))], inc=("ve", 1))
        w0 = [("ve", s.val("ve")), ("ac", s.val("ac"))]
        s.op("vector", lambda v: v.tensor_scalar(m_st[:, 2:3], m_st[:, 0:1], 1.0 / DM, None, ALU.mult),
             waits=w0, inc=("ve", 1))
        s.op("vector", lambda v: v.tensor_scalar(m_st[:, 3:4], m_st[:, 1:2], 1.0 / DM, None, ALU.mult),
             waits=[("ve", s.val("ve"))], inc=("ve", 1))
        s.op("vector", lambda v: v.tensor_mul(m_st[:, 4:5], m_st[:, 2:3], m_st[:, 2:3]),
             waits=[("ve", s.val("ve"))], inc=("ve", 1))
        s.op("vector", lambda v: v.tensor_sub(m_st[:, 3:4], m_st[:, 3:4], m_st[:, 4:5]),
             waits=[("ve", s.val("ve"))], inc=("ve", 1))
        s.op("vector", lambda v: v.tensor_scalar(m_st[:, 3:4], m_st[:, 3:4], EPS_LN, None, ALU.add),
             waits=[("ve", s.val("ve"))], inc=("ve", 1))
        s.op("scalar", lambda a: a.activation(m_st[:, 5:6], m_st[:, 3:4], AF.Sqrt, bias=0.0, scale=1.0),
             waits=[("ve", s.val("ve"))], inc=("ac", 1))
        s.op("vector", lambda v: v.reciprocal(m_st[:, 5:6], m_st[:, 5:6]),
             waits=[("ac", s.val("ac"))], inc=("ve", 1))
        s.op("vector", lambda v: v.tensor_mul(m_st[:, 6:7], m_st[:, 2:3], m_st[:, 5:6]),
             waits=[("ve", s.val("ve"))], inc=("ve", 1))
        s.op("vector", lambda v: v.tensor_scalar(m_st[:, 6:7], m_st[:, 6:7], -1.0, None, ALU.mult),
             waits=[("ve", s.val("ve"))], inc=("ve", 1))
        s.op("scalar", lambda a: a.activation(xn32[:], xn32[:], AF.Identity,
                                              bias=m_st[:, 6:7], scale=m_st[:, 5:6]),
             waits=[("ve", s.val("ve"))], inc=("ac", 1))
        ac_xn32 = s.val("ac")

        def mini_tr(src_f32_or_bf16, rows, out_fm, nchunk=8, outrows=None):
            outrows = outrows or rows
            s.op("vector", lambda v: v.tensor_copy(mst[0:rows, :], src_f32_or_bf16),
                 waits=s.gate("mst_free"), inc=("ve", 1))
            ve_stg = s.val("ve")
            for dc in range(nchunk):
                pt = p_tr[dc % 2]
                s.op("tensor", lambda t_, dc=dc, pt=pt: t_.transpose(
                    pt[:], mst[:, dc * 128:(dc + 1) * 128], idbt[:]),
                    waits=[("ve", ve_stg)] + s.gate("ptr%d_free" % (dc % 2)) + [("ld", ld_c)],
                    inc=("mm", 1))
                s.op("vector", lambda v, dc=dc, pt=pt: v.tensor_copy(
                    out_fm[:, dc, 0:outrows], pt[:, 0:outrows]),
                    waits=[("mm", s.val("mm"))], inc=("ve", 1))
                s.setg("ptr%d_free" % (dc % 2), "ve")
            s.setg("mst_free", "ve")

        s.op("vector", lambda v: v.tensor_copy(mst[0:32, :], xn32[:]),
             waits=[("ac", ac_xn32), ("gp", gp_ms)], inc=("ve", 1))
        ve_stg = s.val("ve")
        for dc in range(8):
            pt = p_tr[dc % 2]
            s.op("tensor", lambda t_, dc=dc, pt=pt: t_.transpose(
                pt[:], mst[:, dc * 128:(dc + 1) * 128], idbt[:]),
                waits=[("ve", ve_stg)] + s.gate("ptr%d_free" % (dc % 2)) + [("ld", ld_c)],
                inc=("mm", 1))
            s.op("vector", lambda v, dc=dc, pt=pt: v.tensor_copy(m_fm[:, dc, :], pt[:, 0:32]),
                 waits=[("mm", s.val("mm"))], inc=("ve", 1))
            s.setg("ptr%d_free" % (dc % 2), "ve")
        s.setg("mst_free", "ve")

        for (whbm, out_tm, out_fm, do_norm) in ((wq, m_q, q_fm, True), (wk, m_k, k_fm, True),
                                                (wkp, m_kp, kp_fm, True), (wv, m_v, v_fm, False)):
            for dc in range(8):
                s.op("sync", lambda e, whbm=whbm, dc=dc: e.dma_start(
                    out=wsf[:], in_=whbm.rearrange("(c p) f -> p c f", p=128)[:, dc, :]),
                    waits=s.gate("wsf_free"), inc=("ld", 16))
                s.op("vector", lambda v, dc=dc: v.tensor_copy(wbig[:, dc, :], wsf[:]),
                     waits=[("ld", s.val("ld"))] + s.gate("wbig_free"), inc=("ve", 1))
                s.setg("wsf_free", "ve")
            ve_w = s.val("ve")
            for half in range(2):
                pp = p_d[2 + half]
                for dc in range(8):
                    s.op("tensor", lambda t_, dc=dc, half=half, pp=pp: t_.matmul(
                        pp[0:32, :], m_fm[:, dc, :], wbig[:, dc, half * 512:(half + 1) * 512],
                        start=(dc == 0), stop=(dc == 7)),
                        waits=[("ve", ve_w)] + s.gate("pd%d_free" % (2 + half)),
                        inc=("mm", 1) if dc == 7 else None)
                s.op("vector", lambda v, half=half, pp=pp, out_tm=out_tm: v.tensor_copy(
                    out_tm[:, half * 512:(half + 1) * 512], pp[0:32, :]),
                    waits=[("mm", s.val("mm"))], inc=("ve", 1))
                s.setg("pd%d_free" % (2 + half), "ve")
            s.setg("wbig_free", "ve")
            ve_proj = s.val("ve")
            if do_norm:
                s.op("scalar", lambda a, out_tm=out_tm: a.activation(
                    sqmini[:], out_tm[:], AF.Square, bias=0.0, scale=1.0),
                    waits=[("ve", ve_proj)], inc=("ac", 1))
                s.op("vector", lambda v: v.reduce_sum(
                    m_st[:, 0:16], sqmini[:].rearrange("q (h d) -> q h d", h=NH), axis=AX.X),
                    waits=[("ac", s.val("ac"))], inc=("ve", 1))
                s.op("scalar", lambda a: a.activation(m_st2[:], m_st[:], AF.Sqrt, bias=0.0, scale=1.0),
                     waits=[("ve", s.val("ve"))], inc=("ac", 1))
                s.op("vector", lambda v: v.tensor_scalar(m_st2[:], m_st2[:], EPS_NORM, None, ALU.add),
                     waits=[("ac", s.val("ac"))], inc=("ve", 1))
                s.op("vector", lambda v: v.reciprocal(m_st2[:], m_st2[:]),
                     waits=[("ve", s.val("ve"))], inc=("ve", 1))
                ve_rn = s.val("ve")
                for h in range(NH):
                    s.op("vector", lambda v, h=h, out_tm=out_tm: v.tensor_scalar(
                        out_tm[:, h * 64:(h + 1) * 64], out_tm[:, h * 64:(h + 1) * 64],
                        m_st2[:, h:h + 1], None, ALU.mult),
                        waits=[("ve", ve_rn)],
                        inc=("ve", 1) if h == NH - 1 else None)
            mini_tr(out_tm[:], 32, out_fm)

        for h in range(NH):
            cc, ro = h // 2, 64 * (h % 2)
            qf = q_fm[:, cc, :][ro:ro + 64, :]
            kf = k_fm[:, cc, :][ro:ro + 64, :]
            kpf = kp_fm[:, cc, :][ro:ro + 64, :]
            vf = v_fm[:, cc, :][ro:ro + 64, :]
            kv = kf.rearrange("d (j u) -> d j u", u=1).broadcast_to([64, 32, 32])
            kpv = kpf.rearrange("d (u k) -> d u k", u=1).broadcast_to([64, 32, 32])
            wfm_h = wfm[ro:ro + 64, :]
            s.op("vector", lambda v, kv=kv, kpv=kpv, wfm_h=wfm_h: v.tensor_tensor(
                wfm_h.rearrange("d (j k) -> d j k", k=32), kv, kpv, ALU.mult),
                waits=s.gate("wfm_free"), inc=("ve", 1))
            ve_wfm = s.val("ve")
            for half in range(2):
                pp = p_d[half]
                s.op("tensor", lambda t_, half=half, pp=pp, qf=qf, wfm_h=wfm_h: t_.matmul(
                    pp[0:32, :], qf, wfm_h[:, half * 512:(half + 1) * 512],
                    start=True, stop=True),
                    waits=[("ve", ve_wfm)] + s.gate("pd%d_free" % half),
                    inc=("mm", 1))
                s.op("vector", lambda v, half=half, pp=pp: v.scalar_tensor_tensor(
                    m_sc[:, half * 512:(half + 1) * 512], pp[0:32, :], 0.125,
                    wmt[:, half * 512:(half + 1) * 512], ALU.mult, ALU.add),
                    waits=[("mm", s.val("mm")), ("ld", ld_c)], inc=("ve", 1))
                s.setg("pd%d_free" % half, "ve")
            s.op("vector", lambda v: v.reduce_max(m_st[:, 0:1], m_sc[:], axis=AX.X),
                 inc=("ve", 1))
            s.op("vector", lambda v: v.tensor_scalar(m_st[:, 1:2], m_st[:, 0:1], -1.0, None, ALU.mult),
                 waits=[("ve", s.val("ve"))], inc=("ve", 1))
            s.op("scalar", lambda a: a.activation(m_sc[:], m_sc[:], AF.Exp,
                                                  bias=m_st[:, 1:2], scale=1.0),
                 waits=[("ve", s.val("ve"))], inc=("ac", 1))
            s.op("vector", lambda v: v.reduce_sum(m_st[:, 2:3], m_sc[:], axis=AX.X),
                 waits=[("ac", s.val("ac"))], inc=("ve", 1))
            s.op("vector", lambda v: v.reciprocal(m_st[:, 3:4], m_st[:, 2:3]),
                 waits=[("ve", s.val("ve"))], inc=("ve", 1))
            ve_recip = s.val("ve")
            mini_tr(m_sc[:], 32, ptsb)
            vv1 = vf.rearrange("d (j u) -> d j u", u=1).broadcast_to([64, 32, 32])
            vv2 = vf.rearrange("d (u k) -> d u k", u=1).broadcast_to([64, 32, 32])
            s.op("vector", lambda v, vv1=vv1, vv2=vv2, wfm_h=wfm_h: v.tensor_tensor(
                wfm_h.rearrange("d (j k) -> d j k", k=32), vv1, vv2, ALU.mult),
                inc=("ve", 1))
            mini_tr(wfm_h, 64, utsb)
            s.setg("wfm_free", "ve")
            pz = p_d[2]
            for dc in range(8):
                s.op("tensor", lambda t_, dc=dc, pz=pz: t_.matmul(
                    pz[0:32, 0:64], ptsb[:, dc, :], utsb[:, dc, :],
                    start=(dc == 0), stop=(dc == 7)),
                    waits=[("ve", s.val("ve"))] + s.gate("pd2_free"),
                    inc=("mm", 1) if dc == 7 else None)
            s.op("vector", lambda v, h=h, pz=pz: v.tensor_scalar(
                z_tm[:, h * 64:(h + 1) * 64], pz[0:32, 0:64], m_st[:, 3:4], None, ALU.mult),
                waits=[("mm", s.val("mm")), ("ve", ve_recip)], inc=("ve", 1))
            s.setg("pd2_free", "ve")
        for dc in range(8):
            s.op("sync", lambda e, dc=dc: e.dma_start(
                out=wsf[:], in_=wo.rearrange("(c p) f -> p c f", p=128)[:, dc, :]),
                waits=s.gate("wsf_free"), inc=("ld", 16))
            s.op("vector", lambda v, dc=dc: v.tensor_copy(wbig[:, dc, :], wsf[:]),
                 waits=[("ld", s.val("ld"))] + s.gate("wbig_free"), inc=("ve", 1))
            s.setg("wsf_free", "ve")
        ve_wo = s.val("ve")
        mini_tr(z_tm[:], 32, m_fm)
        ve_zfm = s.val("ve")
        for half in range(2):
            pp = p_d[half]
            for dc in range(8):
                s.op("tensor", lambda t_, dc=dc, half=half, pp=pp: t_.matmul(
                    pp[0:32, :], m_fm[:, dc, :], wbig[:, dc, half * 512:(half + 1) * 512],
                    start=(dc == 0), stop=(dc == 7)),
                    waits=[("ve", ve_wo), ("ve", ve_zfm)] + s.gate("pd%d_free" % half),
                    inc=("mm", 1) if dc == 7 else None)
            s.op("vector", lambda v, half=half, pp=pp: v.tensor_copy(
                zwo[:, half * 512:(half + 1) * 512], pp[0:32, :]),
                waits=[("mm", s.val("mm"))], inc=("ve", 1))
            s.setg("pd%d_free" % half, "ve")
        s.op("gpsimd", lambda e: e.dma_start(out=Cc[96:128, NJ - 1, :], in_=zwo[:]),
             waits=[("ve", s.val("ve"))], inc=("cst", 16))
        cc_done = s.val("cst")

        if stop_after == "mini":
            return
        fmready = {}
        xnready = {}
        xnbready = {}
        sdone = {}
        ldx_of = {}
        so_xn1 = {}

        def emit_load_x(tt):
            xb = xbuf[tt % 2]
            s.op("sync", lambda e, tt=tt, xb=xb: e.dma_start(
                out=xb[:], in_=x.rearrange("(t j p) d -> t p j d", j=NJ, p=128)[tt]),
                waits=s.gate("xb%d_free" % (tt % 2)), inc=("ld", 16))
            ldx_of[tt] = s.val("ld")

        def emit_ln0(tt):
            xb = xbuf[tt % 2]
            _ln_stats(s, xb[:], NJ, st2, sqscr2, [("ld", ldx_of[tt])], gkey="sq2_free")
            ve_rs = s.val("ve")
            for j in range(NJ):
                s.op("scalar", lambda a, j=j, xb=xb, tt=tt: a.activation(
                    xn2[tt % 2][:, j, :], xb[:, j, :], AF.Identity,
                    bias=st2["negmr"][:, j:j + 1], scale=st2["rstd"][:, j:j + 1]),
                    waits=[("ve", ve_rs)] + s.gate("xn%d_free" % (tt % 2)), inc=("ac", 1))
            xnready[tt] = s.val("ac")
            s.op("vector", lambda v, tt=tt: v.tensor_copy(xnb2[:], xn2[tt % 2][:]),
                 waits=[("ac", xnready[tt])] + s.gate("xnb2_free"), inc=("ve", 1))
            xnbready[tt] = s.val("ve")

        def emit_tr0(tt):
            w_store = [("so", so_xn1[tt - 2])] if (tt - 2) in so_xn1 else []
            for jj in range(NJ):
                for dc in range(8):
                    pt = p_tr[(jj * 8 + dc) % 2]
                    s.op("tensor", lambda t_, jj=jj, dc=dc, pt=pt, tt=tt: t_.transpose(
                        pt[:], xnb2[:, jj, dc * 128:(dc + 1) * 128], idbt[:]),
                        waits=[("ve", xnbready[tt])] + s.gate("ptr%d_free" % ((jj * 8 + dc) % 2)),
                        inc=("mm", 1))
                    s.op("vector", lambda v, jj=jj, dc=dc, pt=pt, tt=tt: v.tensor_copy(
                        xnfm2[tt % 2][:, dc, jj * 128:(jj + 1) * 128], pt[:]),
                        waits=w_store + [("mm", s.val("mm"))], inc=("ve", 1))
                    s.setg("ptr%d_free" % ((jj * 8 + dc) % 2), "ve")
            fmready[tt] = s.val("ve")
            s.setg("xnb2_free", "ve")

        def emit_attnln(tt):
            xb = xbuf[tt % 2]
            xnt = xn2[tt % 2]
            if tt == NT0 - 1:
                s.op("vector", lambda v, xnt=xnt: v.tensor_add(xnt[:], xnt[:], Cc[:]),
                     waits=[("cst", cc_done), ("ac", xnready[tt])], inc=("ve", 1))
            _ln_stats(s, xnt[:], NJ, st, sqscr,
                      [("ve", s.val("ve")), ("ac", xnready[tt])], gkey="sq1_free")
            ve_rs2 = s.val("ve")
            for j in range(NJ):
                s.op("scalar", lambda a, j=j, xnt=xnt: a.activation(
                    t1[:, j, :], xnt[:, j, :], AF.Identity,
                    bias=st["negmr"][:, j:j + 1], scale=st["rstd"][:, j:j + 1]),
                    waits=[("ve", ve_rs2)], inc=("ac", 1))
            ac_t1 = s.val("ac")
            s.setg("xn%d_free" % (tt % 2), "ac")
            gv = _bc(gamb[:], NJ)
            bv = _bc(betb[:], NJ)
            s.op("vector", lambda v, gv=gv: v.tensor_tensor(t1[:], t1[:], gv, ALU.mult),
                 waits=[("ac", ac_t1)] + s.gate("gb_ready"))
            s.op("vector", lambda v, bv=bv: v.tensor_tensor(t1[:], t1[:], bv, ALU.add))
            s.op("vector", lambda v, xb=xb: v.tensor_add(xb[:], xb[:], t1[:]), inc=("ve", 1))
            sdone[tt] = s.val("ve")

        def emit_tile_end(tt, mm_down):
            xb = xbuf[tt % 2]
            for mt in range(NJ):
                for half in range(2):
                    pi = mt * 2 + half
                    s.op("vector", lambda v, mt=mt, half=half, pi=pi, xb=xb:
                         v.scalar_tensor_tensor(
                             xb[:, mt, half * 512:(half + 1) * 512], p_d[pi][:],
                             SCL * SCL, xb[:, mt, half * 512:(half + 1) * 512],
                             ALU.mult, ALU.add),
                         waits=[("ve", sdone[tt]), ("mm", mm_down)],
                         inc=("ve", 1) if pi == 3 else None)
            for pi in range(4):
                s.setg("pd%d_free" % pi, "ve")
            ve_x1 = s.val("ve")
            s.op("gpsimd", lambda e, tt=tt, xb=xb: e.dma_start(
                out=x1.rearrange("(t j p) d -> t p j d", j=NJ, p=128)[tt], in_=xb[:]),
                waits=[("ve", ve_x1)], inc=("so", 16))
            _ln_stats(s, xb[:], NJ, st, sqscr, [("ve", ve_x1), ("ac", s.val("ac"))],
                      gkey="sq1_free")
            ve_rs3 = s.val("ve")
            for j in range(NJ):
                s.op("scalar", lambda a, j=j, xb=xb: a.activation(
                    xnb[:, j, :], xb[:, j, :], AF.Identity,
                    bias=st["negmr"][:, j:j + 1], scale=st["rstd"][:, j:j + 1]),
                    waits=[("ve", ve_rs3)], inc=("ac", 1))
            ac_xn1 = s.val("ac")
            for jj in range(NJ):
                for dc in range(8):
                    pt = p_tr[(jj * 8 + dc) % 2]
                    s.op("tensor", lambda t_, jj=jj, dc=dc, pt=pt: t_.transpose(
                        pt[:], xnb[:, jj, dc * 128:(dc + 1) * 128], idbt[:]),
                        waits=[("ac", ac_xn1)] + s.gate("ptr%d_free" % ((jj * 8 + dc) % 2)),
                        inc=("mm", 1))
                    s.op("vector", lambda v, jj=jj, dc=dc, pt=pt, tt=tt: v.tensor_copy(
                        xnfm2[tt % 2][:, dc, jj * 128:(jj + 1) * 128], pt[:]),
                        waits=[("mm", s.val("mm"))], inc=("ve", 1))
                    s.setg("ptr%d_free" % ((jj * 8 + dc) % 2), "ve")
            ve_fm1 = s.val("ve")
            s.op("gpsimd", lambda e, tt=tt: e.dma_start(
                out=xn1[:, :, tt * TT:(tt + 1) * TT].rearrange("c p u -> p c u"),
                in_=xnfm2[tt % 2][:]),
                waits=[("ve", ve_fm1)], inc=("so", 16))
            so_xn1[tt] = s.val("so")
            for mt in range(NJ):
                mtg = tt * NJ + mt
                for dc in range(8):
                    s.op("tensor", lambda t_, mt=mt, dc=dc, tt=tt: t_.matmul(
                        p_gu[mt % 2][:, 0:16], xnfm2[tt % 2][:, dc, mt * 128:(mt + 1) * 128],
                        rwb[:, dc, :],
                        start=(dc == 0), stop=(dc == 7)),
                        waits=[("ve", ve_fm1)] + s.gate("pgu%d_free" % (mt % 2)),
                        inc=("mm", 1) if dc == 7 else None)
                s.op("vector", lambda v, mtg=mtg, mt=mt: v.tensor_tensor(
                    lg_all[:, mtg, :], p_gu[mt % 2][:, 0:16], rbb[:], ALU.add),
                    waits=[("mm", s.val("mm"))], inc=("ve", 1))
                s.setg("pgu%d_free" % (mt % 2), "ve")
            s.g["xb%d_free" % (tt % 2)] = ("so", s.val("so"))

        emit_load_x(0)
        emit_ln0(0)
        emit_tr0(0)
        for t in range(ntiles):
            if t + 1 < ntiles:
                emit_load_x(t + 1)
            ve_fm = fmready[t]
            xnfm = xnfm2[t % 2]
            def emit_down_l0(pend):
                hcp, ve_hidp = pend
                wbp = hcp % 4
                hbp = hcp % 2
                for mt in range(NJ):
                    for half in range(2):
                        pi = mt * 2 + half
                        last = (mt == NJ - 1 and half == 1)
                        s.op("tensor", lambda t_, mt=mt, half=half, pi=pi, hcp=hcp, wbp=wbp, hbp=hbp: t_.matmul(
                            p_d[pi][:], hidb[hbp][:, mt * 128:(mt + 1) * 128],
                            wdc[wbp][:, half * 512:(half + 1) * 512],
                            start=(hcp == 0), stop=(hcp == NHC - 1)),
                            waits=([("ve", ve_hidp)] +
                                   (s.gate("pd0_free", "pd1_free", "pd2_free", "pd3_free")
                                    if hcp == 0 else [])),
                            inc=("mm", 1) if last else None)
                s.setg("hid%d_free" % hbp, "mm")
                s.setg("wbuf%d_free" % wbp, "mm")

            pend_down = None
            for hc in range(NHC):
                wb = hc % 4
                if hc == 2:
                    emit_attnln(t)
                if hc == 4 and t + 1 < ntiles:
                    emit_ln0(t + 1)
                if hc == 6 and t + 1 < ntiles:
                    emit_tr0(t + 1)
                if t == 0:
                    for wi, wsrcT in enumerate((wgT, wuT, wdT)):
                        sj = (hc * 3 + wi) % 2
                        if wi < 2:
                            src = wsrcT.rearrange("(c p) h -> p c h", p=128)[:, :, hc * 128:(hc + 1) * 128]
                            dst = wstage[sj][:].rearrange("p (c h) -> p c h", c=8)
                        else:
                            src = wsrcT[hc * 128:(hc + 1) * 128, :]
                            dst = wstage[sj][:]
                        s.op("sync", lambda e, src=src, dst=dst: e.dma_start(out=dst, in_=src),
                             waits=s.gate("wstage%d_free" % sj), inc=("mld", 16))
                        cdst = wcb[wb][:, wi, :]
                        s.op("vector", lambda v, cdst=cdst, sj=sj: v.tensor_copy(cdst, wstage[sj][:]),
                             waits=[("mld", s.val("mld")), ("wst", s.val("wst"))] +
                                   (s.gate("wbuf%d_free" % wb) if wi == 0 else []),
                             inc=("ve", 1))
                        s.setg("wstage%d_free" % sj, "ve")
                        s.op("gpsimd", lambda e, cdst=cdst, wi=wi, hc=hc: e.dma_start(
                            out=wall[hc, wi], in_=cdst),
                            waits=[("ve", s.val("ve"))], inc=("wst", 16))
                    ve_wcast = s.val("ve")
                    mm_waits = [("ve", ve_wcast), ("ve", ve_fm)]
                else:
                    wait_w = [("wst", s.val("wst"))] if t == 1 and hc < 2 else []
                    s.op("sync", lambda e, hc=hc, wb=wb: e.dma_start(
                        out=wcb[wb][:], in_=wall[hc].rearrange("w p f -> p w f")),
                        waits=wait_w + s.gate("wbuf%d_free" % wb), inc=("mld", 16))
                    mm_waits = [("mld", s.val("mld")), ("ve", ve_fm)]
                gu = p_gu[hc % 2]
                for dc in range(8):
                    s.op("tensor", lambda t_, dc=dc, wb=wb, xnfm=xnfm, gu=gu: t_.matmul(
                        gu[:, 0:TT], wgc[wb][:, dc, :], xnfm[:, dc, :],
                        start=(dc == 0), stop=(dc == 7)),
                        waits=mm_waits + s.gate("pgu%d_free" % (hc % 2)),
                        inc=("mm", 1) if dc == 7 else None)
                mm_g = s.val("mm")
                for dc in range(8):
                    s.op("tensor", lambda t_, dc=dc, wb=wb, xnfm=xnfm, gu=gu: t_.matmul(
                        gu[:, TT:2 * TT], wuc[wb][:, dc, :], xnfm[:, dc, :],
                        start=(dc == 0), stop=(dc == 7)),
                        inc=("mm", 1) if dc == 7 else None)
                mm_u = s.val("mm")
                s.op("scalar", lambda a, gu=gu: a.activation(gslf[:], gu[:, 0:TT], AF.Silu,
                                                             bias=0.0, scale=SCL),
                     waits=[("mm", mm_u)] + s.gate("gslf_free"), inc=("ac", 1))
                s.op("vector", lambda v, hb=hc % 2, gu=gu: v.tensor_tensor(
                    hidb[hb][:], gslf[:], gu[:, TT:2 * TT], ALU.mult),
                     waits=[("ac", s.val("ac")), ("mm", mm_u)] + s.gate("hid%d_free" % (hc % 2)),
                     inc=("ve", 1))
                s.setg("pgu%d_free" % (hc % 2), "ve")
                s.setg("gslf_free", "ve")
                ve_hid = s.val("ve")
                if pend_down is not None:
                    emit_down_l0(pend_down)
                pend_down = (hc, ve_hid)
            emit_down_l0(pend_down)
            mm_down = s.val("mm")
            emit_tile_end(t, mm_down)

        if stop_after == "tiles":
            return
        ve_lg = s.val("ve")
        iv = _bc(idxt[:], 32)
        imv = _bc(idxm[:], 32)
        bc32 = lambda m: m[:].rearrange("p (m u) -> p m u", u=1).broadcast_to([128, 32, 16])
        def vg(emit, extra=()):
            s.op("vector", emit, waits=[("ve", s.val("ve"))] + list(extra), inc=("ve", 1))
        vg(lambda v: v.reduce_max(r_m1[:], lg_all[:], axis=AX.X), [("ve", ve_lg)])
        vg(lambda v: v.tensor_tensor(r_c[:], lg_all[:], bc32(r_m1), ALU.is_equal))
        vg(lambda v: v.tensor_tensor(r_c[:], r_c[:], imv, ALU.mult))
        vg(lambda v: v.tensor_scalar(r_c[:], r_c[:], 65536.0, None, ALU.add))
        vg(lambda v: v.tensor_reduce(r_i[:], r_c[:], axis=AX.X, op=ALU.min))
        vg(lambda v: v.tensor_tensor(r_c[:], iv, bc32(r_i), ALU.is_equal))
        vg(lambda v: v.scalar_tensor_tensor(r_d[:], r_c[:], -1e30, lg_all[:],
                                            ALU.mult, ALU.add))
        vg(lambda v: v.reduce_max(r_m2[:], r_d[:], axis=AX.X))
        vg(lambda v: v.tensor_tensor(r_d[:], r_d[:], bc32(r_m2), ALU.is_equal))
        vg(lambda v: v.tensor_tensor(r_d[:], r_d[:], imv, ALU.mult))
        vg(lambda v: v.tensor_scalar(r_d[:], r_d[:], 65536.0, None, ALU.add))
        vg(lambda v: v.tensor_reduce(r_i[:], r_d[:], axis=AX.X, op=ALU.min))
        vg(lambda v: v.tensor_tensor(r_d[:], iv, bc32(r_i), ALU.is_equal))
        vg(lambda v: v.tensor_sub(r_p[:], r_m1[:], r_m2[:]))
        s.op("scalar", lambda a: a.activation(r_p[:], r_p[:], AF.Sigmoid, bias=0.0, scale=1.0),
             waits=[("ve", s.val("ve"))], inc=("ac", 1))
        vg(lambda v: v.tensor_tensor(r_c[:], r_c[:], bc32(r_p), ALU.mult),
           [("ac", s.val("ac"))])
        vg(lambda v: v.tensor_scalar(r_p[:], r_p[:], -1.0, 1.0, ALU.mult, ALU.add))
        vg(lambda v: v.tensor_tensor(r_d[:], r_d[:], bc32(r_p), ALU.mult))
        vg(lambda v: v.tensor_add(r_c[:], r_c[:], r_d[:]))
        s.op("gpsimd", lambda e: e.dma_start(
            out=w16.rearrange("(m p) e -> p m e", p=128), in_=r_c[:]),
            waits=[("ve", s.val("ve"))], inc=("so", 16))

        return




def emit_moe(nc, s, dram):
    ap = dram
    xn1 = ap("xn1", [8, 128, S_LEN], BF16, kind="ExternalInput")
    w16 = ap("w16", [S_LEN, 16], F32, kind="ExternalInput")
    eup = ap("eup", [NEXP, DM, EDIM], F32, kind="ExternalInput")
    egt = ap("egt", [NEXP, DM, EDIM], F32, kind="ExternalInput")
    edn = ap("edn", [NEXP, EDIM, DM], F32, kind="ExternalInput")
    shgT = ap("shgT", [DM, DM], F32, kind="ExternalInput")
    shuT = ap("shuT", [DM, DM], F32, kind="ExternalInput")
    shdT = ap("shdT", [DM, DM], F32, kind="ExternalInput")
    moe = ap("moe", [S_LEN, DM], F32, kind="ExternalOutput")
    shb = [ap("shb%d" % i, [8, 128, 1024], BF16, kind="Internal") for i in range(3)]

    es = ExitStack()
    S = lambda n, *a: es.enter_context(nc.sbuf_tensor("b_" + n, *a))
    P = lambda n, *a: es.enter_context(nc.psum_tensor("b_" + n, *a))

    HTOK = S_LEN // 2
    NJM = HTOK // 128
    xh = S("xh", [128, 8, HTOK], BF16)
    acc = S("acc", [128, NJM, DM], F32)
    w16h = S("w16h", [128, NJM, 16], F32)
    gateb = [S("gateb%d" % i, [128, 8, EDIM], BF16) for i in range(2)]
    upb = [S("upb%d" % i, [128, 8, EDIM], BF16) for i in range(2)]
    dnb = [S("dnb%d" % i, [128, 4, DM], BF16) for i in range(2)]
    wstgB = [S("wstgB%d" % i, [128, 2048], F32) for i in range(2)]
    hid = [S("hid0", [128, 4, 512], BF16), S("hid1", [128, 4, 512], BF16)]
    scr = S("scr", [128, 512], F32)
    shw = [S("shw%d_%d" % (i, b), [128, 8, 128], BF16) for i in range(2) for b in range(2)]
    shd = [S("shd%d" % b, [128, 1024], BF16) for b in range(2)]
    shstage = S("shstage", [128, 1024], F32)
    shstg_b = S("shstg_b", [128, 1024], BF16)
    hsh = [S("hsha", [128, 256], BF16), S("hshb", [128, 256], BF16)]
    scrsh = S("scrsh", [128, 256], F32)
    p_g2 = [P("p_ga", [128, 512], F32), P("p_gb", [128, 512], F32)]
    p_u2 = [P("p_ua", [128, 512], F32), P("p_ub", [128, 512], F32)]
    p_d = [P("p_d0m", [128, 512], F32), P("p_d1m", [128, 512], F32),
           P("p_d2m", [128, 512], F32), P("p_d3m", [128, 512], F32)]

    with es:
        for wi, wsrc in enumerate((shgT, shuT, shdT)):
            for hc in range(8):
                if wi < 2:
                    src_ap = wsrc.rearrange("(c p) h -> p c h", p=128)[:, :, hc * 128:(hc + 1) * 128]
                    dst_ap = shstage[:].rearrange("p (c h) -> p c h", c=8)
                else:
                    src_ap = wsrc[hc * 128:(hc + 1) * 128, :]
                    dst_ap = shstage[:]
                s.op("sync", lambda e, src_ap=src_ap, dst_ap=dst_ap: e.dma_start(
                    out=dst_ap, in_=src_ap),
                    waits=s.gate("shstage_free"), inc=("ld", 16))
                s.op("vector", lambda v: v.tensor_copy(shstg_b[:], shstage[:]),
                     waits=[("ld", s.val("ld"))] + s.gate("shstgb_free"), inc=("ve", 1))
                s.setg("shstage_free", "ve")
                s.op("gpsimd", lambda e, wi=wi, hc=hc: e.dma_start(
                    out=shb[wi][hc], in_=shstg_b[:]),
                    waits=[("ve", s.val("ve"))], inc=("wst", 16))
                s.setg("shstgb_free", "wst")
        wst_cast = s.val("wst")

        pend_moe = [None]

        def emit_down_moe(pend):
            kp, ttp, ve_hidp, wbp, ld_xp, gp_accp = pend
            hb = hid[ttp % 2]
            for mt in range(4):
                jm = ttp * 4 + mt
                for dh in range(2):
                    pb = p_d[(mt * 2 + dh) % 2]
                    for ec in range(4):
                        s.op("tensor", lambda t_, ec=ec, mt=mt, dh=dh, pb=pb, wbp=wbp, hb=hb: t_.matmul(
                            pb[:], hb[:, ec, mt * 128:(mt + 1) * 128],
                            dnb[wbp][:, ec, dh * 512:(dh + 1) * 512],
                            start=(ec == 0), stop=(ec == 3)),
                            waits=[("ve", ve_hidp)] + s.gate("pd%d_free" % ((mt * 2 + dh) % 2)),
                            inc=("mm", 1) if ec == 3 else None)
                    s.op("vector", lambda v, jm=jm, dh=dh, pb=pb, kp=kp: v.scalar_tensor_tensor(
                        acc[:, jm, dh * 512:(dh + 1) * 512], pb[:],
                        w16h[:, jm, kp:kp + 1],
                        acc[:, jm, dh * 512:(dh + 1) * 512], ALU.mult, ALU.add),
                        waits=[("mm", s.val("mm")), ("ld", ld_xp), ("gp", gp_accp)],
                        inc=("ve", 1))
                    s.setg("pd%d_free" % ((mt * 2 + dh) % 2), "ve")
            s.setg("mhid%d_free" % (ttp % 2), "mm")
            if ttp == 3:
                s.setg("dnb%d_free" % wbp, "mm")

        wld_map = {}

        def emit_wload_moe(kk, part):
            wbk = kk % 2
            chunks = [("g", 0), ("g", 1), ("u", 0), ("u", 1), ("d", 0), ("d", 1)]
            sel = chunks[:3] if part == 0 else chunks[3:]
            for jj, (kind, j) in enumerate(sel):
                gi = part * 3 + jj
                sb = gi % 2
                if kind == "g" or kind == "u":
                    wsrc = egt if kind == "g" else eup
                    wdst = gateb if kind == "g" else upb
                    src_ap = wsrc[kk, j * 512:(j + 1) * 512, :].rearrange("(c p) e -> p c e", p=128)
                    stv = wstgB[sb][:].rearrange("p (c e) -> p c e", c=4)
                    dstv = wdst[wbk][:, j * 4:(j + 1) * 4, :]
                    first = (kind == "g" and j == 0) or (kind == "u" and j == 0)
                    gname = ("gateb%d_free" % wbk) if kind == "g" else ("upb%d_free" % wbk)
                else:
                    src_ap = edn[kk, j * 256:(j + 1) * 256, :].rearrange("(c p) d -> p c d", p=128)
                    stv = wstgB[sb][:].rearrange("p (c d) -> p c d", c=2)
                    dstv = dnb[wbk][:, j * 2:(j + 1) * 2, :]
                    first = (j == 0)
                    gname = "dnb%d_free" % wbk
                s.op("sync", lambda e, src_ap=src_ap, stv=stv: e.dma_start(out=stv, in_=src_ap),
                     waits=s.gate("wstgB%d_free" % sb), inc=("ld", 16))
                s.op("vector", lambda v, dstv=dstv, stv=stv: v.tensor_copy(dstv, stv),
                     waits=[("ld", s.val("ld"))] + (s.gate(gname) if first and j == 0 else []),
                     inc=("ve", 1))
                s.setg("wstgB%d_free" % sb, "ve")
            if part == 1:
                wld_map[kk] = [("ld", s.val("ld"))]

        for h in range(2):
            s.op("sync", lambda e, h=h: e.dma_start(
                out=xh[:], in_=xn1[:, :, h * HTOK:(h + 1) * HTOK].rearrange("c p t -> p c t")),
                waits=s.gate("xh_free"), inc=("ld", 16))
            s.op("sync", lambda e, h=h: e.dma_start(
                out=w16h[:], in_=w16.rearrange("(m p) e -> p m e", p=128)[:, h * NJM:(h + 1) * NJM, :]),
                inc=("ld", 16))
            ld_x = s.val("ld")
            s.op("gpsimd", lambda e: e.memset(acc[:], 0.0),
                 waits=s.gate("acc_free"), inc=("gp", 1))
            gp_acc = s.val("gp")
            emit_wload_moe(0, 0)
            emit_wload_moe(0, 1)
            for k in range(NEXP):
                wb = k % 2
                wld = wld_map[k]
                for tt in range(4):
                    t0 = tt * 512
                    hb = hid[tt % 2]
                    for ec in range(4):
                        pg = p_g2[ec % 2]
                        pu = p_u2[ec % 2]
                        for dc in range(8):
                            s.op("tensor", lambda t_, ec=ec, dc=dc, wb=wb, t0=t0, pg=pg: t_.matmul(
                                pg[:], gateb[wb][:, dc, ec * 128:(ec + 1) * 128],
                                xh[:, dc, t0:t0 + 512],
                                start=(dc == 0), stop=(dc == 7)),
                                waits=wld + [("ld", ld_x)] + s.gate("pg%d_free" % (ec % 2)),
                                inc=("mm", 1) if dc == 7 else None)
                        mm_g = s.val("mm")
                        for dc in range(8):
                            s.op("tensor", lambda t_, ec=ec, dc=dc, wb=wb, t0=t0, pu=pu: t_.matmul(
                                pu[:], upb[wb][:, dc, ec * 128:(ec + 1) * 128],
                                xh[:, dc, t0:t0 + 512],
                                start=(dc == 0), stop=(dc == 7)),
                                waits=s.gate("pu%d_free" % (ec % 2)),
                                inc=("mm", 1) if dc == 7 else None)
                        mm_u = s.val("mm")
                        s.op("scalar", lambda a, pg=pg: a.activation(scr[:], pg[:], AF.Silu,
                                                                     bias=0.0, scale=1.0),
                             waits=[("mm", mm_g)] + s.gate("scr_free"), inc=("ac", 1))
                        s.setg("pg%d_free" % (ec % 2), "ac")
                        s.op("vector", lambda v, ec=ec, hb=hb, pu=pu: v.tensor_tensor(
                            hb[:, ec, :], scr[:], pu[:], ALU.mult),
                            waits=[("ac", s.val("ac")), ("mm", mm_u)] + s.gate("mhid%d_free" % (tt % 2)),
                            inc=("ve", 1))
                        s.setg("pu%d_free" % (ec % 2), "ve")
                        s.setg("scr_free", "ve")
                    ve_hid = s.val("ve")
                    if pend_moe[0] is not None:
                        emit_down_moe(pend_moe[0])
                    pend_moe[0] = (k, tt, ve_hid, wb, ld_x, gp_acc)
                    if k + 1 < NEXP and tt < 2:
                        emit_wload_moe(k + 1, tt)
                s.setg("gateb%d_free" % wb, "mm")
                s.setg("upb%d_free" % wb, "mm")
            if pend_moe[0] is not None:
                emit_down_moe(pend_moe[0])
                pend_moe[0] = None

            def emit_down_sh(pend):
                hcp, ve_hshp = pend
                sbp = hcp % 2
                for mt in range(2):
                    for dh in range(2):
                        pi = mt * 2 + dh
                        s.op("tensor", lambda t_, mt=mt, dh=dh, pi=pi, sbp=sbp, hcp=hcp: t_.matmul(
                            p_d[pi][:], hsh[sbp][:, mt * 128:(mt + 1) * 128],
                            shd[sbp][:, dh * 512:(dh + 1) * 512],
                            start=(hcp == 0), stop=(hcp == 7)),
                            waits=[("ve", ve_hshp)] +
                                  (s.gate("pd0_free", "pd1_free", "pd2_free", "pd3_free")
                                   if hcp == 0 else []),
                            inc=("mm", 1) if (mt == 1 and dh == 1) else None)
                s.setg("hsh%d_free" % sbp, "mm")
                s.setg("shwg%d_free" % sbp, "mm")

            for tt in range(8):
                t0 = tt * 256
                pend_sh = None
                for hc in range(8):
                    sb = hc % 2
                    wait_cast = [("wst", wst_cast)] if (h == 0 and tt == 0 and hc < 2) else []
                    s.op("sync", lambda e, hc=hc, sb=sb: e.dma_start(
                        out=shw[0 * 2 + sb][:].rearrange("p a b -> p (a b)"), in_=shb[0][hc]),
                        waits=wait_cast + s.gate("shwg%d_free" % sb), inc=("ld", 16))
                    s.op("sync", lambda e, hc=hc, sb=sb: e.dma_start(
                        out=shw[1 * 2 + sb][:].rearrange("p a b -> p (a b)"), in_=shb[1][hc]),
                        inc=("ld", 16))
                    s.op("sync", lambda e, hc=hc, sb=sb: e.dma_start(
                        out=shd[sb][:], in_=shb[2][hc]),
                        inc=("ld", 16))
                    shld = s.val("ld")
                    gu = p_g2[hc % 2]
                    for dc in range(8):
                        s.op("tensor", lambda t_, dc=dc, sb=sb, t0=t0, gu=gu: t_.matmul(
                            gu[:, 0:256], shw[sb][:, dc, :], xh[:, dc, t0:t0 + 256],
                            start=(dc == 0), stop=(dc == 7)),
                            waits=[("ld", shld)] + s.gate("pg%d_free" % (hc % 2)),
                            inc=("mm", 1) if dc == 7 else None)
                    mm_g = s.val("mm")
                    for dc in range(8):
                        s.op("tensor", lambda t_, dc=dc, sb=sb, t0=t0, gu=gu: t_.matmul(
                            gu[:, 256:512], shw[2 + sb][:, dc, :], xh[:, dc, t0:t0 + 256],
                            start=(dc == 0), stop=(dc == 7)),
                            inc=("mm", 1) if dc == 7 else None)
                    mm_u = s.val("mm")
                    s.op("scalar", lambda a, gu=gu: a.activation(scrsh[:], gu[:, 0:256], AF.Silu,
                                                                 bias=0.0, scale=1.0),
                         waits=[("mm", mm_u)] + s.gate("scrsh_free"), inc=("ac", 1))
                    s.op("vector", lambda v, sb=sb, gu=gu: v.tensor_tensor(
                        hsh[sb][:], scrsh[:], gu[:, 256:512], ALU.mult),
                         waits=[("ac", s.val("ac")), ("mm", mm_u)] + s.gate("hsh%d_free" % sb),
                         inc=("ve", 1))
                    s.setg("pg%d_free" % (hc % 2), "ve")
                    s.setg("scrsh_free", "ve")
                    ve_hsh = s.val("ve")
                    if pend_sh is not None:
                        emit_down_sh(pend_sh)
                    pend_sh = (hc, ve_hsh)
                emit_down_sh(pend_sh)
                mm_shd = s.val("mm")
                for mt in range(2):
                    jm = tt * 2 + mt
                    for dh in range(2):
                        pi = mt * 2 + dh
                        s.op("vector", lambda v, jm=jm, dh=dh, pi=pi: v.tensor_tensor(
                            acc[:, jm, dh * 512:(dh + 1) * 512], p_d[pi][:],
                            acc[:, jm, dh * 512:(dh + 1) * 512], ALU.add),
                            waits=[("mm", mm_shd)],
                            inc=("ve", 1) if pi == 3 else None)
                for pi in range(4):
                    s.setg("pd%d_free" % pi, "ve")
            ve_acc = s.val("ve")
            s.op("gpsimd", lambda e, h=h: e.dma_start(
                out=moe.rearrange("(m p) d -> p m d", p=128)[:, h * NJM:(h + 1) * NJM, :],
                in_=acc[:]),
                waits=[("ve", ve_acc)], inc=("so", 16))
            s.setg("acc_free", "so")
            s.setg("xh_free", "so")

        return




def emit_rope_ops(s, ktmp, krop, tmpa, tmpb, cost, sint, jt):
    cosv = cost[:, jt, :].rearrange("p (u i) -> p u i", u=1).broadcast_to([128, 8, 32])
    sinv = sint[:, jt, :].rearrange("p (u i) -> p u i", u=1).broadcast_to([128, 8, 32])
    tv = ktmp[:].rearrange("p (h i z) -> p h i z", h=8, z=2)
    E = tv[:, :, :, 0]
    O = tv[:, :, :, 1]
    ov = krop[:].rearrange("p (h z i) -> p h z i", h=8, z=2)
    outA = ov[:, :, 0, :]
    outB = ov[:, :, 1, :]
    ta = tmpa[:].rearrange("p (h i) -> p h i", h=8)
    tb = tmpb[:].rearrange("p (h i) -> p h i", h=8)
    def vg(emit):
        s.op("vector", emit, waits=[("ve", s.val("ve"))], inc=("ve", 1))
    vg(lambda v: v.tensor_tensor(ta, E, cosv, ALU.mult))
    vg(lambda v: v.tensor_tensor(tb, O, sinv, ALU.mult))
    vg(lambda v: v.tensor_tensor(outA, ta, tb, ALU.subtract))
    vg(lambda v: v.tensor_tensor(ta, E, sinv, ALU.mult))
    vg(lambda v: v.tensor_tensor(tb, O, cosv, ALU.mult))
    vg(lambda v: v.tensor_tensor(outB, ta, tb, ALU.add))


def emit_rope_pair(s, ktmp2, krop2, tmpa, tmpb, cost, sint, jt):
    cosv = cost[:, jt:jt + 2, :].rearrange("p j (u i) -> p j u i", u=1).broadcast_to([128, 2, 8, 32])
    sinv = sint[:, jt:jt + 2, :].rearrange("p j (u i) -> p j u i", u=1).broadcast_to([128, 2, 8, 32])
    tv = ktmp2[:].rearrange("p (j h i z) -> p j h i z", j=2, h=8, z=2)
    E = tv[:, :, :, :, 0]
    O = tv[:, :, :, :, 1]
    ov = krop2[:].rearrange("p (j h z i) -> p j h z i", j=2, h=8, z=2)
    outA = ov[:, :, :, 0, :]
    outB = ov[:, :, :, 1, :]
    ta = tmpa[:].rearrange("p (j h i) -> p j h i", j=2, h=8)
    tb = tmpb[:].rearrange("p (j h i) -> p j h i", j=2, h=8)
    def vg(emit):
        s.op("vector", emit, waits=[("ve", s.val("ve"))], inc=("ve", 1))
    vg(lambda v: v.tensor_tensor(ta, E, cosv, ALU.mult))
    vg(lambda v: v.tensor_tensor(tb, O, sinv, ALU.mult))
    vg(lambda v: v.tensor_tensor(outA, ta, tb, ALU.subtract))
    vg(lambda v: v.tensor_tensor(ta, E, sinv, ALU.mult))
    vg(lambda v: v.tensor_tensor(tb, O, cosv, ALU.mult))
    vg(lambda v: v.tensor_tensor(outB, ta, tb, ALU.add))


def emit_attn(nc, s, dram):
    ap = dram
    xn1 = ap("xn1", [8, 128, S_LEN], BF16, kind="ExternalInput")
    x1 = ap("x1", [S_LEN, DM], F32, kind="ExternalInput")
    moe = ap("moe", [S_LEN, DM], F32, kind="ExternalInput")
    wqa = ap("wqa", [DM, DM], F32, kind="ExternalInput")
    wka = ap("wka", [DM, DM], F32, kind="ExternalInput")
    wva = ap("wva", [DM, DM], F32, kind="ExternalInput")
    woa = ap("woa", [DM, DM], F32, kind="ExternalInput")
    bq = ap("bq", [1, DM], F32, kind="ExternalInput")
    bk = ap("bk", [1, DM], F32, kind="ExternalInput")
    bv = ap("bv", [1, DM], F32, kind="ExternalInput")
    bo = ap("bo", [1, DM], F32, kind="ExternalInput")
    rcos = ap("rcos", [S_LEN, 32], F32, kind="ExternalInput")
    rsin = ap("rsin", [S_LEN, 32], F32, kind="ExternalInput")
    idb = ap("idb", [128, 128], BF16, kind="ExternalInput")
    dmask = ap("dmask", [4, 128, 512], BF16, kind="ExternalInput")
    out = ap("out", [S_LEN, DM], F32, kind="ExternalOutput")
    att0 = ap("att0", [8, 128, 4, 512], BF16, kind="Internal")

    es = ExitStack()
    S = lambda n, *a: es.enter_context(nc.sbuf_tensor("c_" + n, *a))
    P = lambda n, *a: es.enter_context(nc.psum_tensor("c_" + n, *a))

    NQT = 8
    xn1t = [S("xn1t0", [128, 8, 512], BF16), S("xn1t1", [128, 8, 512], BF16)]
    kfm = S("kfm", [128, 4, S_LEN], BF16)
    vaug = S("vaug", [128, 32, 520], BF16)
    qfm = S("qfm", [128, 4, 512], BF16)
    attg0t = [S("attg0t0", [128, 4, 512], BF16), S("attg0t1", [128, 4, 512], BF16)]
    atile = S("atile", [128, 4, 512], BF16)
    wqb = S("wqb", [128, 8, 512], BF16)
    wkv = S("wkv", [128, 8, 512], BF16)
    wob = S("wob", [128, 8, DM], BF16)
    wstg = [S("wstg0", [128, 512], F32), S("wstg1", [128, 512], F32)]
    cost = S("cost", [128, 32, 32], F32)
    sint = S("sint", [128, 32, 32], F32)
    maskt = S("maskt", [128, 4, 512], BF16)
    ktmp2 = S("ktmp2", [128, 1024], F32)
    krop2 = S("krop2", [128, 1024], F32)
    tmpa = S("tmpa", [128, 512], F32)
    tmpb = S("tmpb", [128, 512], F32)
    krb2 = [S("krb2a", [128, 1024], BF16), S("krb2b", [128, 1024], BF16)]
    sqs2 = S("sqs2", [128, 1024], F32)
    knall = S("knall", [128, 32, 8], F32)
    qnall = S("qnall", [128, 32, 8], F32)
    knpad = S("knpad", [128, 128], BF16)
    mstat = S("mstat", [128, 8], F32)
    negMb = S("negMb", [128, 1], F32)
    rs_t = S("rs_t", [128, 512], F32)
    rb_s = S("rb_s", [64, 512], F32)
    atnb = S("atnb", [64, 512], BF16)
    pbuf = [S("pbuf%d" % i, [128, 512], BF16) for i in range(3)]
    bqb = S("bqb", [128, 512], F32)
    bkb = S("bkb", [128, 512], F32)
    bvb = S("bvb", [128, 512], F32)
    bob = S("bob", [128, DM], F32)
    x1m = S("x1m", [128, DM], F32)
    moem = S("moem", [128, DM], F32)
    base = S("base", [128, DM], F32)
    outt = S("outt", [128, DM], F32)
    onesb = S("onesb", [128, 128], F32)
    idbt = S("idbt", [128, 128], BF16)
    id64 = S("id64", [128, 128], BF16)

    p_sc = [P("p_sc%d" % i, [128, 512], F32) for i in range(3)]
    p_at = P("p_at", [128, 512], F32)
    p_aux = P("p_auxa", [128, 512], F32)
    p_qh = [P("p_qh0", [128, 512], F32), P("p_qh1", [128, 512], F32)]
    p_pr = p_qh[0][:]
    p_tr = P("p_tra", [128, 1024], BF16)

    with es:
        s.op("sync", lambda e: e.dma_start(out=idbt[:], in_=idb[:]), inc=("ld", 16))
        s.op("sync", lambda e: e.dma_start(
            out=cost[:], in_=rcos.rearrange("(j p) i -> p j i", p=128)), inc=("ld", 16))
        s.op("sync", lambda e: e.dma_start(
            out=sint[:], in_=rsin.rearrange("(j p) i -> p j i", p=128)), inc=("ld", 16))
        s.op("sync", lambda e: e.dma_start(
            out=maskt[:], in_=dmask.rearrange("m p f -> p m f")), inc=("ld", 16))
        s.op("sync", lambda e: e.dma_start(out=bob[0:1, :], in_=bo[:]), inc=("ld", 16))
        ld_c = s.val("ld")
        s.op("gpsimd", lambda e: e.memset(onesb[:], 1.0), inc=("gp", 1))
        s.op("gpsimd", lambda e: e.memset(knpad[:], 0.0), inc=("gp", 1))
        s.op("gpsimd", lambda e: e.memset(id64[:], 0.0), inc=("gp", 1))
        gp_c = s.val("gp")
        s.op("vector", lambda v: v.tensor_copy(id64[0:64, 0:64], idbt[0:64, 0:64]),
             waits=[("ld", ld_c), ("gp", gp_c)], inc=("ve", 1))
        for half in range(2):
            s.op("tensor", lambda t_, half=half: t_.matmul(
                p_aux[:], onesb[0:1, :], bob[0:1, half * 512:(half + 1) * 512],
                start=True, stop=True),
                waits=[("ld", ld_c), ("gp", gp_c)] + s.gate("paux_free"), inc=("mm", 1))
            s.op("vector", lambda v, half=half: v.tensor_copy(
                bob[:, half * 512:(half + 1) * 512], p_aux[:]),
                waits=[("mm", s.val("mm"))], inc=("ve", 1))
            s.setg("paux_free", "ve")

        def load_whalf(whbm, dst, tag, gc, ncols=512):
            for dc in range(8):
                sb = dc % 2
                s.op("sync", lambda e, whbm=whbm, dc=dc, sb=sb, gc=gc, ncols=ncols: e.dma_start(
                    out=wstg[sb][:, 0:ncols], in_=whbm[dc * 128:(dc + 1) * 128, gc:gc + ncols]),
                    waits=s.gate("wstg%d_free" % sb), inc=("ld", 16))
                s.op("vector", lambda v, dc=dc, sb=sb, dst=dst, ncols=ncols: v.tensor_copy(
                    dst[:, dc, 0:ncols], wstg[sb][:, 0:ncols]),
                    waits=[("ld", s.val("ld"))] + (s.gate(tag + "_free") if dc == 0 else []),
                    inc=("ve", 1))
                s.setg("wstg%d_free" % sb, "ve")
            s.setg(tag + "_ready", "ve")

        def proj_psum(src_tile, wtile, mt, wait_extra):
            for dc in range(8):
                s.op("tensor", lambda t_, dc=dc, src_tile=src_tile, wtile=wtile, mt=mt: t_.matmul(
                    p_pr[:], src_tile[:, dc, mt * 128:(mt + 1) * 128], wtile[:, dc, 0:512],
                    start=(dc == 0), stop=(dc == 7)),
                    waits=wait_extra + (s.gate("ppr_free") if dc == 0 else []),
                    inc=("mm", 1) if dc == 7 else None)
            return s.val("mm")

        def pmax_to_scalar(src_col_f32, dst_slot):
            s.op("vector", lambda v: v.tensor_copy(knpad[:, 0:1], src_col_f32),
                 waits=[("ve", s.val("ve"))] + s.gate("knpad_free"), inc=("ve", 1))
            s.op("tensor", lambda t_: t_.transpose(p_tr[:, 0:128], knpad[:], idbt[:]),
                 waits=[("ve", s.val("ve")), ("ld", ld_c)] + s.gate("ptr_free"),
                 inc=("mm", 1))
            s.op("vector", lambda v, dst_slot=dst_slot: v.reduce_max(
                mstat[0:1, dst_slot:dst_slot + 1], p_tr[0:1, 0:128], axis=AX.X),
                waits=[("mm", s.val("mm"))], inc=("ve", 1))
            s.setg("ptr_free", "ve")
            s.setg("knpad_free", "ve")

        def emit_tr(pend):
            ac_krbp, fm_dstp, pjp = pend
            kb = krb2[pjp]
            for c in range(4):
                for jj in range(2):
                    sl = c * 2 + jj
                    s.op("tensor", lambda t_, c=c, jj=jj, sl=sl, kb=kb, ac_krbp=ac_krbp: t_.transpose(
                        p_tr[:, sl * 128:(sl + 1) * 128],
                        kb[:, (jj * 4 + c) * 128:(jj * 4 + c + 1) * 128], idbt[:]),
                        waits=[("ac", ac_krbp), ("ld", ld_c)] +
                              (s.gate("ptr_free") if sl == 0 else []),
                        inc=("mm", 1) if sl == 7 else None)
            mm_tr = s.val("mm")
            for c in range(4):
                s.op("vector", lambda v, c=c, fm_dstp=fm_dstp: v.tensor_copy(
                    fm_dstp(c), p_tr[:, (c * 2) * 128:(c * 2 + 2) * 128]),
                    waits=[("mm", mm_tr)], inc=("ve", 1))
            s.setg("ptr_free", "ve")
            s.setg("krb%d_free" % pjp, "ve")

        def prep_pair(xb, wtile, wready_tag, jt, mt, bias_tile, nall, fm_dst,
                      pend_tr, wait_x):
            pj = (jt // 2) % 2
            for half in range(2):
                for dc in range(8):
                    s.op("tensor", lambda t_, dc=dc, half=half, xb=xb, wtile=wtile, mt=mt: t_.matmul(
                        p_qh[half][:],
                        xb[:, dc, (mt + half) * 128:(mt + half + 1) * 128],
                        wtile[:, dc, 0:512],
                        start=(dc == 0), stop=(dc == 7)),
                        waits=(wait_x + s.gate(wready_tag) + s.gate("pq2_free"))
                              if (half == 0 and dc == 0) else [],
                        inc=("mm", 1) if (half == 1 and dc == 7) else None)
            mm_p = s.val("mm")
            for half in range(2):
                s.op("vector", lambda v, half=half: v.tensor_tensor(
                    ktmp2[:, half * 512:(half + 1) * 512], p_qh[half][:],
                    bias_tile[:], ALU.add),
                    waits=[("mm", mm_p)] + (s.gate("ktmp_free") if half == 0 else []),
                    inc=("ve", 1) if half == 1 else None)
            s.setg("pq2_free", "ve")
            ve_ktmp = s.val("ve")
            s.op("scalar", lambda a: a.activation(sqs2[:], ktmp2[:], AF.Square,
                                                  bias=0.0, scale=1.0),
                 waits=[("ve", ve_ktmp)] + s.gate("sqs_free"), inc=("ac", 1))
            s.op("vector", lambda v, jt=jt, nall=nall: v.reduce_sum(
                nall[:, jt:jt + 2, :].rearrange("p j h -> p (j h)"),
                sqs2[:].rearrange("p (g d) -> p g d", g=16), axis=AX.X),
                waits=[("ac", s.val("ac"))], inc=("ve", 1))
            s.setg("sqs_free", "ve")
            emit_rope_pair(s, ktmp2, krop2, tmpa, tmpb, cost, sint, jt)
            s.op("scalar", lambda a, pj=pj: a.activation(krb2[pj][:], krop2[:],
                                                         AF.Identity, bias=0.0, scale=1.0),
                 waits=[("ve", s.val("ve"))] + s.gate("krb%d_free" % pj), inc=("ac", 1))
            s.setg("ktmp_free", "ac")
            ac_krb = s.val("ac")
            if pend_tr[0] is not None:
                emit_tr(pend_tr[0])
            pend_tr[0] = (ac_krb, fm_dst, pj)

        for g in range(2):
            gc = g * 512
            for (bsrc, btile) in ((bq, bqb), (bk, bkb), (bv, bvb)):
                s.op("sync", lambda e, bsrc=bsrc, btile=btile, gc=gc: e.dma_start(
                    out=btile[0:1, :], in_=bsrc[:, gc:gc + 512]),
                    inc=("ld", 16))
                s.op("tensor", lambda t_, btile=btile: t_.matmul(
                    p_aux[:], onesb[0:1, :], btile[0:1, :], start=True, stop=True),
                    waits=[("ld", s.val("ld")), ("gp", gp_c)] + s.gate("paux_free"),
                    inc=("mm", 1))
                s.op("vector", lambda v, btile=btile: v.tensor_copy(btile[:], p_aux[:]),
                     waits=[("mm", s.val("mm"))], inc=("ve", 1))
                s.setg("paux_free", "ve")
            load_whalf(wka, wkv, "wkv", gc)
            s.op("gpsimd", lambda e: e.memset(vaug[:], 1.0),
                 waits=s.gate("vaug_free"), inc=("gp", 1))
            gp_vaug = s.val("gp")

            pend_tr = [None]
            for tt in range(NQT):
                t0 = tt * 512
                xb = xn1t[tt % 2]
                s.op("sync", lambda e, t0=t0, xb=xb: e.dma_start(
                    out=xb[:], in_=xn1[:, :, t0:t0 + 512].rearrange("c p t -> p c t")),
                    waits=s.gate("xn1t%d_free" % (tt % 2)), inc=("ld", 16))
                ld_x = s.val("ld")
                for mt in (0, 2):
                    jt = tt * 4 + mt
                    prep_pair(xb, wkv, "wkv_ready", jt, mt, bkb, knall,
                              (lambda c, jt=jt: kfm[:, c, jt * 128:(jt + 2) * 128]),
                              pend_tr, [("ld", ld_x)])
                s.setg("xn1t%d_free" % (tt % 2), "mm")
            emit_tr(pend_tr[0]); pend_tr[0] = None
            s.op("vector", lambda v: v.reduce_max(
                mstat[:, 2:3], knall[:].rearrange("p a b -> p (a b)"), axis=AX.X),
                waits=[("ve", s.val("ve"))], inc=("ve", 1))
            pmax_to_scalar(mstat[:, 2:3], 0)
            s.setg("wkv_free", "mm")
            load_whalf(wva, wkv, "wkv", gc)
            for tt in range(NQT):
                t0 = tt * 512
                xb = xn1t[tt % 2]
                s.op("sync", lambda e, t0=t0, xb=xb: e.dma_start(
                    out=xb[:], in_=xn1[:, :, t0:t0 + 512].rearrange("c p t -> p c t")),
                    waits=s.gate("xn1t%d_free" % (tt % 2)), inc=("ld", 16))
                ld_x = s.val("ld")
                for mt in (0, 2):
                    jt = tt * 4 + mt
                    for half in range(2):
                        for dc in range(8):
                            s.op("tensor", lambda t_, dc=dc, half=half, xb=xb, mt=mt: t_.matmul(
                                p_qh[half][:],
                                xb[:, dc, (mt + half) * 128:(mt + half + 1) * 128],
                                wkv[:, dc, 0:512],
                                start=(dc == 0), stop=(dc == 7)),
                                waits=([("ld", ld_x)] + s.gate("wkv_ready") + s.gate("pq2_free"))
                                      if (half == 0 and dc == 0) else [],
                                inc=("mm", 1) if (half == 1 and dc == 7) else None)
                    mm_v = s.val("mm")
                    for half in range(2):
                        vv = vaug[:, jt + half, :].rearrange("p (h e) -> p h e", h=8)[:, :, 0:64]
                        pv = p_qh[half][:].rearrange("p (h e) -> p h e", h=8)
                        bb = bvb[:].rearrange("p (h e) -> p h e", h=8)
                        s.op("vector", lambda v, vv=vv, pv=pv, bb=bb: v.tensor_tensor(
                            vv, pv, bb, ALU.add),
                            waits=[("mm", mm_v), ("gp", gp_vaug)],
                            inc=("ve", 1) if half == 1 else None)
                    s.setg("pq2_free", "ve")
                s.setg("xn1t%d_free" % (tt % 2), "mm")
            s.setg("wkv_free", "mm")
            load_whalf(wqa, wqb, "wqb", gc)
            if g == 1:
                for dc in range(8):
                    for half in range(2):
                        sb = half
                        s.op("sync", lambda e, dc=dc, half=half, sb=sb: e.dma_start(
                            out=wstg[sb][:], in_=woa[dc * 128:(dc + 1) * 128,
                                                    half * 512:(half + 1) * 512]),
                            waits=s.gate("wstg%d_free" % sb), inc=("ld", 16))
                        s.op("vector", lambda v, dc=dc, half=half, sb=sb: v.tensor_copy(
                            wob[:, dc, half * 512:(half + 1) * 512], wstg[sb][:]),
                            waits=[("ld", s.val("ld"))], inc=("ve", 1))
                        s.setg("wstg%d_free" % sb, "ve")
            ve_wo = s.val("ve")

            for qt in range(NQT):
                t0 = qt * 512
                nkc = 4 * (qt + 1)
                xb = xn1t[qt % 2]
                s.op("sync", lambda e, t0=t0, xb=xb: e.dma_start(
                    out=xb[:], in_=xn1[:, :, t0:t0 + 512].rearrange("c p t -> p c t")),
                    waits=s.gate("xn1t%d_free" % (qt % 2)), inc=("ld", 16))
                ld_x = s.val("ld")
                if g == 1:
                    ab = attg0t[qt % 2]
                    s.op("sync", lambda e, qt=qt, ab=ab: e.dma_start(
                        out=ab[:], in_=att0[qt]),
                        waits=s.gate("attg0t%d_free" % (qt % 2)), inc=("ld", 16))
                else:
                    ab = None
                ld_a = s.val("ld")
                pend_tr = [None]
                for mt in (0, 2):
                    jt = qt * 4 + mt
                    prep_pair(xb, wqb, "wqb_ready", jt, mt, bqb, qnall,
                              (lambda c, mt=mt: qfm[:, c, mt * 128:(mt + 2) * 128]),
                              pend_tr, [("ld", ld_x)])
                emit_tr(pend_tr[0]); pend_tr[0] = None
                s.op("vector", lambda v, qt=qt: v.reduce_max(
                    mstat[:, 3:4],
                    qnall[:, qt * 4:(qt + 1) * 4, :].rearrange("p a b -> p (a b)"), axis=AX.X),
                    waits=[("ve", s.val("ve"))], inc=("ve", 1))
                pmax_to_scalar(mstat[:, 3:4], 1)
                def vg1(emit):
                    s.op("vector", emit, waits=[("ve", s.val("ve"))], inc=("ve", 1))
                vg1(lambda v: v.tensor_mul(mstat[0:1, 4:5], mstat[0:1, 0:1], mstat[0:1, 1:2]))
                vg1(lambda v: v.tensor_scalar(mstat[0:1, 4:5], mstat[0:1, 4:5], 1e-30, None, ALU.add))
                s.op("scalar", lambda a: a.activation(mstat[0:1, 5:6], mstat[0:1, 4:5],
                                                      AF.Ln, bias=0.0, scale=1.0),
                     waits=[("ve", s.val("ve"))], inc=("ac", 1))
                s.op("scalar", lambda a: a.activation(mstat[0:1, 6:7], mstat[0:1, 5:6],
                                                      AF.Exp, bias=0.0, scale=0.5),
                     waits=[("ac", s.val("ac"))], inc=("ac", 1))
                vg1(lambda v: v.tensor_scalar(mstat[0:1, 7:8], mstat[0:1, 6:7], -0.125, None, ALU.mult))
                s.op("tensor", lambda t_: t_.matmul(
                    p_aux[:, 0:1], onesb[0:1, :], mstat[0:1, 7:8], start=True, stop=True),
                    waits=[("ve", s.val("ve")), ("ac", s.val("ac"))] + s.gate("paux_free"),
                    inc=("mm", 1))
                s.op("vector", lambda v: v.tensor_copy(negMb[:], p_aux[:, 0:1]),
                     waits=[("mm", s.val("mm"))] + s.gate("negM_inuse"), inc=("ve", 1))
                s.setg("paux_free", "ve")
                s.setg("negM_ready", "ve")
                ve_qfm = s.val("ve")
                norm_pend = [None]

                def emit_norm(np_):
                    hlp, mm_pvp = np_
                    ccp = hlp // 2
                    s.op("vector", lambda v, mm_pvp=mm_pvp: v.reciprocal(rs_t[64:65, :], p_at[64:65, :]),
                         waits=[("mm", mm_pvp)] + s.gate("rst_free"), inc=("ve", 1))
                    s.op("tensor", lambda t_: t_.matmul(
                        p_aux[0:64, :], onesb[64:65, 0:64], rs_t[64:65, :],
                        start=True, stop=True),
                        waits=[("ve", s.val("ve"))] + s.gate("paux_free"), inc=("mm", 1))
                    s.op("vector", lambda v: v.tensor_copy(rb_s[:], p_aux[0:64, :]),
                         waits=[("mm", s.val("mm"))], inc=("ve", 1))
                    s.setg("rst_free", "ve")
                    if hlp % 2 == 0:
                        s.op("vector", lambda v, ccp=ccp: v.tensor_tensor(
                            atile[0:64, ccp, :], p_at[0:64, :], rb_s[:], ALU.mult),
                            waits=[("ve", s.val("ve"))] +
                                  (s.gate("atile_free") if hlp == 0 else []),
                            inc=("ve", 1))
                        s.setg("pat_free", "ve")
                        s.setg("paux_free", "ve")
                        s.setg("negM_inuse", "ve")
                    else:
                        s.op("vector", lambda v: v.tensor_tensor(
                            atnb[:], p_at[0:64, :], rb_s[:], ALU.mult),
                            waits=[("ve", s.val("ve"))], inc=("ve", 1))
                        s.setg("pat_free", "ve")
                        s.op("tensor", lambda t_: t_.matmul(
                            p_aux[64:128, :], id64[0:64, 0:64], atnb[:],
                            start=True, stop=True),
                            waits=[("ve", s.val("ve"))], inc=("mm", 1))
                        s.op("vector", lambda v, ccp=ccp: v.tensor_copy(
                            atile[64:128, ccp, :], p_aux[64:128, :]),
                            waits=[("mm", s.val("mm"))], inc=("ve", 1))
                        s.setg("paux_free", "ve")
                        s.setg("negM_inuse", "ve")

                for hl in range(8):
                    cc, ro = hl // 2, 64 * (hl % 2)

                    def emit_pv(prev, hl=hl, nkc=nkc):
                        kc, ac_exp, ve_pb = prev
                        pp = kc % (ATTN_LAG + 1)
                        s.op("tensor", lambda t_, kc=kc, pp=pp, hl=hl, nkc=nkc: t_.matmul(
                            p_at[0:65, :], vaug[:, kc, hl * 65:hl * 65 + 65],
                            pbuf[pp][:],
                            start=(kc == 0), stop=(kc == nkc - 1)),
                            waits=[("ac", ac_exp), ("ve", ve_pb)] +
                                  (s.gate("pat_free") if kc == 0 else []),
                            inc=("mm", 1) if kc == nkc - 1 else None)
                        s.setg("pbuf%d_free" % pp, "mm")

                    pend = []
                    for kc in range(nkc):
                        if kc == 2 and norm_pend[0] is not None:
                            emit_norm(norm_pend[0])
                            norm_pend[0] = None
                        pp = kc % (ATTN_LAG + 1)
                        s.op("tensor", lambda t_, kc=kc, pp=pp, cc=cc, ro=ro: t_.matmul(
                            p_sc[pp][:],
                            kfm[:, cc, kc * 128:(kc + 1) * 128][ro:ro + 64, :],
                            qfm[:, cc, :][ro:ro + 64, :],
                            start=True, stop=True),
                            waits=[("ve", ve_qfm)] + s.gate("psc%d_free" % pp),
                            inc=("mm", 1))
                        mm_st = s.val("mm")
                        s.op("scalar", lambda a, pp=pp: a.activation(
                            pbuf[pp][:], p_sc[pp][:],
                            AF.Exp, bias=negMb[:, 0:1], scale=0.125),
                            waits=[("mm", mm_st)] + s.gate("pbuf%d_free" % pp) +
                                  s.gate("negM_ready"),
                            inc=("ac", 1))
                        s.setg("psc%d_free" % pp, "ac")
                        ac_exp = s.val("ac")
                        j = kc - 4 * qt
                        if j >= 0:
                            s.op("vector", lambda v, pp=pp, j=j: v.tensor_tensor(
                                pbuf[pp][:], pbuf[pp][:],
                                maskt[:, j, :], ALU.mult),
                                waits=[("ac", ac_exp), ("ld", ld_c)], inc=("ve", 1))
                        ve_pb = s.val("ve")
                        pend.append((kc, ac_exp, ve_pb))
                        if len(pend) > ATTN_LAG:
                            emit_pv(pend.pop(0))
                    for pr in pend:
                        emit_pv(pr)
                    norm_pend[0] = (hl, s.val("mm"))
                if norm_pend[0] is not None:
                    emit_norm(norm_pend[0])
                    norm_pend[0] = None
                ve_atile = s.val("ve")
                if g == 0:
                    s.op("gpsimd", lambda e, qt=qt: e.dma_start(out=att0[qt], in_=atile[:]),
                         waits=[("ve", ve_atile)], inc=("so", 16))
                    s.setg("atile_free", "so")
                else:
                    for mt in range(4):
                        jt = qt * 4 + mt
                        s.op("sync", lambda e, jt=jt: e.dma_start(
                            out=x1m[:], in_=x1.rearrange("(m p) d -> p m d", p=128)[:, jt, :]),
                            waits=s.gate("x1m_free"), inc=("ld", 16))
                        s.op("sync", lambda e, jt=jt: e.dma_start(
                            out=moem[:], in_=moe.rearrange("(m p) d -> p m d", p=128)[:, jt, :]),
                            inc=("ld", 16))
                        ld_e = s.val("ld")
                        s.op("vector", lambda v: v.tensor_add(base[:], x1m[:], moem[:]),
                             waits=[("ld", ld_e)] + s.gate("base_free"), inc=("ve", 1))
                        s.op("vector", lambda v: v.tensor_add(base[:], base[:], bob[:]),
                             waits=[("ve", s.val("ve"))], inc=("ve", 1))
                        s.setg("x1m_free", "ve")
                        ve_base = s.val("ve")
                        for dh in range(2):
                            pw = p_pr if dh == 0 else p_at
                            gkey = "ppr_free" if dh == 0 else "pat_free"
                            for c in range(4):
                                s.op("tensor", lambda t_, c=c, mt=mt, dh=dh, pw=pw, ab=ab: t_.matmul(
                                    pw[:], ab[:, c, mt * 128:(mt + 1) * 128],
                                    wob[:, c, dh * 512:(dh + 1) * 512],
                                    start=(c == 0), stop=False),
                                    waits=([("ld", ld_a), ("ve", ve_atile), ("ve", ve_wo)] +
                                           (s.gate(gkey) if c == 0 else [])))
                            for c in range(4):
                                s.op("tensor", lambda t_, c=c, mt=mt, dh=dh, pw=pw: t_.matmul(
                                    pw[:], atile[:, c, mt * 128:(mt + 1) * 128],
                                    wob[:, 4 + c, dh * 512:(dh + 1) * 512],
                                    start=False, stop=(c == 3)),
                                    inc=("mm", 1) if c == 3 else None)
                            s.op("vector", lambda v, dh=dh, pw=pw: v.tensor_tensor(
                                outt[:, dh * 512:(dh + 1) * 512], pw[:],
                                base[:, dh * 512:(dh + 1) * 512], ALU.add),
                                waits=[("mm", s.val("mm")), ("ve", ve_base)], inc=("ve", 1))
                            s.setg("ppr_free" if dh == 0 else "pat_free", "ve")
                        s.setg("base_free", "ve")
                        ve_out = s.val("ve")
                        s.op("gpsimd", lambda e, jt=jt: e.dma_start(
                            out=out.rearrange("(m p) d -> p m d", p=128)[:, jt, :], in_=outt[:]),
                            waits=[("ve", ve_out)], inc=("so", 16))
                    s.setg("attg0t%d_free" % (qt % 2), "mm")
                    s.setg("atile_free", "mm")
                s.setg("xn1t%d_free" % (qt % 2), "mm")
            s.setg("vaug_free", "mm")
            s.setg("wqb_free", "mm")
            s.setg("wkv_free", "mm")

        return



def _l0_host_inputs(inputs, core):
    f32 = np.float32
    d = {}
    d["x"] = np.ascontiguousarray(np.asarray(inputs["L0_attn_x"])[core], dtype=f32)
    d["wq"] = np.ascontiguousarray(np.asarray(inputs["L0_attn_W_Q"], dtype=f32))
    d["wk"] = np.ascontiguousarray(np.asarray(inputs["L0_attn_W_K"], dtype=f32))
    d["wkp"] = np.ascontiguousarray(np.asarray(inputs["L0_attn_W_Kp"], dtype=f32))
    d["wv"] = np.ascontiguousarray(np.asarray(inputs["L0_attn_W_V"], dtype=f32))
    d["wo"] = np.ascontiguousarray(np.asarray(inputs["L0_attn_W_O"], dtype=f32))
    d["gam"] = np.ascontiguousarray(np.asarray(inputs["L0_attn_ln_gamma"], dtype=f32)[None, :])
    d["bet"] = np.ascontiguousarray(np.asarray(inputs["L0_attn_ln_beta"], dtype=f32)[None, :])
    d["wgT"] = np.ascontiguousarray(np.asarray(inputs["L0_ffn_w_gate"], dtype=np.int32).T)
    d["wuT"] = np.ascontiguousarray(np.asarray(inputs["L0_ffn_w_up"], dtype=np.int32).T)
    d["wdT"] = np.ascontiguousarray(np.asarray(inputs["L0_ffn_w_down"], dtype=np.int32).T)
    d["rwT"] = np.ascontiguousarray(np.asarray(inputs["L1_ffn_router_w"], dtype=f32).T)
    d["rbias"] = np.ascontiguousarray(np.asarray(inputs["L1_ffn_router_bias"], dtype=f32)[None, :])
    d["idb"] = np.eye(128, dtype=BFNP)
    q = np.arange(32)[:, None, None]
    j = np.arange(32)[None, :, None]
    k = np.arange(32)[None, None, :]
    wm = np.where((j > q) | (k > q), np.float32(-1e9), np.float32(0.0))
    d["wmask"] = np.ascontiguousarray(wm.reshape(32, 1024))
    d["idx16"] = np.ascontiguousarray(np.broadcast_to(np.arange(16, dtype=f32), (128, 16)))
    return d


def _host_consts_l0():
    f32 = np.float32
    q = np.arange(32)[:, None, None]
    j = np.arange(32)[None, :, None]
    k = np.arange(32)[None, None, :]
    wm = np.where((j > q) | (k > q), np.float32(-1e9), np.float32(0.0))
    return {
        "idb": np.eye(128, dtype=BFNP),
        "wmask": np.ascontiguousarray(wm.reshape(32, 1024).astype(f32)),
        "idx16": np.ascontiguousarray(np.broadcast_to(np.arange(16, dtype=f32), (128, 16))),
    }


def _host_consts_attn():
    f32 = np.float32
    inv = (10000.0 ** (-2.0 * np.arange(32, dtype=f32) / 64.0)).astype(f32)
    ang = np.arange(S_LEN, dtype=f32)[:, None] * inv[None, :]
    kk = np.arange(128)[:, None]
    qq = np.arange(512)[None, :]
    dm = np.stack([(kk <= qq - 128 * j).astype(BFNP) for j in range(4)])
    return {
        "rcos": np.cos(ang).astype(f32),
        "rsin": np.sin(ang).astype(f32),
        "idb": np.eye(128, dtype=BFNP),
        "dmask": np.ascontiguousarray(dm),
    }


def kernel(**inputs):
    f32 = np.float32
    A = lambda k: np.asarray(inputs[k], f32)
    x_all = A("L0_attn_x")

    l0_shared = {
        "wq": np.ascontiguousarray(A("L0_attn_W_Q")),
        "wk": np.ascontiguousarray(A("L0_attn_W_K")),
        "wkp": np.ascontiguousarray(A("L0_attn_W_Kp")),
        "wv": np.ascontiguousarray(A("L0_attn_W_V")),
        "wo": np.ascontiguousarray(A("L0_attn_W_O")),
        "gam": np.ascontiguousarray(A("L0_attn_ln_gamma")[None, :]),
        "bet": np.ascontiguousarray(A("L0_attn_ln_beta")[None, :]),
        "wgT": np.ascontiguousarray(np.asarray(inputs["L0_ffn_w_gate"], np.int32).T),
        "wuT": np.ascontiguousarray(np.asarray(inputs["L0_ffn_w_up"], np.int32).T),
        "wdT": np.ascontiguousarray(np.asarray(inputs["L0_ffn_w_down"], np.int32).T),
        "rwT": np.ascontiguousarray(A("L1_ffn_router_w").T),
        "rbias": np.ascontiguousarray(A("L1_ffn_router_bias")[None, :]),
    }
    l0_shared.update(_host_consts_l0())
    moe_shared = {
        "eup": np.ascontiguousarray(A("L1_ffn_up_proj")),
        "egt": np.ascontiguousarray(A("L1_ffn_gate_proj")),
        "edn": np.ascontiguousarray(A("L1_ffn_down_proj")),
        "shgT": np.ascontiguousarray(A("L1_ffn_sh_gate").T),
        "shuT": np.ascontiguousarray(A("L1_ffn_sh_up").T),
        "shdT": np.ascontiguousarray(A("L1_ffn_sh_down").T),
    }
    attn_shared = {
        "wqa": np.ascontiguousarray(A("L1_attn_Wq")),
        "wka": np.ascontiguousarray(A("L1_attn_Wk")),
        "wva": np.ascontiguousarray(A("L1_attn_Wv")),
        "woa": np.ascontiguousarray(A("L1_attn_Wo")),
        "bq": np.ascontiguousarray(A("L1_attn_bq")[None, :]),
        "bk": np.ascontiguousarray(A("L1_attn_bk")[None, :]),
        "bv": np.ascontiguousarray(A("L1_attn_bv")[None, :]),
        "bo": np.ascontiguousarray(A("L1_attn_bo")[None, :]),
    }
    attn_shared.update(_host_consts_attn())

    cores = list(range(NCORES))
    nc = build_fused()
    shared = {}
    shared.update(l0_shared)
    shared.update(moe_shared)
    shared.update(attn_shared)
    maps = []
    for c in cores:
        m = dict(shared)
        m["x"] = np.ascontiguousarray(x_all[c])
        maps.append(m)
    r3 = run_bass_kernel_spmd(nc, maps, core_ids=cores)
    return np.stack([np.asarray(r3.results[c]["out"], f32) for c in cores], axis=0)
